# version 02
# baseline (speedup 1.0000x reference)
import numpy as np
from contextlib import ExitStack
import concourse.bass as bass
import concourse.mybir as mybir
from concourse.bass_utils import run_bass_kernel_spmd

F32 = mybir.dt.float32
BF16 = mybir.dt.bfloat16
AF = mybir.ActivationFunctionType
ALU = mybir.AluOpType
AXX = mybir.AxisListType.X

MAGIC = float(np.float32(1.5 * 2 ** 23))
STAGES = 4
FFN_PART = 4
FFN_P1 = 9
C_PART = 9
C2_SUB = 9
C2_SER = False
C3_SER = False
DEBUG = False
DEBUG_B = False
DEBUG_C = False

T = 2048
NT = 16
NW = 8
E = 1024
EC = 8
F = 4096
FC = 32
H = 16
NC = 8


class Sem:
    def __init__(self, es, nc, name):
        self.h = es.enter_context(nc.semaphore(name))
        self.v = 0

    def bump(self, n=1):
        self.v += n
        return self.v


class Em:
    ENGINES = ("tensor", "vector", "scalar", "gpsimd", "sync")

    def __init__(self, nc):
        self.nc = nc
        self.q = {e: [] for e in self.ENGINES}

    def op(self, eng, fn, waits=(), inc=None, inc_n=1):
        waits = [(s, v) for (s, v) in waits if v > 0]
        for s, v in waits:
            assert v <= s.v, f"wait on future event {v} > {s.v}"
        ev = inc.bump(inc_n) if inc is not None else None

        def g(h, fn=fn, waits=waits, inc=inc, inc_n=inc_n):
            for s, v in waits:
                h.wait_ge(s.h, v)
            ins = fn(h)
            if inc is not None:
                ins.then_inc(inc.h, inc_n)
        self.q[eng].append(g)
        return ev

    def flush(self, nc):
        with nc.Block() as block:
            for eng in self.ENGINES:
                lst = self.q[eng]
                if not lst:
                    continue
                getattr(block, eng)(lambda h, lst=lst: [f(h) for f in lst])
        self.q = {e: [] for e in self.ENGINES}


def build_nc():
    nc = bass.Bass("TRN2", target_bir_lowering=False, debug=False)
    x_in = nc.dram_tensor("x", [T, E], F32, kind="ExternalInput")
    wattnT = nc.dram_tensor("wattnT", [E, 3 * E], F32, kind="ExternalInput")
    wprojT = nc.dram_tensor("wprojT", [E, E], F32, kind="ExternalInput")
    w1T = nc.dram_tensor("w1T", [E, F], F32, kind="ExternalInput")
    w2T = nc.dram_tensor("w2T", [F, E], F32, kind="ExternalInput")
    cw_d = nc.dram_tensor("cw", [128, 96], F32, kind="ExternalInput")
    cb_d = nc.dram_tensor("cb", [128, 32], F32, kind="ExternalInput")
    gam_d = nc.dram_tensor("gam", [128, 8], F32, kind="ExternalInput")
    bet_d = nc.dram_tensor("bet", [128, 8], F32, kind="ExternalInput")
    b1_d = nc.dram_tensor("b1c", [128, 32], F32, kind="ExternalInput")
    b2_d = nc.dram_tensor("b2c", [128, 8], F32, kind="ExternalInput")
    sel_d = nc.dram_tensor("sel", [16, 2], BF16, kind="ExternalInput")
    wq_d = nc.dram_tensor("wqm", [E, E], F32, kind="ExternalInput")
    wk_d = nc.dram_tensor("wkm", [E, E], F32, kind="ExternalInput")
    wv_d = nc.dram_tensor("wvm", [E, E], F32, kind="ExternalInput")
    wo_d = nc.dram_tensor("wom", [E, E], F32, kind="ExternalInput")
    bw1_d = nc.dram_tensor("bw1", [E, F], F32, kind="ExternalInput")
    bw2_d = nc.dram_tensor("bw2", [F, E], F32, kind="ExternalInput")
    out_d = nc.dram_tensor("out", [T, E], F32, kind="ExternalOutput")
    dbg = {}
    if DEBUG_C:
        for nm, shape in [("cxr9", [128, 1024]), ("cln2c0", [128, 2048]),
                          ("cln2t15", [128, 2048]), ("cln2c1", [128, 2048]),
                          ("cad9", [128, 1024]),
                          ("cln1f", [128, 2048]), ("cqp", [128, 2048]),
                          ("ckvsb", [64, 1040]), ("ckvss", [128, 1040]),
                          ("catt", [128, 1024]), ("cad1", [128, 1024]),
                          ("cden", [128, 36]), ("cln2f", [128, 2048])]:
            dbg[nm] = nc.dram_tensor(nm, shape, F32, kind="ExternalOutput")
    if DEBUG_B:
        dbg["bw1q"] = nc.dram_tensor("bw1q", [E, F], F32, kind="ExternalOutput")
        dbg["bm1"] = nc.dram_tensor("bm1", [128, 2], F32, kind="ExternalOutput")
        for nm, shape in [("bxqf", [128, 8 * 2052]), ("bg", [128, 32 * 256]),
                          ("bmb", [128, 256]), ("bqst", [128, 64]),
                          ("bs2b", [128, 256]), ("bgq", [128, 512]),
                          ("bstat", [128, 128])]:
            dbg[nm] = nc.dram_tensor(nm, shape, F32, kind="ExternalOutput")
    if DEBUG:
        for nm, shape in [("dlnt", [128, 2048]), ("dlnf", [128, 2048]),
                          ("dqf", [128, 2048]), ("dkf", [128, 2048]),
                          ("dvt", [128, 2048]), ("desc", [128, 512]),
                          ("detb", [128, 1024]), ("dyf", [128, 2048]),
                          ("dstat", [128, 128])]:
            dbg[nm] = nc.dram_tensor(nm, shape, F32, kind="ExternalOutput")

    w1q_sp = nc.dram_tensor("w1q_sp", [E, F], BF16)
    w2q_sp = nc.dram_tensor("w2q_sp", [F, E], BF16)
    bw1_sp = nc.dram_tensor("bw1_sp", [E, F], BF16)
    bw2_sp = nc.dram_tensor("bw2_sp", [F, E], BF16)
    ag_in = [nc.dram_tensor(f"ag_in{i}", [2, E], BF16) for i in range(2)]
    ag_out = [nc.dram_tensor(f"ag_out{i}", [NC * 2, E], BF16) for i in range(2)]
    ar_in = nc.dram_tensor("ar_in", [64, H * 65], F32)
    ar_out = nc.dram_tensor("ar_out", [64, H * 65], F32)

    es = ExitStack()
    sbuf = lambda n, s, d: es.enter_context(nc.sbuf_tensor(n, s, d))
    with es:
        ID = sbuf("ID", [128, 128], BF16)
        IDF = sbuf("IDF", [128, 128], F32)
        ONE1F = sbuf("ONE1F", [1, 128], F32)
        ONECB = sbuf("ONECB", [128, 1], BF16)
        CWM = sbuf("CWM", [128, 96], F32)
        CBc = sbuf("CBc", [128, 32], F32)
        GAMc = sbuf("GAMc", [128, 8], F32)
        BETc = sbuf("BETc", [128, 8], F32)
        B1c = sbuf("B1c", [128, 32], F32)
        B2c = sbuf("B2c", [128, 8], F32)
        SELc = sbuf("SELc", [16, 2], BF16)
        STAT = sbuf("STAT", [128, 128], F32)
        M1B = sbuf("M1B", [128, 2], F32)
        M2B = sbuf("M2B", [128, 2], F32)
        SQT = sbuf("SQT", [128, E], F32)

        P = [es.enter_context(nc.psum_tensor(f"ps{i}", [128, 512], F32))
             for i in range(8)]

        s_ld = Sem(es, nc, "s_ld")
        s_st = Sem(es, nc, "s_st")
        s_pe = Sem(es, nc, "s_pe")
        s_dv = Sem(es, nc, "s_dv")
        s_ac = Sem(es, nc, "s_ac")
        s_pl = Sem(es, nc, "s_pl")
        s_cc = Sem(es, nc, "s_cc")

        em = Em(nc)

        def dma(eng, dst, src, waits=(), sem=s_ld):
            return em.op(eng, lambda h: h.dma_start(out=dst, in_=src),
                         waits=waits, inc=sem, inc_n=16)

        def V(t, width):
            return t[:, :].rearrange("p (j w) -> p j w", w=width)

        dma("sync", CWM[:, :], cw_d[:, :])
        dma("sync", CBc[:, :], cb_d[:, :])
        dma("sync", GAMc[:, :], gam_d[:, :])
        dma("sync", BETc[:, :], bet_d[:, :])
        dma("sync", B1c[:, :], b1_d[:, :])
        dma("sync", B2c[:, :], b2_d[:, :])
        ev_consts = dma("sync", SELc[:, :], sel_d[:, :])
        em.op("gpsimd", lambda h: h.memset(ID[:, :], 0.0))
        em.op("gpsimd", lambda h: h.affine_select(
            out=ID[:, :], in_=ID[:, :], compare_op=ALU.not_equal, fill=1.0,
            base=0, pattern=[[-1, 128]], channel_multiplier=1))
        em.op("gpsimd", lambda h: h.memset(IDF[:, :], 0.0))
        em.op("gpsimd", lambda h: h.affine_select(
            out=IDF[:, :], in_=IDF[:, :], compare_op=ALU.not_equal, fill=1.0,
            base=0, pattern=[[-1, 128]], channel_multiplier=1))
        em.op("gpsimd", lambda h: h.memset(ONE1F[:, :], 1.0))
        ev_id = em.op("gpsimd", lambda h: h.memset(ONECB[:, :], 1.0), inc=s_pl)

        if STAGES >= 2:
            with nc.sbuf_tensor("WTMP", [128, EC * F], F32) as WTMP, \
                 nc.sbuf_tensor("WQT", [128, EC * F], BF16) as WQT:

                def quantize_weight(dram_w, rows, cols, spill, mB):
                    nchunk = rows // 128
                    ncols = nchunk * cols
                    ev_w = dma("sync",
                               WTMP[:, 0:ncols].rearrange("p (j f) -> p j f", f=cols),
                               dram_w.ap().rearrange("(j p) f -> p j f", p=128))
                    nsl = 8
                    sl = ncols // nsl
                    for g in range(nsl):
                        em.op("vector", lambda h, g=g, sl=sl: h.reduce_sum(
                            STAT[:, 96 + g:97 + g], WTMP[:, g * sl:(g + 1) * sl],
                            axis=AXX, apply_absolute_value=True),
                            waits=[(s_ld, ev_w)] if g == 0 else [],
                            inc=(s_dv if g == nsl - 1 else None))
                    _evp = s_dv.v
                    em.op("vector", lambda h: h.reduce_sum(
                        STAT[:, 104:105], STAT[:, 96:104], axis=AXX),
                        waits=[(s_dv, _evp)], inc=s_dv)
                    ev_part = s_dv.v
                    ev_tr = em.op("tensor", lambda h: h.matmul(
                        P[0][0:1, 0:128], STAT[:, 104:105], IDF[:, :],
                        start=True, stop=True),
                        waits=[(s_dv, ev_part), (s_pl, ev_id)], inc=s_pe)
                    ev_tsum = em.op("vector", lambda h: h.reduce_sum(
                        STAT[0:1, 105:106], P[0][0:1, 0:128], axis=AXX),
                        waits=[(s_pe, ev_tr)], inc=s_dv)
                    em.op("vector", lambda h, rc=float(rows * cols): h.tensor_scalar(
                        STAT[0:1, 106:107], STAT[0:1, 105:106], 1.0 / rc, 1e-5,
                        ALU.mult, ALU.max), waits=[(s_dv, ev_tsum)], inc=s_dv)
                    ev_m = s_dv.v
                    ev_bc = em.op("tensor", lambda h: h.matmul(
                        P[1][:, 0:1], ONE1F[0:1, :], STAT[0:1, 106:107],
                        start=True, stop=True),
                        waits=[(s_dv, ev_m)], inc=s_pe)
                    ev_mb0 = em.op("vector", lambda h: h.tensor_copy(
                        mB[:, 0:1], P[1][:, 0:1]),
                        waits=[(s_pe, ev_bc)], inc=s_dv)
                    em.op("vector", lambda h: h.reciprocal(mB[:, 1:2], mB[:, 0:1]),
                          waits=[(s_dv, ev_mb0)])
                    em.op("vector", lambda h: h.drain(), inc=s_dv)
                    sl2 = ncols // 8
                    for g in range(8):
                        a, b = g * sl2, (g + 1) * sl2
                        em.op("vector", lambda h, a=a, b=b: h.tensor_scalar(
                            WTMP[:, a:b], WTMP[:, a:b], mB[:, 1:2], MAGIC,
                            ALU.mult, ALU.add))
                        em.op("vector", lambda h, a=a, b=b: h.tensor_scalar(
                            WTMP[:, a:b], WTMP[:, a:b], -MAGIC, 1.0,
                            ALU.add, ALU.min))
                        em.op("vector", lambda h, a=a, b=b: h.tensor_scalar(
                            WQT[:, a:b], WTMP[:, a:b], -1.0, None, ALU.max),
                            inc=s_dv)
                    ev_q = s_dv.v
                    return dma("sync", spill.ap().rearrange("(j p) f -> p j f", p=128),
                               WQT[:, 0:ncols].rearrange("p (j f) -> p j f", f=cols),
                               waits=[(s_dv, ev_q)], sem=s_st)

                quantize_weight(w1T, E, F, w1q_sp, M1B)
                em.op("vector", lambda h: h.tensor_scalar(
                    CWM[:, :], CWM[:, :], M1B[:, 0:1], None, ALU.mult),
                    waits=[(s_ld, ev_consts)])
                quantize_weight(w2T, F, E, w2q_sp, M2B)

                if STAGES >= 3:
                    def cast_weight(dram_w, rows, cols, spill):
                        nchunk = rows // 128
                        ncols = nchunk * cols
                        ev_w = dma("sync",
                                   WTMP[:, 0:ncols].rearrange("p (j f) -> p j f", f=cols),
                                   dram_w.ap().rearrange("(j p) f -> p j f", p=128))
                        sl = ncols // 8
                        for g in range(8):
                            a, b = g * sl, (g + 1) * sl
                            em.op("vector", lambda h, a=a, b=b: h.tensor_copy(
                                WQT[:, a:b], WTMP[:, a:b]),
                                waits=([(s_ld, ev_w)] if g == 0 else []),
                                inc=(s_dv if g == 7 else None))
                        return dma("sync",
                                   spill.ap().rearrange("(j p) f -> p j f", p=128),
                                   WQT[:, 0:ncols].rearrange("p (j f) -> p j f", f=cols),
                                   waits=[(s_dv, s_dv.v)], sem=s_st)

                    cast_weight(bw1_d, E, F, bw1_sp)
                    cast_weight(bw2_d, F, E, bw2_sp)

                em.flush(nc)

        XR = sbuf("XR", [128, NT * E], F32)
        ev_x = dma("sync", V(XR, E), x_in.ap().rearrange("(i p) f -> p i f", p=128))

        def emit_stage_stats(eps, extra_rho=False, src=None, ev_ready=None):
            for tt in range(NT):
                em.op("vector", lambda h, tt=tt: h.reduce_sum(
                    STAT[:, tt:tt + 1], (src or XR)[:, E * tt:E * (tt + 1)], axis=AXX),
                    waits=[(s_ld, ev_ready or ev_x)] if tt == 0 else [],
                    inc=(s_dv if tt == NT - 1 else None))
                em.op("scalar", lambda h, tt=tt: h.activation(
                    SQT[:, :], (src or XR)[:, E * tt:E * (tt + 1)], AF.Square,
                    accum_out=STAT[:, 16 + tt:17 + tt]),
                    waits=[(s_ld, ev_ready or ev_x)] if tt == 0 else [])
            em.op("scalar", lambda h: h.drain(), inc=s_ac)
            ev_sums = s_dv.v
            ev_sq = s_ac.v
            ev_mean = em.op("vector", lambda h: h.tensor_scalar(
                STAT[:, 32:48], STAT[:, 0:16], 1.0 / E, None, ALU.mult),
                waits=[(s_dv, ev_sums)], inc=s_dv)
            ev_msq = em.op("scalar", lambda h: h.activation(
                STAT[:, 80:96], STAT[:, 32:48], AF.Square),
                waits=[(s_dv, ev_mean)], inc=s_ac)
            ev_ssq = em.op("vector", lambda h: h.tensor_scalar(
                STAT[:, 48:64], STAT[:, 16:32], 1.0 / E, None, ALU.mult),
                waits=[(s_ac, ev_sq)], inc=s_dv)
            ev_var = em.op("vector", lambda h: h.tensor_tensor(
                STAT[:, 48:64], STAT[:, 48:64], STAT[:, 80:96], ALU.subtract),
                waits=[(s_ac, ev_msq), (s_dv, ev_ssq)], inc=s_dv)
            ev_ve = em.op("vector", lambda h, eps=eps: h.tensor_scalar(
                STAT[:, 80:96], STAT[:, 48:64], eps, None, ALU.add),
                waits=[(s_dv, ev_var)], inc=s_dv)
            ev_sd = em.op("scalar", lambda h: h.activation(
                STAT[:, 96:112], STAT[:, 80:96], AF.Sqrt),
                waits=[(s_dv, ev_ve)], inc=s_ac)
            em.op("vector", lambda h: h.reciprocal(
                STAT[:, 112:128], STAT[:, 96:112]),
                waits=[(s_ac, ev_sd)])
            ev_r = em.op("vector", lambda h: h.drain(), inc=s_dv)
            ev_cp = em.op("vector", lambda h: h.tensor_copy(
                STAT[:, 48:64], STAT[:, 112:128]),
                waits=[(s_dv, ev_r)], inc=s_dv)
            if extra_rho:
                ev1 = em.op("scalar", lambda h: h.activation(
                    STAT[:, 80:96], STAT[:, 48:64], AF.Square),
                    waits=[(s_dv, ev_cp)], inc=s_ac)
                em.op("vector", lambda h: h.reciprocal(
                    STAT[:, 96:112], STAT[:, 80:96]),
                    waits=[(s_ac, ev1)])
                ev2 = em.op("vector", lambda h: h.drain(), inc=s_dv)
                ev3 = em.op("vector", lambda h, eps=eps: h.tensor_scalar(
                    STAT[:, 96:112], STAT[:, 96:112], -eps, None, ALU.add),
                    waits=[(s_dv, ev2)], inc=s_dv)
                ev4 = em.op("vector", lambda h: h.tensor_tensor(
                    STAT[:, 96:112], STAT[:, 96:112], STAT[:, 80:96], ALU.mult),
                    waits=[(s_dv, ev3)], inc=s_dv)
                ev5 = em.op("vector", lambda h: h.tensor_scalar(
                    STAT[:, 96:112], STAT[:, 96:112], 1e-6, None, ALU.add),
                    waits=[(s_dv, ev4)], inc=s_dv)
                ev6 = em.op("scalar", lambda h: h.activation(
                    STAT[:, 96:112], STAT[:, 96:112], AF.Sqrt),
                    waits=[(s_dv, ev5)], inc=s_ac)
                em.op("vector", lambda h: h.reciprocal(
                    STAT[:, 112:128], STAT[:, 96:112]),
                    waits=[(s_ac, ev6)])
                ev7 = em.op("vector", lambda h: h.drain(), inc=s_dv)
                ev_cp = em.op("vector", lambda h: h.tensor_tensor(
                    STAT[:, 64:80], STAT[:, 112:128], STAT[:, 48:64], ALU.mult),
                    waits=[(s_dv, ev7)], inc=s_dv)
            return ev_cp

        with nc.sbuf_tensor("WAB", [128, EC * 3 * E], BF16) as WAB, \
             nc.sbuf_tensor("WPB", [128, EC * E], BF16) as WPB, \
             nc.sbuf_tensor("LNT", [128, 2 * 2 * E], BF16) as LNT, \
             nc.sbuf_tensor("LNF", [128, 2 * EC * 256], BF16) as LNF, \
             nc.sbuf_tensor("QF", [128, 2 * 2048], BF16) as QF, \
             nc.sbuf_tensor("KF", [128, 2 * 2048], BF16) as KF, \
             nc.sbuf_tensor("VT", [128, 2 * 2048], BF16) as VT, \
             nc.sbuf_tensor("ESC", [128, 2 * 256], F32) as ESC, \
             nc.sbuf_tensor("EB", [128, 2 * 256], BF16) as EB, \
             nc.sbuf_tensor("ETB", [128, 2 * 512], BF16) as ETB, \
             nc.sbuf_tensor("YF", [128, 2 * 2048], BF16) as YF, \
             nc.sbuf_tensor("SST", [128, 8], F32) as SST:

            ev_wa = em.op("gpsimd", lambda h: h.dma_start(
                out=V(WAB, 3 * E), in_=wattnT.ap().rearrange("(c p) e -> p c e", p=128)),
                inc=s_ld, inc_n=16)
            ev_wp = em.op("gpsimd", lambda h: h.dma_start(
                out=V(WPB, E), in_=wprojT.ap().rearrange("(c p) e -> p c e", p=128)),
                inc=s_ld, inc_n=16)
            ev_ldall = s_ld.v
            globals()["_EV_LDALL"] = ev_ldall

            emit_stage_stats(1e-5, ev_ready=ev_ldall)
            ev_stats = s_dv.v

            last = {}

            for w in range(NW):
                par = w % 2
                lnt0 = par * 2 * E
                lnf0 = par * EC * 256
                qf0 = kf0 = vt0 = yf0 = par * 2048
                ev_lnt = []
                for s in range(2):
                    tt = 2 * w + s
                    ev = em.op("vector", lambda h, tt=tt, s=s, lnt0=lnt0: h.tensor_scalar(
                        LNT[:, lnt0 + s * E:lnt0 + (s + 1) * E],
                        XR[:, E * tt:E * (tt + 1)],
                        STAT[:, 32 + tt:33 + tt], STAT[:, 48 + tt:49 + tt],
                        ALU.subtract, ALU.mult),
                        waits=[(s_dv, ev_stats)] + last.get(("LNT", par), []),
                        inc=s_dv)
                    ev_lnt.append(ev)
                ev_lnf = []
                for s in range(2):
                    for grp in range(2):
                        bk = grp % 2
                        evt = None
                        for jj in range(4):
                            j = grp * 4 + jj
                            evt = em.op("tensor", lambda h, s=s, j=j, jj=jj, bk=bk,
                                        lnt0=lnt0: h.matmul(
                                P[bk][:, jj * 128:(jj + 1) * 128],
                                LNT[:, lnt0 + s * E + j * 128:lnt0 + s * E + (j + 1) * 128],
                                ID[:, :], start=True, stop=True),
                                waits=([(s_dv, ev_lnt[s]), (s_pl, ev_id)]
                                       + last.get(("bank", bk), []))
                                      if jj == 0 else [],
                                inc=(s_pe if jj == 3 else None))
                        ev = em.op("vector", lambda h, s=s, grp=grp, bk=bk,
                                   lnf0=lnf0: h.tensor_copy(
                            V(LNF, 256)[:, lnf0 // 256 + grp * 4:
                                        lnf0 // 256 + grp * 4 + 4, 128 * s:128 * s + 128],
                            P[bk][:, 0:512].rearrange("p (c w) -> p c w", w=128)),
                            waits=[(s_pe, evt)], inc=s_dv)
                        last[("bank", bk)] = [(s_dv, ev)]
                        ev_lnf.append(ev)
                ev_lnf_all = ev_lnf[-1]
                last[("LNT", par)] = [(s_pe, s_pe.v)]
                ev_qk = []
                for j in range(16):
                    bk = 2 + (j % 2)
                    isq = j < 8
                    off = (128 * j) if isq else (E + 128 * (j - 8))
                    evt = None
                    for c in range(EC):
                        evt = em.op("tensor", lambda h, c=c, off=off, bk=bk,
                                    lnf0=lnf0: h.matmul(
                            P[bk][:, 0:256],
                            WAB[:, 3 * E * c + off:3 * E * c + off + 128],
                            LNF[:, lnf0 + 256 * c:lnf0 + 256 * c + 256],
                            start=(c == 0), stop=(c == EC - 1)),
                            waits=([(s_dv, ev_lnf_all), (s_ld, ev_ldall)]
                                   + last.get(("bank", bk), [])) if c == 0 else [],
                            inc=(s_pe if c == EC - 1 else None))
                    if isq:
                        ev = em.op("vector", lambda h, j=j, bk=bk, qf0=qf0: h.tensor_scalar(
                            QF[:, qf0 + 256 * j:qf0 + 256 * j + 256], P[bk][:, 0:256],
                            0.125, None, ALU.mult),
                            waits=[(s_pe, evt)] + last.get(("QF", par), []), inc=s_dv)
                    else:
                        ev = em.op("vector", lambda h, j=j, bk=bk, kf0=kf0: h.tensor_copy(
                            KF[:, kf0 + 256 * (j - 8):kf0 + 256 * (j - 8) + 256],
                            P[bk][:, 0:256]),
                            waits=[(s_pe, evt)] + last.get(("KF", par), []), inc=s_dv)
                    last[("bank", bk)] = [(s_dv, ev)]
                    ev_qk.append(ev)
                last[("QF", par)] = []
                last[("KF", par)] = []
                ev_v = []
                for sv in range(2):
                    for half in range(2):
                        bk = 4 + ((2 * sv + half) % 2)
                        evt = None
                        for c in range(EC):
                            evt = em.op("tensor", lambda h, c=c, sv=sv, half=half,
                                        bk=bk, lnf0=lnf0: h.matmul(
                                P[bk][:, 0:512],
                                LNF[:, lnf0 + 256 * c + 128 * sv:
                                    lnf0 + 256 * c + 128 * sv + 128],
                                WAB[:, 3 * E * c + 2 * E + 512 * half:
                                     3 * E * c + 2 * E + 512 * half + 512],
                                start=(c == 0), stop=(c == EC - 1)),
                                waits=([(s_dv, ev_lnf_all), (s_ld, ev_ldall)]
                                       + last.get(("bank", bk), [])) if c == 0 else [],
                                inc=(s_pe if c == EC - 1 else None))
                        ev = em.op("vector", lambda h, sv=sv, half=half, bk=bk,
                                   vt0=vt0: h.tensor_copy(
                            VT[:, vt0 + 1024 * sv + 512 * half:
                               vt0 + 1024 * sv + 512 * half + 512], P[bk][:, 0:512]),
                            waits=[(s_pe, evt)] + last.get(("VT", par), []), inc=s_dv)
                        last[("bank", bk)] = [(s_dv, ev)]
                        ev_v.append(ev)
                last[("VT", par)] = []
                ev_v_all = ev_v[-1]
                ev_y = []
                for hh in range(H):
                    hp = hh % 2
                    poff = 64 * (hh % 2)
                    chq = hh // 2
                    ev_sc = []
                    for isub in range(2):
                        bk = 2 + isub
                        ev = em.op("tensor", lambda h, hh=hh, isub=isub, bk=bk,
                                   poff=poff, chq=chq, qf0=qf0, kf0=kf0: h.matmul(
                            P[bk][:, 0:256],
                            QF[poff:poff + 64, qf0 + 256 * chq + 128 * isub:
                               qf0 + 256 * chq + 128 * isub + 128],
                            KF[poff:poff + 64, kf0 + 256 * chq:kf0 + 256 * chq + 256],
                            start=True, stop=True),
                            waits=[(s_dv, max(ev_qk[chq], ev_qk[8 + chq]))]
                                  + last.get(("bank", bk), []),
                            inc=s_pe)
                        ev_sc.append(ev)
                    for isub in range(2):
                        bk = 2 + isub
                        em.op("vector", lambda h, isub=isub, bk=bk: h.reduce_max(
                            SST[:, isub:isub + 1], P[bk][:, 0:256], axis=AXX,
                            negate=True),
                            waits=[(s_pe, ev_sc[isub])] + last.get(("SSTa", isub), []))
                    ev_nm = em.op("vector", lambda h: h.drain(), inc=s_dv)
                    for isub in range(2):
                        bk = 2 + isub
                        em.op("scalar", lambda h, isub=isub, bk=bk: h.activation(
                            ESC[:, 256 * isub:256 * isub + 256], P[bk][:, 0:256],
                            AF.Exp, bias=SST[:, isub:isub + 1], scale=1.0,
                            accum_out=SST[:, 2 + isub:3 + isub]),
                            waits=[(s_dv, ev_nm)] + last.get(("ESC", isub), []))
                    ev_exp = em.op("scalar", lambda h: h.drain(), inc=s_ac)
                    last[("bank", 2)] = [(s_ac, ev_exp)]
                    last[("bank", 3)] = [(s_ac, ev_exp)]
                    for isub in range(2):
                        em.op("vector", lambda h, isub=isub: h.reciprocal(
                            SST[:, 4 + isub:5 + isub], SST[:, 2 + isub:3 + isub]),
                            waits=[(s_ac, ev_exp)])
                    ev_rc = em.op("vector", lambda h: h.drain(), inc=s_dv)
                    ev_eb = []
                    for isub in range(2):
                        ev = em.op("gpsimd", lambda h, isub=isub: h.tensor_scalar(
                            EB[:, 256 * isub:256 * isub + 256],
                            ESC[:, 256 * isub:256 * isub + 256],
                            SST[:, 4 + isub:5 + isub], None, ALU.mult),
                            waits=[(s_dv, ev_rc)] + last.get(("EB", isub), []),
                            inc=s_pl)
                        last[("SSTa", isub)] = [(s_pl, ev)]
                        last[("ESC", isub)] = [(s_pl, ev)]
                        ev_eb.append(ev)
                    evt = None
                    for isub in range(2):
                        for jj in range(2):
                            evt = em.op("tensor", lambda h, isub=isub, jj=jj: h.matmul(
                                P[6][:, 256 * jj + 128 * isub:256 * jj + 128 * isub + 128],
                                EB[:, 256 * isub + 128 * jj:256 * isub + 128 * jj + 128],
                                ID[:, :], start=True, stop=True),
                                waits=([(s_pl, ev_eb[isub])]
                                       + (last.get(("bank", 6), []) if (isub == 0 and jj == 0) else [])),
                                inc=(s_pe if (isub == 1 and jj == 1) else None))
                    ev_etb = em.op("scalar", lambda h, hp=hp: h.activation(
                        ETB[:, 512 * hp:512 * hp + 512], P[6][:, 0:512], AF.Copy),
                        waits=[(s_pe, evt)] + last.get(("ETB", hp), []), inc=s_ac)
                    last[("bank", 6)] = [(s_ac, ev_etb)]
                    last[("EB", 0)] = [(s_pe, evt)]
                    last[("EB", 1)] = [(s_pe, evt)]
                    evt = None
                    for jsub in range(2):
                        evt = em.op("tensor", lambda h, hh=hh, jsub=jsub, hp=hp,
                                    poff=poff, vt0=vt0: h.matmul(
                            P[7][poff:poff + 64, 0:256],
                            VT[:, vt0 + 1024 * jsub + 64 * hh:
                               vt0 + 1024 * jsub + 64 * hh + 64],
                            ETB[:, 512 * hp + 256 * jsub:512 * hp + 256 * jsub + 256],
                            start=(jsub == 0), stop=(jsub == 1),
                            tile_position=(0, poff)),
                            waits=([(s_ac, ev_etb), (s_dv, ev_v_all)]
                                   + last.get(("bank", 7), [])) if jsub == 0 else [],
                            inc=(s_pe if jsub == 1 else None))
                    ev = em.op("vector", lambda h, hh=hh, poff=poff, chq=chq,
                               yf0=yf0: h.tensor_copy(
                        YF[poff:poff + 64, yf0 + 256 * chq:yf0 + 256 * chq + 256],
                        P[7][poff:poff + 64, 0:256]),
                        waits=[(s_pe, evt)] + last.get(("YF", par), []), inc=s_dv)
                    last[("bank", 7)] = [(s_dv, ev)]
                    last[("ETB", hp)] = [(s_pe, evt)]
                    ev_y.append(ev)
                last[("YF", par)] = []
                ev_y_all = ev_y[-1]
                for s in range(2):
                    for half in range(2):
                        bk = (2 * s + half) % 2
                        evt = None
                        for c in range(EC):
                            evt = em.op("tensor", lambda h, c=c, s=s, half=half,
                                        bk=bk, yf0=yf0: h.matmul(
                                P[bk][:, 0:512],
                                YF[:, yf0 + 256 * c + 128 * s:
                                   yf0 + 256 * c + 128 * s + 128],
                                WPB[:, E * c + 512 * half:E * c + 512 * half + 512],
                                start=(c == 0), stop=(c == EC - 1)),
                                waits=([(s_dv, ev_y_all), (s_ld, ev_ldall)]
                                       + last.get(("bank", bk), [])) if c == 0 else [],
                                inc=(s_pe if c == EC - 1 else None))
                        ev = em.op("vector", lambda h, s=s, half=half, bk=bk,
                                   w=w: h.tensor_tensor(
                            XR[:, E * (2 * w + s) + 512 * half:
                               E * (2 * w + s) + 512 * half + 512],
                            XR[:, E * (2 * w + s) + 512 * half:
                               E * (2 * w + s) + 512 * half + 512],
                            P[bk][:, 0:512], ALU.add),
                            waits=[(s_pe, evt)], inc=s_dv)
                        last[("bank", bk)] = [(s_dv, ev)]
                last[("LNT", par)] = [(s_dv, s_dv.v)]
                if DEBUG and w == 0:
                    with nc.sbuf_tensor("DBGF", [128, 2048], F32) as DBGF:
                        def dump(name, srcap, n):
                            ev = em.op("vector", lambda h, srcap=srcap, n=n:
                                       h.tensor_copy(DBGF[:, 0:n], srcap),
                                       waits=[(s_dv, s_dv.v), (s_ac, s_ac.v),
                                              (s_pe, s_pe.v), (s_pl, s_pl.v),
                                              (s_st, s_st.v)],
                                       inc=s_dv)
                            dma("sync", dbg[name][:, 0:n], DBGF[:, 0:n],
                                waits=[(s_dv, ev)], sem=s_st)
                        dump("dlnt", LNT[:, 0:2048], 2048)
                        dump("dlnf", LNF[:, 0:2048], 2048)
                        dump("dqf", QF[:, 0:2048], 2048)
                        dump("dkf", KF[:, 0:2048], 2048)
                        dump("dvt", VT[:, 0:2048], 2048)
                        dump("desc", ESC[:, 0:512], 512)
                        dump("detb", ETB[:, 0:1024], 1024)
                        dump("dyf", YF[:, 0:2048], 2048)
                        dump("dstat", STAT[:, :], 128)
                        em.flush(nc)
            em.flush(nc)

        s_w1s = [Sem(es, nc, f"s_w1s{i}") for i in range(2)]
        s_w2s = [Sem(es, nc, f"s_w2s{i}") for i in range(2)]
        s_hl = Sem(es, nc, "s_hl")
        ev_spills = s_st.v

        def emit_ffn(idx):
            with nc.sbuf_tensor(f"XQF{idx}", [128, EC * 2052], BF16) as XQF, \
                 nc.sbuf_tensor(f"LNS{idx}", [128, 2 * E], F32) as LNS, \
                 nc.sbuf_tensor(f"XQS{idx}", [128, 2 * E], BF16) as XQS, \
                 nc.sbuf_tensor(f"AG16{idx}", [16, E], BF16) as AG16, \
                 nc.sbuf_tensor(f"W1S{idx}", [128, 2 * 4096], BF16) as W1S, \
                 nc.sbuf_tensor(f"W2S{idx}", [128, 2 * 4096], BF16) as W2S, \
                 nc.sbuf_tensor(f"G{idx}", [128, FC * 256], F32) as G, \
                 nc.sbuf_tensor(f"CT{idx}", [128, 2 * 512], F32) as CT, \
                 nc.sbuf_tensor(f"GS{idx}", [128, 2 * 256], F32) as GS, \
                 nc.sbuf_tensor(f"GQ{idx}", [128, 2 * 256], BF16) as GQ, \
                 nc.sbuf_tensor(f"MB{idx}", [128, 256], F32) as MB, \
                 nc.sbuf_tensor(f"S2R{idx}", [1, 256], F32) as S2R, \
                 nc.sbuf_tensor(f"S2B{idx}", [128, 256], F32) as S2B, \
                 nc.sbuf_tensor(f"QST{idx}", [128, 64], F32) as QST:

                last = {}
                ev_stats = emit_stage_stats(1e-5, ev_ready=_EV_LDALL)
                em.op("gpsimd", lambda h: h.memset(
                    V(XQF, 2052)[:, :, 0:1], 0.0))
                ev_guard0 = em.op("gpsimd", lambda h: h.memset(
                    V(XQF, 2052)[:, :, 2049:2052], 0.0), inc=s_pl)

                ev_xqs = {}
                for tt in range(NT):
                    par = tt % 2
                    ev_ln = em.op("vector", lambda h, tt=tt, par=par: h.tensor_scalar(
                        LNS[:, par * E:(par + 1) * E], XR[:, E * tt:E * (tt + 1)],
                        STAT[:, 32 + tt:33 + tt], STAT[:, 48 + tt:49 + tt],
                        ALU.subtract, ALU.mult),
                        waits=[(s_dv, ev_stats)], inc=s_dv)
                    ev_mx = em.op("vector", lambda h, tt=tt, par=par: h.reduce_max(
                        QST[:, tt:tt + 1], LNS[:, par * E:(par + 1) * E], axis=AXX,
                        apply_absolute_value=True),
                        waits=[(s_dv, ev_ln)], inc=s_dv)
                    ev_mc = em.op("vector", lambda h, tt=tt: h.tensor_scalar(
                        QST[:, 16 + tt:17 + tt], QST[:, tt:tt + 1], 1e-5, None,
                        ALU.max), waits=[(s_dv, ev_mx)], inc=s_dv)
                    em.op("vector", lambda h, tt=tt: h.reciprocal(
                        QST[:, 32 + tt:33 + tt], QST[:, 16 + tt:17 + tt]),
                        waits=[(s_dv, ev_mc)])
                    ev_ri = em.op("vector", lambda h: h.drain(), inc=s_dv)
                    em.op("vector", lambda h, tt=tt, par=par: h.tensor_scalar(
                        LNS[:, par * E:(par + 1) * E], LNS[:, par * E:(par + 1) * E],
                        QST[:, 32 + tt:33 + tt], 127.0, ALU.mult, ALU.mult),
                        waits=[(s_dv, ev_ri)] + last.get(("XQS", par), []))
                    em.op("vector", lambda h, par=par: h.tensor_scalar(
                        LNS[:, par * E:(par + 1) * E], LNS[:, par * E:(par + 1) * E],
                        MAGIC, -MAGIC, ALU.add, ALU.add))
                    ev_xi = em.op("vector", lambda h, tt=tt, par=par: h.tensor_scalar(
                        XQS[:, par * E:(par + 1) * E], LNS[:, par * E:(par + 1) * E],
                        QST[:, 16 + tt:17 + tt], 1.0 / 127.0, ALU.mult, ALU.mult),
                        inc=s_dv)
                    ev_xqs[tt] = ev_xi
                    if tt == 0:
                        ev_h0 = dma("sync", ag_in[idx][0:1, :], XQS[0:1, 0:E],
                                    waits=[(s_dv, ev_xi)], sem=s_hl)
                    if tt == NT - 1:
                        ev_h1 = dma("sync", ag_in[idx][1:2, :], XQS[127:128, par * E:(par + 1) * E],
                                    waits=[(s_dv, ev_xi)], sem=s_hl)
                    for grp in range(2):
                        bk = grp
                        evt = None
                        for jj in range(4):
                            j = grp * 4 + jj
                            evt = em.op("tensor", lambda h, j=j, jj=jj, bk=bk, par=par: h.matmul(
                                P[bk][:, jj * 128:(jj + 1) * 128],
                                XQS[:, par * E + j * 128:par * E + (j + 1) * 128],
                                ID[:, :], start=True, stop=True),
                                waits=([(s_dv, ev_xi)] + last.get(("bank", bk), []))
                                      if jj == 0 else [],
                                inc=(s_pe if jj == 3 else None))
                        ev = em.op("vector", lambda h, tt=tt, grp=grp, bk=bk: h.tensor_copy(
                            V(XQF, 2052)[:, grp * 4:grp * 4 + 4, 1 + 128 * tt:1 + 128 * tt + 128],
                            P[bk][:, 0:512].rearrange("p (c w) -> p c w", w=128)),
                            waits=[(s_pe, evt)], inc=s_dv)
                        last[("bank", bk)] = [(s_dv, ev)]
                    last[("XQS", par)] = [(s_pe, s_pe.v)] + (
                        [(s_hl, ev_h0)] if tt == 0 else [])
                ev_ag = em.op("gpsimd", lambda h: h.collective_compute(
                    "AllGather", ALU.bypass, replica_groups=[list(range(NC))],
                    ins=[ag_in[idx].ap().opt()], outs=[ag_out[idx].ap().opt()]),
                    waits=[(s_hl, s_hl.v)], inc=s_cc)
                ev_ag16 = em.op("gpsimd", lambda h: h.dma_start(
                    out=AG16[:, :], in_=ag_out[idx][:, :]),
                    waits=[(s_cc, ev_ag)], inc=s_ld, inc_n=16)
                evg = None
                for side in range(2):
                    for j in range(8):
                        evg = em.op("tensor", lambda h, side=side, j=j: h.matmul(
                            P[2][:, 8 * side + j:8 * side + j + 1],
                            AG16[:, 128 * j:128 * j + 128], SELc[:, side:side + 1],
                            start=True, stop=True),
                            waits=[(s_ld, s_ld.v), (s_pl, ev_guard0)]
                                  if (side == 0 and j == 0) else [],
                            inc=(s_pe if (side == 1 and j == 7) else None))
                ev_gl = em.op("vector", lambda h: h.tensor_copy(
                    V(XQF, 2052)[:, :, 0:1],
                    P[2][:, 0:8].rearrange("p (c w) -> p c w", w=1)),
                    waits=[(s_pe, evg)], inc=s_dv)
                ev_gr = em.op("vector", lambda h: h.tensor_copy(
                    V(XQF, 2052)[:, :, 2049:2050],
                    P[2][:, 8:16].rearrange("p (c w) -> p c w", w=1)),
                    inc=s_dv)
                last[("bank", 2)] = [(s_dv, ev_gr)]
                ev_xqf = s_dv.v
                if FFN_PART <= 1:
                    em.flush(nc)
                    return

                spill1 = w1q_sp if idx == 0 or True else None
                for k in range(8):
                    def w1_dma(g):
                        slot = g % 2
                        wv = [(s_pe, last.get(("w1use", slot), 0)), (s_st, ev_spills)]
                        return em.op("sync", lambda h, g=g, slot=slot: h.dma_start(
                            out=V(W1S, 4096)[:, slot:slot + 1, :].rearrange(
                                "p a (j f) -> p (a j) f", f=512),
                            in_=w1q_sp.ap()[:, 512 * g:512 * (g + 1)].rearrange(
                                "(j p) f -> p j f", p=128)),
                            waits=[w for w in wv if w[1] > 0], inc=s_w1s[slot], inc_n=16)
                    def w2_dma(g):
                        slot = g % 2
                        wv = [(s_pe, last.get(("w2use", slot), 0)), (s_st, ev_spills)]
                        return em.op("sync", lambda h, g=g, slot=slot: h.dma_start(
                            out=V(W2S, 4096)[:, slot:slot + 1, :].rearrange(
                                "p a (j f) -> p (a j) f", f=1024),
                            in_=w2q_sp.ap()[512 * g:512 * (g + 1), :].rearrange(
                                "(j p) f -> p j f", p=128)),
                            waits=[w for w in wv if w[1] > 0], inc=s_w2s[slot], inc_n=16)

                    ev_w1 = {}
                    ev_w1[0] = w1_dma(0)
                    ev_w1[1] = w1_dma(1)
                    for c in range(FC):
                        g = c // 4
                        slot = g % 2
                        par = c % 2
                        bk = par
                        evt = None
                        for d in range(EC):
                            evt = em.op("tensor", lambda h, c=c, d=d, k=k, slot=slot,
                                        bk=bk: h.matmul(
                                P[bk][:, 0:258],
                                V(W1S, 4096)[:, slot, 512 * d + 128 * (c % 4):
                                             512 * d + 128 * (c % 4) + 128],
                                XQF[:, 2052 * d + 256 * k:2052 * d + 256 * k + 258],
                                start=(d == 0), stop=(d == EC - 1)),
                                waits=([(s_w1s[slot], ev_w1[g]), (s_dv, ev_xqf)]
                                       + last.get(("bank", bk), [])) if d == 0 else [],
                                inc=(s_pe if d == EC - 1 else None))
                        if c % 4 == 3:
                            last[("w1use", slot)] = evt
                            if g + 2 <= 7:
                                ev_w1[g + 2] = w1_dma(g + 2)
                        if FFN_P1 == 1:
                            evd = em.op("vector", lambda h, par=par, bk=bk: h.tensor_copy(
                                CT[:, 512 * par:512 * par + 256], P[bk][:, 0:256]),
                                waits=[(s_pe, evt)] + last.get(("CTp", par), []), inc=s_dv)
                            em.op("scalar", lambda h, c=c, par=par: h.activation(
                                G[:, 256 * c:256 * c + 256], CT[:, 512 * par:512 * par + 256],
                                AF.Silu), waits=[(s_dv, evd)], inc=s_ac)
                            last[("bank", bk)] = [(s_dv, evd)]
                            last[("CTp", par)] = [(s_ac, s_ac.v)]
                            continue
                        ev_t0 = em.op("scalar", lambda h, c=c, par=par, bk=bk: h.activation(
                            CT[:, 512 * par:512 * par + 256], P[bk][:, 0:256],
                            AF.Copy, scale=CWM[:, 3 * c:3 * c + 1]),
                            waits=[(s_pe, evt)] + last.get(("CTp", par), []), inc=s_ac)
                        ev_t2 = em.op("scalar", lambda h, c=c, par=par, bk=bk: h.activation(
                            CT[:, 512 * par + 256:512 * par + 512], P[bk][:, 2:258],
                            AF.Copy, scale=CWM[:, 3 * c + 2:3 * c + 3]),
                            inc=s_ac)
                        if FFN_P1 == 2:
                            evd = em.op("vector", lambda h, par=par, bk=bk: h.tensor_copy(
                                CT[:, 512 * par:512 * par + 256], P[bk][:, 0:256]),
                                waits=[(s_ac, ev_t2)], inc=s_dv)
                            em.op("scalar", lambda h, c=c, par=par: h.activation(
                                G[:, 256 * c:256 * c + 256], CT[:, 512 * par:512 * par + 256],
                                AF.Silu), waits=[(s_dv, evd)], inc=s_ac)
                            last[("bank", bk)] = [(s_dv, evd)]
                            last[("CTp", par)] = [(s_ac, s_ac.v)]
                            continue
                        ev_st = em.op("vector", lambda h, c=c, par=par, bk=bk: h.scalar_tensor_tensor(
                            CT[:, 512 * par:512 * par + 256], P[bk][:, 1:257],
                            CWM[:, 3 * c + 1:3 * c + 2],
                            CT[:, 512 * par:512 * par + 256], ALU.mult, ALU.add),
                            waits=[(s_ac, ev_t2)], inc=s_dv)
                        last[("bank", bk)] = [(s_dv, ev_st), (s_ac, ev_t2)]
                        if FFN_P1 == 3:
                            em.op("scalar", lambda h, c=c, par=par: h.activation(
                                G[:, 256 * c:256 * c + 256], CT[:, 512 * par:512 * par + 256],
                                AF.Silu), waits=[(s_dv, ev_st)], inc=s_ac)
                            last[("CTp", par)] = [(s_ac, s_ac.v)]
                            continue
                        ev_pl = em.op("gpsimd", lambda h, par=par: h.tensor_tensor(
                            CT[:, 512 * par:512 * par + 256],
                            CT[:, 512 * par:512 * par + 256],
                            CT[:, 512 * par + 256:512 * par + 512], ALU.add),
                            waits=[(s_dv, ev_st), (s_ac, ev_t2)], inc=s_pl)
                        ev_si = em.op("scalar", lambda h, c=c, par=par: h.activation(
                            G[:, 256 * c:256 * c + 256], CT[:, 512 * par:512 * par + 256],
                            AF.Silu, bias=CBc[:, c:c + 1], scale=1.0),
                            waits=[(s_pl, ev_pl)], inc=s_ac)
                        last[("CTp", par)] = [(s_pl, ev_pl)]
                    if FFN_PART <= 2:
                        continue
                    ev_mb = em.op("vector", lambda h: h.reduce_max(
                        MB[:, :].rearrange("p (t u) -> p t u", u=1),
                        G[:, :].rearrange("p (c t) -> p t c", c=FC),
                        axis=AXX, apply_absolute_value=True),
                        waits=[(s_ac, s_ac.v), (s_dv, s_dv.v)]
                              + last.get(("MBr", 0), []), inc=s_dv)
                    evt = None
                    for sM in range(2):
                        evt = em.op("tensor", lambda h, sM=sM: h.matmul(
                            P[2][:, 128 * sM:128 * sM + 128], MB[:, 128 * sM:128 * sM + 128],
                            IDF[:, :], start=True, stop=True),
                            waits=([(s_dv, ev_mb)] + last.get(("bank", 2), []))
                                  if sM == 0 else [],
                            inc=(s_pe if sM == 1 else None))
                    ev_mt = []
                    for sM in range(2):
                        e1 = em.op("vector", lambda h, sM=sM: h.reduce_max(
                            QST[:, 48 + sM:49 + sM], P[2][:, 128 * sM:128 * sM + 128],
                            axis=AXX), waits=[(s_pe, evt)], inc=s_dv)
                        ev_mt.append(e1)
                    ev_mc2 = em.op("vector", lambda h: h.tensor_scalar(
                        QST[:, 50:52], QST[:, 48:50], 1e-5, None, ALU.max),
                        waits=[(s_dv, ev_mt[-1])], inc=s_dv)
                    last[("bank", 2)] = [(s_dv, ev_mt[-1])]
                    ev_u2 = em.op("vector", lambda h: h.tensor_scalar(
                        QST[:, 52:54], QST[:, 50:52], M2B[:, 0:1], 1.0 / 127.0,
                        ALU.mult, ALU.mult), waits=[(s_dv, ev_mc2)], inc=s_dv)
                    em.op("vector", lambda h: h.reciprocal(
                        QST[:, 54:56], QST[:, 50:52]),
                        waits=[(s_dv, ev_mc2)])
                    ev_rc2 = em.op("vector", lambda h: h.drain(), inc=s_dv)
                    ev_s2c = em.op("vector", lambda h: h.tensor_scalar(
                        QST[:, 56:58], QST[:, 54:56], 127.0, None, ALU.mult),
                        waits=[(s_dv, ev_rc2)], inc=s_dv)
                    evt = None
                    for sM in range(2):
                        evt = em.op("tensor", lambda h, sM=sM: h.matmul(
                            P[3][0:1, 128 * sM:128 * sM + 128],
                            QST[:, 56 + sM:57 + sM], IDF[:, :],
                            start=True, stop=True),
                            waits=([(s_dv, ev_s2c)] + last.get(("bank", 3), []))
                                  if sM == 0 else [],
                            inc=(s_pe if sM == 1 else None))
                    ev_s2r = em.op("vector", lambda h: h.tensor_copy(
                        S2R[0:1, :], P[3][0:1, 0:256]),
                        waits=[(s_pe, evt)], inc=s_dv)
                    last[("bank", 3)] = [(s_dv, ev_s2r)]
                    ev_bc = em.op("tensor", lambda h: h.matmul(
                        P[4][:, 0:256], ONE1F[0:1, :], S2R[0:1, :],
                        start=True, stop=True),
                        waits=[(s_dv, ev_s2r)] + last.get(("bank", 4), []), inc=s_pe)
                    ev_s2b = em.op("vector", lambda h: h.tensor_copy(
                        S2B[:, :], P[4][:, 0:256]),
                        waits=[(s_pe, ev_bc)], inc=s_dv)
                    last[("bank", 4)] = [(s_dv, ev_s2b)]
                    if FFN_PART <= 3:
                        continue
                    ev_w2 = {}
                    ev_w2[0] = w2_dma(0)
                    ev_w2[1] = w2_dma(1)
                    ev_gq = {}
                    for c in range(FC):
                        g = c // 4
                        slot = g % 2
                        par = c % 2
                        em.op("vector", lambda h, c=c, par=par: h.tensor_tensor(
                            GS[:, 256 * par:256 * par + 256], G[:, 256 * c:256 * c + 256],
                            S2B[:, :], ALU.mult),
                            waits=[(s_dv, ev_s2b)] + last.get(("GQp", par), []))
                        ev_q = em.op("vector", lambda h, c=c, par=par: h.tensor_scalar(
                            GQ[:, 256 * par:256 * par + 256], GS[:, 256 * par:256 * par + 256],
                            MAGIC, -MAGIC, ALU.add, ALU.add), inc=s_dv)
                        evt = None
                        for sM in range(2):
                            for eh in range(2):
                                bi = 4 + 2 * sM + eh
                                evt = em.op("tensor", lambda h, c=c, sM=sM, eh=eh, par=par,
                                            slot=slot, bi=bi: h.matmul(
                                    P[bi][:, 0:512],
                                    GQ[:, 256 * par + 128 * sM:256 * par + 128 * sM + 128],
                                    V(W2S, 4096)[:, slot, 1024 * (c % 4) + 512 * eh:
                                                 1024 * (c % 4) + 512 * eh + 512],
                                    start=(c == 0), stop=(c == FC - 1)),
                                    waits=([(s_dv, ev_q), (s_w2s[slot], ev_w2[g])]
                                           + (last.get(("bank", bi), []) if c == 0 else [])),
                                    inc=(s_pe if (sM == 1 and eh == 1) else None))
                        ev_gq[c] = evt
                        if c % 4 == 3:
                            last[("w2use", slot)] = evt
                            if g + 2 <= 7:
                                ev_w2[g + 2] = w2_dma(g + 2)
                        last[("GQp", par)] = [(s_pe, evt)]
                    for sM in range(2):
                        for eh in range(2):
                            bi = 4 + 2 * sM + eh
                            ev = em.op("vector", lambda h, k=k, sM=sM, eh=eh, bi=bi: h.scalar_tensor_tensor(
                                XR[:, E * (2 * k + sM) + 512 * eh:E * (2 * k + sM) + 512 * eh + 512],
                                P[bi][:, 0:512], QST[:, 52 + sM:53 + sM],
                                XR[:, E * (2 * k + sM) + 512 * eh:E * (2 * k + sM) + 512 * eh + 512],
                                ALU.mult, ALU.add),
                                waits=[(s_pe, ev_gq[FC - 1]), (s_dv, ev_u2)], inc=s_dv)
                            last[("bank", bi)] = [(s_dv, ev)]
                    last[("MBr", 0)] = [(s_pe, s_pe.v)]
                    if DEBUG_B and idx == 0 and k == 0:
                        def dumpb(name, srcap):
                            dma("sync", dbg[name][:, 0:srcap.shape[-1]], srcap,
                                waits=[(s_dv, s_dv.v), (s_ac, s_ac.v),
                                       (s_pe, s_pe.v), (s_pl, s_pl.v)], sem=s_st)
                        dumpb("bg", G[:, :])
                        dumpb("bmb", MB[:, :])
                        dumpb("bqst", QST[:, :])
                        dumpb("bs2b", S2B[:, :])
                        em.op("vector", lambda h: h.tensor_copy(
                            GS[:, 0:256], GQ[:, 0:256]),
                            waits=[(s_dv, s_dv.v), (s_pe, s_pe.v)], inc=s_dv)
                        em.op("vector", lambda h: h.tensor_copy(
                            GS[:, 256:512], XQF[:, 0:256]), inc=s_dv)
                        dumpb("bgq", GS[:, 0:512])
                        dumpb("bstat", STAT[:, :])
                        dma("gpsimd", dbg["bxqf"][:, :], XQF[:, :],
                            waits=[(s_dv, s_dv.v), (s_pe, s_pe.v)], sem=s_st)
                        dma("gpsimd", dbg["bw1q"][:, :], w1q_sp.ap(),
                            waits=[(s_st, ev_spills)], sem=s_st)
                        dma("sync", dbg["bm1"][:, :], M1B[:, :],
                            waits=[(s_dv, s_dv.v)], sem=s_st)
                em.flush(nc)

        if STAGES >= 2:
            emit_ffn(0)

        def emit_attnb():
            with nc.sbuf_tensor("LN1F", [128, EC * 2048], BF16) as LN1F:
                emit_stage_stats(1e-5, extra_rho=True)
                ev_rho = s_dv.v
                with nc.sbuf_tensor("WKV", [128, 2 * EC * E], BF16) as WKV, \
                     nc.sbuf_tensor("QP", [128, EC * 2048], BF16) as QP, \
                     nc.sbuf_tensor("LNT2", [128, E], BF16) as LNT2, \
                     nc.sbuf_tensor("KPT", [128, E], BF16) as KPT, \
                     nc.sbuf_tensor("VTT", [128, E], BF16) as VTT, \
                     nc.sbuf_tensor("AD1T", [128, E], F32) as AD1T, \
                     nc.sbuf_tensor("ATT", [128, E], BF16) as ATT, \
                     nc.sbuf_tensor("ATF", [128, E], BF16) as ATF, \
                     nc.sbuf_tensor("DENR", [128, 36], F32) as DENR, \
                     nc.sbuf_tensor("KVSB", [64, H * 65], F32) as KVSB, \
                     nc.sbuf_tensor("KVSS", [128, H * 65], BF16) as KVSS:
                    last = {}
                    ev_wk = em.op("gpsimd", lambda h: h.dma_start(
                        out=V(WKV, E)[:, 0:EC, :],
                        in_=wk_d.ap().rearrange("(c p) e -> p c e", p=128)),
                        inc=s_ld, inc_n=16)
                    ev_wv = em.op("gpsimd", lambda h: h.dma_start(
                        out=V(WKV, E)[:, EC:2 * EC, :],
                        in_=wv_d.ap().rearrange("(c p) e -> p c e", p=128)),
                        inc=s_ld, inc_n=16)
                    ev_wkv = s_ld.v
                    ev_l1 = {}
                    for tt in range(NT):
                        ev_ln = em.op("vector", lambda h, tt=tt: h.tensor_scalar(
                            LNT2[:, :], XR[:, E * tt:E * (tt + 1)],
                            STAT[:, 32 + tt:33 + tt], STAT[:, 64 + tt:65 + tt],
                            ALU.subtract, ALU.mult),
                            waits=[(s_dv, ev_rho)] + last.get(("LNT2",), []), inc=s_dv)
                        for grp in range(2):
                            bk = grp
                            evt = None
                            for jj in range(4):
                                j = grp * 4 + jj
                                evt = em.op("tensor", lambda h, j=j, jj=jj, bk=bk: h.matmul(
                                    P[bk][:, jj * 128:(jj + 1) * 128],
                                    LNT2[:, j * 128:(j + 1) * 128],
                                    ID[:, :], start=True, stop=True),
                                    waits=([(s_dv, ev_ln)] + last.get(("bank", bk), []))
                                          if jj == 0 else [],
                                    inc=(s_pe if jj == 3 else None))
                            ev = None
                            for jj in range(4):
                                j = grp * 4 + jj
                                ev = em.op("vector", lambda h, tt=tt, j=j, jj=jj, bk=bk: h.tensor_scalar(
                                    LN1F[:, 2048 * j + 128 * tt:2048 * j + 128 * tt + 128],
                                    P[bk][:, 128 * jj:128 * jj + 128],
                                    GAMc[:, j:j + 1], BETc[:, j:j + 1],
                                    ALU.mult, ALU.add),
                                    waits=[(s_pe, evt)] if jj == 0 else [],
                                    inc=(s_dv if jj == 3 else None))
                            last[("bank", bk)] = [(s_dv, ev)]
                        last[("LNT2",)] = [(s_pe, s_pe.v)]
                        ev_l1[tt] = s_dv.v
                    if C_PART <= 1:
                        em.flush(nc)
                        return
                    for tt in range(NT):
                        evk = []
                        for (bk, wof) in ((2, 0), (3, EC)):
                            for half in range(2):
                                evt = None
                                for c in range(EC):
                                    evt = em.op("tensor", lambda h, c=c, tt=tt, half=half,
                                                bk=bk, wof=wof: h.matmul(
                                        P[bk][:, 512 * half:512 * half + 512] if False else
                                        P[bk][:, 0:512],
                                        LN1F[:, 2048 * c + 128 * tt:2048 * c + 128 * tt + 128],
                                        WKV[:, E * (wof + c) + 512 * half:
                                            E * (wof + c) + 512 * half + 512],
                                        start=(c == 0), stop=(c == EC - 1)),
                                        waits=([(s_dv, ev_l1[tt]), (s_ld, ev_wkv)]
                                               + last.get(("bank", bk), [])) if c == 0 else [],
                                        inc=(s_pe if c == EC - 1 else None))
                                if bk == 2:
                                    ev = em.op("vector", lambda h, half=half: h.tensor_scalar(
                                        KPT[:, 512 * half:512 * half + 512], P[2][:, 0:512],
                                        0.0, 1e-3, ALU.max, ALU.add),
                                        waits=[(s_pe, evt)] + last.get(("KPT",), []),
                                        inc=s_dv)
                                else:
                                    ev = em.op("vector", lambda h, half=half: h.tensor_copy(
                                        VTT[:, 512 * half:512 * half + 512], P[3][:, 0:512]),
                                        waits=[(s_pe, evt)] + last.get(("VTT",), []),
                                        inc=s_dv)
                                last[("bank", bk)] = [(s_dv, ev)]
                                evk.append(ev)
                        ev_kv = s_dv.v
                        evt = None
                        for hh in range(H):
                            bk = 6 + hh // 8
                            em.op("tensor", lambda h, hh=hh, bk=bk, tt=tt: h.matmul(
                                P[bk][0:64, 64 * (hh % 8):64 * (hh % 8) + 64],
                                KPT[:, 64 * hh:64 * hh + 64],
                                VTT[:, 64 * hh:64 * hh + 64],
                                start=(tt == 0 and hh % 8 == 0),
                                stop=(tt == NT - 1 and hh % 8 == 7),
                                skip_group_check=True),
                                waits=[(s_dv, ev_kv)] if hh == 0 else [])
                            evt = em.op("tensor", lambda h, hh=hh, tt=tt: h.matmul(
                                P[4][0:64, hh:hh + 1],
                                KPT[:, 64 * hh:64 * hh + 64], ONECB[:, 0:1],
                                start=(tt == 0 and hh == 0),
                                stop=(tt == NT - 1 and hh == H - 1),
                                skip_group_check=True),
                                inc=(s_pe if hh == H - 1 else None))
                        last[("KPT",)] = [(s_pe, evt)]
                        last[("VTT",)] = [(s_pe, evt)]
                    ev_kvs_done = s_pe.v
                    if C_PART <= 2:
                        em.flush(nc)
                        return
                    ev_wq = em.op("gpsimd", lambda h: h.dma_start(
                        out=V(WKV, E)[:, 0:EC, :],
                        in_=wq_d.ap().rearrange("(c p) e -> p c e", p=128)),
                        waits=[(s_pe, ev_kvs_done)], inc=s_ld, inc_n=16)
                    ev_wo = em.op("gpsimd", lambda h: h.dma_start(
                        out=V(WKV, E)[:, EC:2 * EC, :],
                        in_=wo_d.ap().rearrange("(c p) e -> p c e", p=128)),
                        inc=s_ld, inc_n=16)
                    for tt in range(NT):
                        for grp in range(2):
                            bk = 2 + grp
                            evt = None
                            for jj in range(4):
                                j = grp * 4 + jj
                                ev0 = None
                                for c in range(EC):
                                    evt = em.op("tensor", lambda h, c=c, j=j, jj=jj, tt=tt,
                                                bk=bk: h.matmul(
                                        P[bk][:, 128 * jj:128 * jj + 128],
                                        WKV[:, E * c + 128 * j:E * c + 128 * j + 128],
                                        LN1F[:, 2048 * c + 128 * tt:2048 * c + 128 * tt + 128],
                                        start=(c == 0), stop=(c == EC - 1)),
                                        waits=([(s_ld, ev_wq)] + last.get(("bank", bk), []))
                                              if (jj == 0 and c == 0) else [],
                                        inc=(s_pe if (jj == 3 and c == EC - 1) else None))
                            ev = em.op("vector", lambda h, tt=tt, grp=grp, bk=bk: h.tensor_scalar(
                                V(QP, 2048)[:, grp * 4:grp * 4 + 4, 128 * tt:128 * tt + 128],
                                P[bk][:, 0:512].rearrange("p (c w) -> p c w", w=128),
                                0.0, 1e-3, ALU.max, ALU.add),
                                waits=[(s_pe, evt)], inc=s_dv)
                            last[("bank", bk)] = [(s_dv, ev)]
                    ev_qp = s_dv.v
                    evs = []
                    for hh in range(H):
                        bk = 6 + hh // 8
                        ev = em.op("vector", lambda h, hh=hh, bk=bk: h.tensor_copy(
                            KVSB[:, 65 * hh:65 * hh + 64],
                            P[bk][0:64, 64 * (hh % 8):64 * (hh % 8) + 64]),
                            waits=[(s_pe, ev_kvs_done)] if hh == 0 else [], inc=None)
                    ev_ks = em.op("vector", lambda h: h.tensor_copy(
                        KVSB[:, :].rearrange("p (x u) -> p x u", u=65)[:, :, 64:65],
                        P[4][0:64, 0:16].rearrange("p (x u) -> p x u", u=1)),
                        inc=s_dv)
                    ev_kvsb = em.op("sync", lambda h: h.dma_start(
                        out=ar_in.ap(), in_=KVSB[:, :]),
                        waits=[(s_dv, ev_ks)], inc=s_st, inc_n=16)
                    ev_ar = em.op("gpsimd", lambda h: h.collective_compute(
                        "AllReduce", ALU.add,
                        replica_groups=[[0, 1, 2, 3], [4, 5, 6, 7]],
                        ins=[ar_in.ap().opt()], outs=[ar_out.ap().opt()]),
                        waits=[(s_st, ev_kvsb)], inc=s_cc)
                    em.op("gpsimd", lambda h: h.dma_start(
                        out=KVSS[0:64, :], in_=ar_out.ap()),
                        waits=[(s_cc, ev_ar)], inc=s_ld, inc_n=16)
                    em.op("gpsimd", lambda h: h.dma_start(
                        out=KVSS[64:128, :], in_=ar_out.ap()),
                        inc=s_ld, inc_n=16)
                    ev_kvss_v = s_ld.v
                    last[("bank", 4)] = [(s_dv, ev_ks)]
                    last[("bank", 6)] = [(s_dv, ev_ks)]
                    last[("bank", 7)] = [(s_dv, ev_ks)]
                    if C_PART <= 3:
                        em.flush(nc)
                        return
                    if DEBUG_C:
                        dma("gpsimd", dbg["cln1f"][:, :], LN1F[:, 0:2048],
                            waits=[(s_dv, s_dv.v), (s_pe, s_pe.v)], sem=s_st)
                        dma("gpsimd", dbg["cqp"][:, :], QP[:, 0:2048],
                            waits=[(s_dv, s_dv.v)], sem=s_st)
                        dma("sync", dbg["ckvsb"][:, :], KVSB[:, :],
                            waits=[(s_dv, s_dv.v)], sem=s_st)
                        dma("gpsimd", dbg["ckvss"][:, :], KVSS[:, :],
                            waits=[(s_ld, s_ld.v)], sem=s_st)
                    for tt in range(NT):
                        ev_nd = [None] * 4
                        for hh in range(H):
                            bk = hh % 4
                            evt = em.op("tensor", lambda h, hh=hh, tt=tt, bk=bk: h.matmul(
                                P[bk][:, 68 * (hh // 4):68 * (hh // 4) + 65],
                                QP[64 * (hh % 2):64 * (hh % 2) + 64,
                                   2048 * (hh // 2) + 128 * tt:2048 * (hh // 2) + 128 * tt + 128],
                                KVSS[64 * (hh % 2):64 * (hh % 2) + 64, 65 * hh:65 * hh + 65],
                                start=True, stop=True),
                                waits=([(s_ld, ev_kvss_v), (s_dv, ev_qp)]
                                       + last.get(("bank", bk), [])) if hh < 4 else [],
                                inc=(s_pe if hh >= 12 else None))
                            if hh >= 12:
                                ev_nd[hh % 4] = evt
                        for b4 in range(4):
                            em.op("vector", lambda h, b4=b4: h.tensor_copy(
                                DENR[:, 4 * b4:4 * b4 + 4].rearrange("p (x u) -> p x u", u=1),
                                P[b4][:, 0:272].rearrange("p (x u) -> p x u", u=68)[:, :, 64:65]),
                                waits=[(s_pe, ev_nd[b4])] + (last.get(("DENR",), []) if b4 == 0 else []))
                        em.op("vector", lambda h: h.drain())
                        em.op("vector", lambda h: h.reciprocal(DENR[:, 16:32], DENR[:, 0:16]))
                        ev_den = em.op("vector", lambda h: h.drain(), inc=s_dv)
                        ev_att = None
                        for hh in range(H):
                            bk = hh % 4
                            di = 16 + 4 * (hh % 4) + hh // 4
                            ev_att = em.op("vector", lambda h, hh=hh, bk=bk, di=di: h.tensor_scalar(
                                ATT[:, 64 * hh:64 * hh + 64],
                                P[bk][:, 68 * (hh // 4):68 * (hh // 4) + 64],
                                DENR[:, di:di + 1], None, ALU.mult),
                                waits=([(s_dv, ev_den)] + last.get(("ATT",), []))
                                      if hh == 0 else [],
                                inc=(s_dv if hh == H - 1 else None))
                        for b4 in range(4):
                            last[("bank", b4)] = [(s_dv, ev_att)]
                        last[("DENR",)] = [(s_dv, ev_att)]
                        if C2_SUB <= 1:
                            continue
                        for grp in range(2):
                            evt = None
                            for jj in range(4):
                                j = grp * 4 + jj
                                evt = em.op("tensor", lambda h, j=j, jj=jj: h.matmul(
                                    P[5][:, 128 * jj:128 * jj + 128],
                                    ATT[:, 128 * j:128 * j + 128], ID[:, :],
                                    start=True, stop=True),
                                    waits=([(s_dv, ev_att)] + last.get(("bank", 5), []))
                                          if jj == 0 else [],
                                    inc=(s_pe if jj == 3 else None))
                            ev = em.op("scalar", lambda h, grp=grp: h.activation(
                                V(ATF, 128)[:, grp * 4:grp * 4 + 4, :],
                                P[5][:, 0:512].rearrange("p (c w) -> p c w", w=128),
                                AF.Copy),
                                waits=[(s_pe, evt)] + (last.get(("ATF",), []) if grp == 0 else []),
                                inc=s_ac)
                            last[("bank", 5)] = [(s_ac, ev)]
                        ev_atf = s_ac.v
                        last[("ATT",)] = [(s_pe, s_pe.v)]
                        ev_wo_mm = []
                        for half in range(2):
                            bk = 6 + half
                            evt = None
                            for j in range(EC):
                                evt = em.op("tensor", lambda h, j=j, half=half, bk=bk: h.matmul(
                                    P[bk][:, 0:512],
                                    ATF[:, 128 * j:128 * j + 128],
                                    WKV[:, E * (EC + j) + 512 * half:
                                        E * (EC + j) + 512 * half + 512],
                                    start=(j == 0), stop=(j == EC - 1)),
                                    waits=([(s_ac, ev_atf), (s_ld, ev_wo)]
                                           + last.get(("bank", bk), [])) if j == 0 else [],
                                    inc=(s_pe if j == EC - 1 else None))
                            ev_wo_mm.append(evt)
                        ev_lo = em.op("vector", lambda h, tt=tt: h.tensor_scalar(
                            AD1T[:, :], XR[:, E * tt:E * (tt + 1)],
                            STAT[:, 32 + tt:33 + tt], STAT[:, 48 + tt:49 + tt],
                            ALU.subtract, ALU.mult),
                            waits=[(s_dv, s_dv.v)] + last.get(("AD1T",), []), inc=s_dv)
                        for half in range(2):
                            bk = 6 + half
                            ev = em.op("vector", lambda h, half=half, bk=bk: h.tensor_tensor(
                                AD1T[:, 512 * half:512 * half + 512],
                                AD1T[:, 512 * half:512 * half + 512],
                                P[bk][:, 0:512], ALU.add),
                                waits=[(s_pe, ev_wo_mm[half]), (s_dv, ev_lo)], inc=s_dv)
                            last[("bank", bk)] = [(s_dv, ev)]
                        ev_add1 = s_dv.v
                        if C2_SUB <= 2:
                            last[("AD1T",)] = [(s_dv, ev_add1)]
                            continue
                        qb = 8 * (tt % 2)
                        ev_s = em.op("vector", lambda h, qb=qb: h.reduce_sum(
                            QST2[:, qb + 0:qb + 1], AD1T[:, :], axis=AXX),
                            waits=[(s_dv, ev_add1)] + last.get(("QST2", tt % 2), []),
                            inc=s_dv)
                        ev_q2 = em.op("scalar", lambda h, qb=qb: h.activation(
                            SQT[:, :], AD1T[:, :], AF.Square, accum_out=QST2[:, qb + 1:qb + 2]),
                            waits=[(s_dv, ev_add1)])
                        ev_q2 = em.op("scalar", lambda h: h.drain(), inc=s_ac)
                        em.op("vector", lambda h, qb=qb: h.tensor_scalar(
                            QST2[:, qb + 2:qb + 3], QST2[:, qb + 0:qb + 1], 1.0 / E, None, ALU.mult),
                            waits=[(s_dv, ev_s)], inc=s_dv)
                        ev_m2x = s_dv.v
                        ev_ms = em.op("scalar", lambda h, qb=qb: h.activation(
                            QST2[:, qb + 3:qb + 4], QST2[:, qb + 2:qb + 3], AF.Square),
                            waits=[(s_dv, ev_m2x)], inc=s_ac)
                        em.op("vector", lambda h, qb=qb: h.tensor_scalar(
                            QST2[:, qb + 4:qb + 5], QST2[:, qb + 1:qb + 2], 1.0 / E, None, ALU.mult),
                            waits=[(s_ac, ev_q2)])
                        em.op("vector", lambda h, qb=qb: h.tensor_tensor(
                            QST2[:, qb + 4:qb + 5], QST2[:, qb + 4:qb + 5], QST2[:, qb + 3:qb + 4],
                            ALU.subtract),
                            waits=[(s_ac, ev_ms)])
                        ev_v2 = em.op("vector", lambda h, qb=qb: h.tensor_scalar(
                            QST2[:, qb + 5:qb + 6], QST2[:, qb + 4:qb + 5], 1e-6, None, ALU.add),
                            inc=s_dv)
                        ev_sd2 = em.op("scalar", lambda h, qb=qb: h.activation(
                            QST2[:, qb + 6:qb + 7], QST2[:, qb + 5:qb + 6], AF.Sqrt),
                            waits=[(s_dv, ev_v2)], inc=s_ac)
                        em.op("vector", lambda h, qb=qb: h.reciprocal(
                            QST2[:, qb + 7:qb + 8], QST2[:, qb + 6:qb + 7]),
                            waits=[(s_ac, ev_sd2)])
                        ev_r2 = em.op("vector", lambda h: h.drain(), inc=s_dv)
                        last[("QST2", tt % 2)] = [(s_ac, ev_sd2)]
                        ev_ln2 = em.op("vector", lambda h, qb=qb: h.tensor_scalar(
                            LNT2[:, :], AD1T[:, :], QST2[:, qb + 2:qb + 3], QST2[:, qb + 7:qb + 8],
                            ALU.subtract, ALU.mult),
                            waits=[(s_dv, ev_r2)] + last.get(("LNT2",), []), inc=s_dv)
                        for grp in range(2):
                            bk = grp
                            evt = None
                            for jj in range(4):
                                j = grp * 4 + jj
                                evt = em.op("tensor", lambda h, j=j, jj=jj, bk=bk: h.matmul(
                                    P[bk][:, jj * 128:(jj + 1) * 128],
                                    LNT2[:, j * 128:(j + 1) * 128],
                                    ID[:, :], start=True, stop=True),
                                    waits=([(s_dv, ev_ln2)] + last.get(("bank", bk), []))
                                          if jj == 0 else [],
                                    inc=(s_pe if jj == 3 else None))
                            ev = None
                            for jj in range(4):
                                j = grp * 4 + jj
                                ev = em.op("vector", lambda h, tt=tt, j=j, jj=jj, bk=bk: h.tensor_scalar(
                                    LN1F[:, 2048 * j + 128 * tt:2048 * j + 128 * tt + 128],
                                    P[bk][:, 128 * jj:128 * jj + 128],
                                    GAMc[:, j:j + 1], BETc[:, j:j + 1],
                                    ALU.mult, ALU.add),
                                    waits=[(s_pe, evt)] if jj == 0 else [],
                                    inc=(s_dv if jj == 3 else None))
                            last[("bank", bk)] = [(s_dv, ev)]
                        last[("LNT2",)] = [(s_pe, s_pe.v)]
                        if DEBUG_C and tt == 12:
                            dma("sync", dbg["cad9"][:, :], AD1T[:, :],
                                waits=[(s_dv, s_dv.v)], sem=s_st)
                            em.op("vector", lambda h: h.engine_nop(),
                                  waits=[(s_st, s_st.v)])
                        if DEBUG_C and tt == 12:
                            dma("gpsimd", dbg["catt"][:, :], ATT[:, :],
                                waits=[(s_dv, s_dv.v), (s_pe, s_pe.v)], sem=s_st)
                            dma("sync", dbg["cad1"][:, :], AD1T[:, :],
                                waits=[(s_dv, s_dv.v)], sem=s_st)
                            dma("sync", dbg["cden"][:, :], DENR[:, :],
                                waits=[(s_dv, s_dv.v)], sem=s_st)
                            dma("gpsimd", dbg["cln2f"][:, :], LN1F[:, 0:2048],
                                waits=[(s_dv, s_dv.v)], sem=s_st)
                            em.op("vector", lambda h: h.engine_nop(),
                                  waits=[(s_st, s_st.v)])
                        if DEBUG_C and tt == 151:
                            dma("gpsimd", dbg["cln2t15"][:, :], LN1F[:, 0:2048],
                                waits=[(s_dv, s_dv.v), (s_pe, s_pe.v)], sem=s_st)
                            em.op("vector", lambda h: h.engine_nop(),
                                  waits=[(s_st, s_st.v)])
                            em.op("gpsimd", lambda h: h.engine_nop(),
                                  waits=[(s_st, s_st.v)])
                        ev_xu = em.op("vector", lambda h, tt=tt: h.tensor_tensor(
                            XR[:, E * tt:E * (tt + 1)], XR[:, E * tt:E * (tt + 1)],
                            AD1T[:, :], ALU.add), inc=s_dv)
                        last[("AD1T",)] = [(s_dv, ev_xu), (s_ac, ev_q2)]
                        if C2_SER:
                            em.op("vector", lambda h: h.drain(),
                                  waits=[(s_ac, s_ac.v), (s_pe, s_pe.v), (s_pl, s_pl.v)],
                                  inc=s_dv)
                            em.op("scalar", lambda h: h.drain(),
                                  waits=[(s_dv, s_dv.v)], inc=s_ac)
                            em.op("tensor", lambda h: h.nop(),
                                  waits=[(s_dv, s_dv.v), (s_ac, s_ac.v)])
                if C_PART <= 4:
                    em.flush(nc)
                    return
                em.flush(nc)
                with nc.sbuf_tensor("HG", [128, FC * 512], BF16) as HG, \
                     nc.sbuf_tensor("OF", [128, EC * 512], BF16) as OF, \
                     nc.sbuf_tensor("BW1S", [128, 2 * 4096], BF16) as BW1S, \
                     nc.sbuf_tensor("BW2S", [128, 2 * 4096], BF16) as BW2S:
                    last = {}
                    s_b1 = [Sem(es, nc, f"s_b1{i}") for i in range(2)]
                    s_b2 = [Sem(es, nc, f"s_b2{i}") for i in range(2)]

                    def b1_dma(g):
                        slot = g % 2
                        wv = [(s_pe, last.get(("b1use", slot), 0)), (s_st, ev_spills)]
                        return em.op("sync", lambda h, g=g, slot=slot: h.dma_start(
                            out=V(BW1S, 512)[:, slot * 8:slot * 8 + 8, :],
                            in_=bw1_sp.ap()[:, 512 * g:512 * (g + 1)].rearrange(
                                "(j p) f -> p j f", p=128)),
                            waits=[w for w in wv if w[1] > 0], inc=s_b1[slot], inc_n=16)

                    def b2_dma(g):
                        slot = g % 2
                        wv = [(s_pe, last.get(("b2use", slot), 0)), (s_st, ev_spills)]
                        return em.op("sync", lambda h, g=g, slot=slot: h.dma_start(
                            out=V(BW2S, 1024)[:, slot * 4:slot * 4 + 4, :],
                            in_=bw2_sp.ap()[512 * g:512 * (g + 1), :].rearrange(
                                "(j p) f -> p j f", p=128)),
                            waits=[w for w in wv if w[1] > 0], inc=s_b2[slot], inc_n=16)

                    for tq in range(4):
                        ev_b1 = {}
                        ev_b1[0] = b1_dma(0)
                        ev_b1[1] = b1_dma(1)
                        for c in range(FC):
                            g = c // 4
                            slot = g % 2
                            bk = c % 2
                            evt = None
                            for j in range(EC):
                                evt = em.op("tensor", lambda h, c=c, j=j, tq=tq, slot=slot,
                                            bk=bk: h.matmul(
                                    P[bk][:, 0:512],
                                    V(BW1S, 512)[:, slot * 8 + j, 128 * (c % 4):
                                                 128 * (c % 4) + 128],
                                    LN1F[:, 2048 * j + 512 * tq:2048 * j + 512 * tq + 512],
                                    start=(j == 0), stop=(j == EC - 1)),
                                    waits=([(s_b1[slot], ev_b1[g]), (s_dv, s_dv.v)]
                                           + last.get(("bank", bk), [])) if j == 0 else [],
                                    inc=(s_pe if j == EC - 1 else None))
                            if c % 4 == 3:
                                last[("b1use", slot)] = evt
                                if g + 2 <= 7:
                                    ev_b1[g + 2] = b1_dma(g + 2)
                            ev = em.op("scalar", lambda h, c=c, bk=bk: h.activation(
                                HG[:, 512 * c:512 * c + 512], P[bk][:, 0:512],
                                AF.Gelu, bias=B1c[:, c:c + 1], scale=1.0),
                                waits=[(s_pe, evt)] + last.get(("HGw", c % 2), []),
                                inc=s_ac)
                            last[("bank", bk)] = [(s_ac, ev)]
                        ev_hg = s_ac.v
                        ev_b2 = {}
                        ev_b2[0] = b2_dma(0)
                        ev_b2[1] = b2_dma(1)
                        ev_ph2 = None
                        for c in range(FC):
                            g = c // 4
                            slot = g % 2
                            for j in range(EC):
                                ev_ph2 = em.op("tensor", lambda h, c=c, j=j, slot=slot: h.matmul(
                                    P[j][:, 0:512],
                                    V(BW2S, 1024)[:, slot * 4 + (c % 4), 128 * j:128 * j + 128],
                                    HG[:, 512 * c:512 * c + 512],
                                    start=(c == 0), stop=(c == FC - 1)),
                                    waits=([(s_b2[slot], ev_b2[g]), (s_ac, ev_hg)]
                                           + sum((last.get(("bank", bb), []) for bb in range(8)), []))
                                          if (c == 0 and j == 0) else [],
                                    inc=(s_pe if (c % 4 == 3 and j == EC - 1) else None))
                            if c % 4 == 3:
                                last[("b2use", slot)] = ev_ph2
                                if g + 2 <= 7:
                                    ev_b2[g + 2] = b2_dma(g + 2)
                        ev_of = []
                        for j in range(EC):
                            ev = em.op("scalar", lambda h, j=j: h.activation(
                                OF[:, 512 * j:512 * j + 512], P[j][:, 0:512],
                                AF.Gelu, bias=B2c[:, j:j + 1], scale=1.0),
                                waits=[(s_pe, ev_ph2)] + (last.get(("OFw",), []) if j == 0 else []),
                                inc=s_ac)
                            ev_of.append(ev)
                        for j in range(EC):
                            last[("bank", j)] = [(s_ac, ev_of[-1])]
                        for sq in range(4):
                            for grp in range(2):
                                bk = grp
                                evt = None
                                for jj in range(4):
                                    j = grp * 4 + jj
                                    evt = em.op("tensor", lambda h, j=j, jj=jj, sq=sq,
                                                bk=bk: h.matmul(
                                        P[bk][:, 128 * jj:128 * jj + 128],
                                        OF[:, 512 * j + 128 * sq:512 * j + 128 * sq + 128],
                                        ID[:, :], start=True, stop=True),
                                        waits=([(s_ac, ev_of[-1])] + last.get(("bank", bk), []))
                                              if jj == 0 else [],
                                        inc=(s_pe if jj == 3 else None))
                                tile = 4 * tq + sq
                                ev = em.op("vector", lambda h, tile=tile, grp=grp, bk=bk: h.tensor_tensor(
                                    XR[:, E * tile + 512 * grp:E * tile + 512 * grp + 512],
                                    XR[:, E * tile + 512 * grp:E * tile + 512 * grp + 512],
                                    P[bk][:, 0:512], ALU.add),
                                    waits=[(s_pe, evt)], inc=s_dv)
                                last[("bank", bk)] = [(s_dv, ev)]
                        last[("OFw",)] = [(s_pe, s_pe.v)]
                        last[("HGw", 0)] = [(s_pe, s_pe.v)]
                        last[("HGw", 1)] = [(s_pe, s_pe.v)]
                        if C3_SER:
                            em.op("vector", lambda h: h.drain(),
                                  waits=[(s_ac, s_ac.v), (s_pe, s_pe.v), (s_pl, s_pl.v)],
                                  inc=s_dv)
                            em.op("scalar", lambda h: h.drain(),
                                  waits=[(s_dv, s_dv.v)], inc=s_ac)
                            em.op("tensor", lambda h: h.nop(),
                                  waits=[(s_dv, s_dv.v), (s_ac, s_ac.v)])
                em.flush(nc)

        if STAGES >= 3:
            with nc.sbuf_tensor("QST2g", [128, 16], F32) as QST2:
                emit_attnb()
        if STAGES >= 4:
            emit_ffn(1)

        ev_out = dma("sync", out_d.ap().rearrange("(i p) f -> p i f", p=128),
                     V(XR, E), waits=[(s_dv, s_dv.v), (s_ac, s_ac.v), (s_pe, s_pe.v)],
                     sem=s_st)
        em.op("sync", lambda h: h.nop(), waits=[(s_st, ev_out)])
        em.flush(nc)

    return nc


def host_layout(inputs):
    xfull = np.ascontiguousarray(inputs["L0_attn_a_x"], dtype=np.float32)
    lay = dict(
        wattnT=np.ascontiguousarray(inputs["L0_attn_a_w_attn"].T),
        wprojT=np.ascontiguousarray(inputs["L0_attn_a_w_proj"].T),
        w1T=np.ascontiguousarray(inputs["L0_ffn_w1"].T),
        w2T=np.ascontiguousarray(inputs["L0_ffn_w2"].T),
        cw=np.ascontiguousarray(
            inputs["L0_ffn_conv_w"].reshape(F, 3).reshape(FC, 128, 3)
            .transpose(1, 0, 2).reshape(128, 96)),
        cb=np.ascontiguousarray(inputs["L0_ffn_conv_b"].reshape(FC, 128).T),
        gam=np.ascontiguousarray(inputs["L0_attn_b_gamma1"].reshape(EC, 128).T),
        bet=np.ascontiguousarray(inputs["L0_attn_b_beta1"].reshape(EC, 128).T),
        b1c=np.ascontiguousarray(inputs["L0_attn_b_b1"].reshape(FC, 128).T),
        b2c=np.ascontiguousarray(inputs["L0_attn_b_b2"].reshape(EC, 128).T),
        wqm=np.ascontiguousarray(inputs["L0_attn_b_wq"].reshape(E, E)),
        wkm=np.ascontiguousarray(inputs["L0_attn_b_wk"].reshape(E, E)),
        wvm=np.ascontiguousarray(inputs["L0_attn_b_wv"].reshape(E, E)),
        wom=np.ascontiguousarray(inputs["L0_attn_b_wo"].reshape(E, E)),
        bw1=np.ascontiguousarray(inputs["L0_attn_b_w1"]),
        bw2=np.ascontiguousarray(inputs["L0_attn_b_w2"]),
    )
    import ml_dtypes
    in_maps = []
    for c in range(NC):
        b, q = c // 4, c % 4
        sel = np.zeros((16, 2), ml_dtypes.bfloat16)
        if q > 0:
            sel[2 * (c - 1) + 1, 0] = 1
        if q < 3:
            sel[2 * (c + 1), 1] = 1
        m = dict(lay)
        m["x"] = np.ascontiguousarray(xfull[b, q * T:(q + 1) * T, :])
        m["sel"] = sel
        in_maps.append(m)
    return in_maps


def kernel(**inputs):
    in_maps = host_layout(inputs)
    nc = build_nc()
    res = run_bass_kernel_spmd(nc, in_maps, core_ids=list(range(NC)))
    outf = np.empty((2, 8192, E), np.float32)
    for c in range(NC):
        b, q = c // 4, c % 4
        outf[b, q * T:(q + 1) * T, :] = res.results[c]["out"]
    return outf


# version 03
# speedup vs baseline: 1.1053x; 1.1053x over previous
import numpy as np
from contextlib import ExitStack
import concourse.bass as bass
import concourse.mybir as mybir
from concourse.bass_utils import run_bass_kernel_spmd

F32 = mybir.dt.float32
BF16 = mybir.dt.bfloat16
AF = mybir.ActivationFunctionType
ALU = mybir.AluOpType
AXX = mybir.AxisListType.X

MAGIC = float(np.float32(1.5 * 2 ** 23))
STAGES = 4
FFN_PART = 4
FFN_P1 = 9
C_PART = 9
C2_SUB = 9
C2_SER = False
C3_SER = False
DEBUG = False
DEBUG_B = False
DEBUG_C = False

T = 2048
NT = 16
NW = 8
E = 1024
EC = 8
F = 4096
FC = 32
H = 16
NC = 8


class Sem:
    def __init__(self, es, nc, name):
        self.h = es.enter_context(nc.semaphore(name))
        self.v = 0

    def bump(self, n=1):
        self.v += n
        return self.v


class Em:
    ENGINES = ("tensor", "vector", "scalar", "gpsimd", "sync")

    def __init__(self, nc):
        self.nc = nc
        self.q = {e: [] for e in self.ENGINES}

    def op(self, eng, fn, waits=(), inc=None, inc_n=1):
        waits = [(s, v) for (s, v) in waits if v > 0]
        for s, v in waits:
            assert v <= s.v, f"wait on future event {v} > {s.v}"
        ev = inc.bump(inc_n) if inc is not None else None

        def g(h, fn=fn, waits=waits, inc=inc, inc_n=inc_n):
            for s, v in waits:
                h.wait_ge(s.h, v)
            ins = fn(h)
            if inc is not None:
                ins.then_inc(inc.h, inc_n)
        self.q[eng].append(g)
        return ev

    def flush(self, nc):
        with nc.Block() as block:
            for eng in self.ENGINES:
                lst = self.q[eng]
                if not lst:
                    continue
                getattr(block, eng)(lambda h, lst=lst: [f(h) for f in lst])
        self.q = {e: [] for e in self.ENGINES}


def build_nc():
    nc = bass.Bass("TRN2", target_bir_lowering=False, debug=False)
    x_in = nc.dram_tensor("x", [T, E], F32, kind="ExternalInput")
    wattnT = nc.dram_tensor("wattnT", [E, 3 * E], F32, kind="ExternalInput")
    wprojT = nc.dram_tensor("wprojT", [E, E], F32, kind="ExternalInput")
    w1T = nc.dram_tensor("w1T", [E, F], F32, kind="ExternalInput")
    w2T = nc.dram_tensor("w2T", [F, E], F32, kind="ExternalInput")
    cw_d = nc.dram_tensor("cw", [128, 96], F32, kind="ExternalInput")
    cb_d = nc.dram_tensor("cb", [128, 32], F32, kind="ExternalInput")
    gam_d = nc.dram_tensor("gam", [128, 8], F32, kind="ExternalInput")
    bet_d = nc.dram_tensor("bet", [128, 8], F32, kind="ExternalInput")
    b1_d = nc.dram_tensor("b1c", [128, 32], F32, kind="ExternalInput")
    b2_d = nc.dram_tensor("b2c", [128, 8], F32, kind="ExternalInput")
    sel_d = nc.dram_tensor("sel", [16, 2], BF16, kind="ExternalInput")
    wq_d = nc.dram_tensor("wqm", [E, E], F32, kind="ExternalInput")
    wk_d = nc.dram_tensor("wkm", [E, E], F32, kind="ExternalInput")
    wv_d = nc.dram_tensor("wvm", [E, E], F32, kind="ExternalInput")
    wo_d = nc.dram_tensor("wom", [E, E], F32, kind="ExternalInput")
    bw1_d = nc.dram_tensor("bw1", [E, F], F32, kind="ExternalInput")
    bw2_d = nc.dram_tensor("bw2", [F, E], F32, kind="ExternalInput")
    out_d = nc.dram_tensor("out", [T, E], F32, kind="ExternalOutput")
    dbg = {}
    if DEBUG_C:
        for nm, shape in [("cxr9", [128, 1024]), ("cln2c0", [128, 2048]),
                          ("cln2t15", [128, 2048]), ("cln2c1", [128, 2048]),
                          ("cad9", [128, 1024]),
                          ("cln1f", [128, 2048]), ("cqp", [128, 2048]),
                          ("ckvsb", [64, 1040]), ("ckvss", [128, 1040]),
                          ("catt", [128, 1024]), ("cad1", [128, 1024]),
                          ("cden", [128, 36]), ("cln2f", [128, 2048])]:
            dbg[nm] = nc.dram_tensor(nm, shape, F32, kind="ExternalOutput")
    if DEBUG_B:
        dbg["bw1q"] = nc.dram_tensor("bw1q", [E, F], F32, kind="ExternalOutput")
        dbg["bm1"] = nc.dram_tensor("bm1", [128, 2], F32, kind="ExternalOutput")
        for nm, shape in [("bxqf", [128, 8 * 2052]), ("bg", [128, 32 * 256]),
                          ("bmb", [128, 256]), ("bqst", [128, 64]),
                          ("bs2b", [128, 256]), ("bgq", [128, 512]),
                          ("bstat", [128, 128])]:
            dbg[nm] = nc.dram_tensor(nm, shape, F32, kind="ExternalOutput")
    if DEBUG:
        for nm, shape in [("dlnt", [128, 2048]), ("dlnf", [128, 2048]),
                          ("dqf", [128, 2048]), ("dkf", [128, 2048]),
                          ("dvt", [128, 2048]), ("desc", [128, 512]),
                          ("detb", [128, 1024]), ("dyf", [128, 2048]),
                          ("dstat", [128, 128])]:
            dbg[nm] = nc.dram_tensor(nm, shape, F32, kind="ExternalOutput")

    w1q_sp = nc.dram_tensor("w1q_sp", [E, F], BF16)
    w2q_sp = nc.dram_tensor("w2q_sp", [F, E], BF16)
    bw1_sp = nc.dram_tensor("bw1_sp", [E, F], BF16)
    bw2_sp = nc.dram_tensor("bw2_sp", [F, E], BF16)
    ag_in = [nc.dram_tensor(f"ag_in{i}", [2, E], BF16) for i in range(2)]
    ag_out = [nc.dram_tensor(f"ag_out{i}", [NC * 2, E], BF16) for i in range(2)]
    ar_in = nc.dram_tensor("ar_in", [64, H * 65], F32)
    ar_out = nc.dram_tensor("ar_out", [64, H * 65], F32)

    es = ExitStack()
    sbuf = lambda n, s, d: es.enter_context(nc.sbuf_tensor(n, s, d))
    with es:
        ID = sbuf("ID", [128, 128], BF16)
        IDF = sbuf("IDF", [128, 128], F32)
        ONE1F = sbuf("ONE1F", [1, 128], F32)
        ONECB = sbuf("ONECB", [128, 1], BF16)
        CWM = sbuf("CWM", [128, 96], F32)
        CBc = sbuf("CBc", [128, 32], F32)
        GAMc = sbuf("GAMc", [128, 8], F32)
        BETc = sbuf("BETc", [128, 8], F32)
        B1c = sbuf("B1c", [128, 32], F32)
        B2c = sbuf("B2c", [128, 8], F32)
        SELc = sbuf("SELc", [16, 2], BF16)
        STAT = sbuf("STAT", [128, 128], F32)
        M1B = sbuf("M1B", [128, 2], F32)
        M2B = sbuf("M2B", [128, 2], F32)
        SQT = sbuf("SQT", [128, E], F32)

        P = [es.enter_context(nc.psum_tensor(f"ps{i}", [128, 512], F32))
             for i in range(8)]

        s_ld = Sem(es, nc, "s_ld")
        s_st = Sem(es, nc, "s_st")
        s_pe = Sem(es, nc, "s_pe")
        s_dv = Sem(es, nc, "s_dv")
        s_ac = Sem(es, nc, "s_ac")
        s_pl = Sem(es, nc, "s_pl")
        s_cc = Sem(es, nc, "s_cc")

        em = Em(nc)

        def dma(eng, dst, src, waits=(), sem=s_ld):
            return em.op(eng, lambda h: h.dma_start(out=dst, in_=src),
                         waits=waits, inc=sem, inc_n=16)

        def V(t, width):
            return t[:, :].rearrange("p (j w) -> p j w", w=width)

        dma("sync", CWM[:, :], cw_d[:, :])
        dma("sync", CBc[:, :], cb_d[:, :])
        dma("sync", GAMc[:, :], gam_d[:, :])
        dma("sync", BETc[:, :], bet_d[:, :])
        dma("sync", B1c[:, :], b1_d[:, :])
        dma("sync", B2c[:, :], b2_d[:, :])
        ev_consts = dma("sync", SELc[:, :], sel_d[:, :])
        em.op("gpsimd", lambda h: h.memset(ID[:, :], 0.0))
        em.op("gpsimd", lambda h: h.affine_select(
            out=ID[:, :], in_=ID[:, :], compare_op=ALU.not_equal, fill=1.0,
            base=0, pattern=[[-1, 128]], channel_multiplier=1))
        em.op("gpsimd", lambda h: h.memset(IDF[:, :], 0.0))
        em.op("gpsimd", lambda h: h.affine_select(
            out=IDF[:, :], in_=IDF[:, :], compare_op=ALU.not_equal, fill=1.0,
            base=0, pattern=[[-1, 128]], channel_multiplier=1))
        em.op("gpsimd", lambda h: h.memset(ONE1F[:, :], 1.0))
        ev_id = em.op("gpsimd", lambda h: h.memset(ONECB[:, :], 1.0), inc=s_pl)

        if STAGES >= 2:
            with nc.sbuf_tensor("WTMP", [128, EC * F], F32) as WTMP, \
                 nc.sbuf_tensor("WQT", [128, EC * F], BF16) as WQT:

                def quantize_weight(dram_w, rows, cols, spill, mB):
                    nchunk = rows // 128
                    ncols = nchunk * cols
                    ev_w = dma("sync",
                               WTMP[:, 0:ncols].rearrange("p (j f) -> p j f", f=cols),
                               dram_w.ap().rearrange("(j p) f -> p j f", p=128))
                    nsl = 8
                    sl = ncols // nsl
                    for g in range(nsl):
                        em.op("vector", lambda h, g=g, sl=sl: h.reduce_sum(
                            STAT[:, 96 + g:97 + g], WTMP[:, g * sl:(g + 1) * sl],
                            axis=AXX, apply_absolute_value=True),
                            waits=[(s_ld, ev_w)] if g == 0 else [],
                            inc=(s_dv if g == nsl - 1 else None))
                    _evp = s_dv.v
                    em.op("vector", lambda h: h.reduce_sum(
                        STAT[:, 104:105], STAT[:, 96:104], axis=AXX),
                        waits=[(s_dv, _evp)], inc=s_dv)
                    ev_part = s_dv.v
                    ev_tr = em.op("tensor", lambda h: h.matmul(
                        P[0][0:1, 0:128], STAT[:, 104:105], IDF[:, :],
                        start=True, stop=True),
                        waits=[(s_dv, ev_part), (s_pl, ev_id)], inc=s_pe)
                    ev_tsum = em.op("vector", lambda h: h.reduce_sum(
                        STAT[0:1, 105:106], P[0][0:1, 0:128], axis=AXX),
                        waits=[(s_pe, ev_tr)], inc=s_dv)
                    em.op("vector", lambda h, rc=float(rows * cols): h.tensor_scalar(
                        STAT[0:1, 106:107], STAT[0:1, 105:106], 1.0 / rc, 1e-5,
                        ALU.mult, ALU.max), waits=[(s_dv, ev_tsum)], inc=s_dv)
                    ev_m = s_dv.v
                    ev_bc = em.op("tensor", lambda h: h.matmul(
                        P[1][:, 0:1], ONE1F[0:1, :], STAT[0:1, 106:107],
                        start=True, stop=True),
                        waits=[(s_dv, ev_m)], inc=s_pe)
                    ev_mb0 = em.op("vector", lambda h: h.tensor_copy(
                        mB[:, 0:1], P[1][:, 0:1]),
                        waits=[(s_pe, ev_bc)], inc=s_dv)
                    em.op("vector", lambda h: h.reciprocal(mB[:, 1:2], mB[:, 0:1]),
                          waits=[(s_dv, ev_mb0)])
                    em.op("vector", lambda h: h.drain(), inc=s_dv)
                    sl2 = ncols // 8
                    for g in range(8):
                        a, b = g * sl2, (g + 1) * sl2
                        em.op("vector", lambda h, a=a, b=b: h.tensor_scalar(
                            WTMP[:, a:b], WTMP[:, a:b], mB[:, 1:2], MAGIC,
                            ALU.mult, ALU.add))
                        em.op("vector", lambda h, a=a, b=b: h.tensor_scalar(
                            WTMP[:, a:b], WTMP[:, a:b], -MAGIC, 1.0,
                            ALU.add, ALU.min))
                        em.op("vector", lambda h, a=a, b=b: h.tensor_scalar(
                            WQT[:, a:b], WTMP[:, a:b], -1.0, None, ALU.max),
                            inc=s_dv)
                    ev_q = s_dv.v
                    return dma("sync", spill.ap().rearrange("(j p) f -> p j f", p=128),
                               WQT[:, 0:ncols].rearrange("p (j f) -> p j f", f=cols),
                               waits=[(s_dv, ev_q)], sem=s_st)

                quantize_weight(w1T, E, F, w1q_sp, M1B)
                em.op("vector", lambda h: h.tensor_scalar(
                    CWM[:, :], CWM[:, :], M1B[:, 0:1], None, ALU.mult),
                    waits=[(s_ld, ev_consts)])
                quantize_weight(w2T, F, E, w2q_sp, M2B)

                if STAGES >= 3:
                    def cast_weight(dram_w, rows, cols, spill):
                        nchunk = rows // 128
                        ncols = nchunk * cols
                        ev_w = dma("sync",
                                   WTMP[:, 0:ncols].rearrange("p (j f) -> p j f", f=cols),
                                   dram_w.ap().rearrange("(j p) f -> p j f", p=128))
                        sl = ncols // 8
                        for g in range(8):
                            a, b = g * sl, (g + 1) * sl
                            em.op("vector", lambda h, a=a, b=b: h.tensor_copy(
                                WQT[:, a:b], WTMP[:, a:b]),
                                waits=([(s_ld, ev_w)] if g == 0 else []),
                                inc=(s_dv if g == 7 else None))
                        return dma("sync",
                                   spill.ap().rearrange("(j p) f -> p j f", p=128),
                                   WQT[:, 0:ncols].rearrange("p (j f) -> p j f", f=cols),
                                   waits=[(s_dv, s_dv.v)], sem=s_st)

                    cast_weight(bw1_d, E, F, bw1_sp)
                    cast_weight(bw2_d, F, E, bw2_sp)

                em.flush(nc)

        XR = sbuf("XR", [128, NT * E], F32)
        ev_x = dma("sync", V(XR, E), x_in.ap().rearrange("(i p) f -> p i f", p=128))

        def emit_stage_stats(eps, extra_rho=False, src=None, ev_ready=None):
            for tt in range(NT):
                em.op("vector", lambda h, tt=tt: h.reduce_sum(
                    STAT[:, tt:tt + 1], (src or XR)[:, E * tt:E * (tt + 1)], axis=AXX),
                    waits=[(s_ld, ev_ready or ev_x)] if tt == 0 else [],
                    inc=(s_dv if tt == NT - 1 else None))
                em.op("scalar", lambda h, tt=tt: h.activation(
                    SQT[:, :], (src or XR)[:, E * tt:E * (tt + 1)], AF.Square,
                    accum_out=STAT[:, 16 + tt:17 + tt]),
                    waits=[(s_ld, ev_ready or ev_x)] if tt == 0 else [])
            em.op("scalar", lambda h: h.drain(), inc=s_ac)
            ev_sums = s_dv.v
            ev_sq = s_ac.v
            ev_mean = em.op("vector", lambda h: h.tensor_scalar(
                STAT[:, 32:48], STAT[:, 0:16], 1.0 / E, None, ALU.mult),
                waits=[(s_dv, ev_sums)], inc=s_dv)
            ev_msq = em.op("scalar", lambda h: h.activation(
                STAT[:, 80:96], STAT[:, 32:48], AF.Square),
                waits=[(s_dv, ev_mean)], inc=s_ac)
            ev_ssq = em.op("vector", lambda h: h.tensor_scalar(
                STAT[:, 48:64], STAT[:, 16:32], 1.0 / E, None, ALU.mult),
                waits=[(s_ac, ev_sq)], inc=s_dv)
            ev_var = em.op("vector", lambda h: h.tensor_tensor(
                STAT[:, 48:64], STAT[:, 48:64], STAT[:, 80:96], ALU.subtract),
                waits=[(s_ac, ev_msq), (s_dv, ev_ssq)], inc=s_dv)
            ev_ve = em.op("vector", lambda h, eps=eps: h.tensor_scalar(
                STAT[:, 80:96], STAT[:, 48:64], eps, None, ALU.add),
                waits=[(s_dv, ev_var)], inc=s_dv)
            ev_sd = em.op("scalar", lambda h: h.activation(
                STAT[:, 96:112], STAT[:, 80:96], AF.Sqrt),
                waits=[(s_dv, ev_ve)], inc=s_ac)
            em.op("vector", lambda h: h.reciprocal(
                STAT[:, 112:128], STAT[:, 96:112]),
                waits=[(s_ac, ev_sd)])
            ev_r = em.op("vector", lambda h: h.drain(), inc=s_dv)
            ev_cp = em.op("vector", lambda h: h.tensor_copy(
                STAT[:, 48:64], STAT[:, 112:128]),
                waits=[(s_dv, ev_r)], inc=s_dv)
            if extra_rho:
                ev1 = em.op("scalar", lambda h: h.activation(
                    STAT[:, 80:96], STAT[:, 48:64], AF.Square),
                    waits=[(s_dv, ev_cp)], inc=s_ac)
                em.op("vector", lambda h: h.reciprocal(
                    STAT[:, 96:112], STAT[:, 80:96]),
                    waits=[(s_ac, ev1)])
                ev2 = em.op("vector", lambda h: h.drain(), inc=s_dv)
                ev3 = em.op("vector", lambda h, eps=eps: h.tensor_scalar(
                    STAT[:, 96:112], STAT[:, 96:112], -eps, None, ALU.add),
                    waits=[(s_dv, ev2)], inc=s_dv)
                ev4 = em.op("vector", lambda h: h.tensor_tensor(
                    STAT[:, 96:112], STAT[:, 96:112], STAT[:, 80:96], ALU.mult),
                    waits=[(s_dv, ev3)], inc=s_dv)
                ev5 = em.op("vector", lambda h: h.tensor_scalar(
                    STAT[:, 96:112], STAT[:, 96:112], 1e-6, None, ALU.add),
                    waits=[(s_dv, ev4)], inc=s_dv)
                ev6 = em.op("scalar", lambda h: h.activation(
                    STAT[:, 96:112], STAT[:, 96:112], AF.Sqrt),
                    waits=[(s_dv, ev5)], inc=s_ac)
                em.op("vector", lambda h: h.reciprocal(
                    STAT[:, 112:128], STAT[:, 96:112]),
                    waits=[(s_ac, ev6)])
                ev7 = em.op("vector", lambda h: h.drain(), inc=s_dv)
                ev_cp = em.op("vector", lambda h: h.tensor_tensor(
                    STAT[:, 64:80], STAT[:, 112:128], STAT[:, 48:64], ALU.mult),
                    waits=[(s_dv, ev7)], inc=s_dv)
            return ev_cp

        with nc.sbuf_tensor("WAB", [128, EC * 3 * E], BF16) as WAB, \
             nc.sbuf_tensor("WPB", [128, EC * E], BF16) as WPB, \
             nc.sbuf_tensor("LNT", [128, 2 * 2 * E], BF16) as LNT, \
             nc.sbuf_tensor("LNF", [128, 2 * EC * 256], BF16) as LNF, \
             nc.sbuf_tensor("QF", [128, 2 * 2048], BF16) as QF, \
             nc.sbuf_tensor("KF", [128, 2 * 2048], BF16) as KF, \
             nc.sbuf_tensor("VT", [128, 2 * 2048], BF16) as VT, \
             nc.sbuf_tensor("ESC", [128, 2 * 256], F32) as ESC, \
             nc.sbuf_tensor("EB", [128, 2 * 256], BF16) as EB, \
             nc.sbuf_tensor("ETB", [128, 2 * 512], BF16) as ETB, \
             nc.sbuf_tensor("YF", [128, 2 * 2048], BF16) as YF, \
             nc.sbuf_tensor("SST", [128, 8], F32) as SST:

            ev_wa = em.op("gpsimd", lambda h: h.dma_start(
                out=V(WAB, 3 * E), in_=wattnT.ap().rearrange("(c p) e -> p c e", p=128)),
                inc=s_ld, inc_n=16)
            ev_wp = em.op("gpsimd", lambda h: h.dma_start(
                out=V(WPB, E), in_=wprojT.ap().rearrange("(c p) e -> p c e", p=128)),
                inc=s_ld, inc_n=16)
            ev_ldall = s_ld.v
            globals()["_EV_LDALL"] = ev_ldall

            emit_stage_stats(1e-5, ev_ready=ev_ldall)
            ev_stats = s_dv.v

            last = {}

            for w in range(NW):
                par = w % 2
                lnt0 = par * 2 * E
                lnf0 = par * EC * 256
                qf0 = kf0 = vt0 = yf0 = par * 2048
                ev_lnt = []
                for s in range(2):
                    tt = 2 * w + s
                    ev = em.op("vector", lambda h, tt=tt, s=s, lnt0=lnt0: h.tensor_scalar(
                        LNT[:, lnt0 + s * E:lnt0 + (s + 1) * E],
                        XR[:, E * tt:E * (tt + 1)],
                        STAT[:, 32 + tt:33 + tt], STAT[:, 48 + tt:49 + tt],
                        ALU.subtract, ALU.mult),
                        waits=[(s_dv, ev_stats)] + last.get(("LNT", par), []),
                        inc=s_dv)
                    ev_lnt.append(ev)
                ev_lnf = []
                for s in range(2):
                    for grp in range(2):
                        bk = grp % 2
                        evt = None
                        for jj in range(4):
                            j = grp * 4 + jj
                            evt = em.op("tensor", lambda h, s=s, j=j, jj=jj, bk=bk,
                                        lnt0=lnt0: h.matmul(
                                P[bk][:, jj * 128:(jj + 1) * 128],
                                LNT[:, lnt0 + s * E + j * 128:lnt0 + s * E + (j + 1) * 128],
                                ID[:, :], start=True, stop=True),
                                waits=([(s_dv, ev_lnt[s]), (s_pl, ev_id)]
                                       + last.get(("bank", bk), []))
                                      if jj == 0 else [],
                                inc=(s_pe if jj == 3 else None))
                        ev = em.op("vector", lambda h, s=s, grp=grp, bk=bk,
                                   lnf0=lnf0: h.tensor_copy(
                            V(LNF, 256)[:, lnf0 // 256 + grp * 4:
                                        lnf0 // 256 + grp * 4 + 4, 128 * s:128 * s + 128],
                            P[bk][:, 0:512].rearrange("p (c w) -> p c w", w=128)),
                            waits=[(s_pe, evt)], inc=s_dv)
                        last[("bank", bk)] = [(s_dv, ev)]
                        ev_lnf.append(ev)
                ev_lnf_all = ev_lnf[-1]
                last[("LNT", par)] = [(s_pe, s_pe.v)]
                ev_qk = []
                for j in range(16):
                    bk = 2 + (j % 2)
                    isq = j < 8
                    off = (128 * j) if isq else (E + 128 * (j - 8))
                    evt = None
                    for c in range(EC):
                        evt = em.op("tensor", lambda h, c=c, off=off, bk=bk,
                                    lnf0=lnf0: h.matmul(
                            P[bk][:, 0:256],
                            WAB[:, 3 * E * c + off:3 * E * c + off + 128],
                            LNF[:, lnf0 + 256 * c:lnf0 + 256 * c + 256],
                            start=(c == 0), stop=(c == EC - 1)),
                            waits=([(s_dv, ev_lnf_all), (s_ld, ev_ldall)]
                                   + last.get(("bank", bk), [])) if c == 0 else [],
                            inc=(s_pe if c == EC - 1 else None))
                    if isq:
                        ev = em.op("vector", lambda h, j=j, bk=bk, qf0=qf0: h.tensor_scalar(
                            QF[:, qf0 + 256 * j:qf0 + 256 * j + 256], P[bk][:, 0:256],
                            0.125, None, ALU.mult),
                            waits=[(s_pe, evt)] + last.get(("QF", par), []), inc=s_dv)
                    else:
                        ev = em.op("vector", lambda h, j=j, bk=bk, kf0=kf0: h.tensor_copy(
                            KF[:, kf0 + 256 * (j - 8):kf0 + 256 * (j - 8) + 256],
                            P[bk][:, 0:256]),
                            waits=[(s_pe, evt)] + last.get(("KF", par), []), inc=s_dv)
                    last[("bank", bk)] = [(s_dv, ev)]
                    ev_qk.append(ev)
                last[("QF", par)] = []
                last[("KF", par)] = []
                ev_v = []
                for sv in range(2):
                    for half in range(2):
                        bk = 4 + ((2 * sv + half) % 2)
                        evt = None
                        for c in range(EC):
                            evt = em.op("tensor", lambda h, c=c, sv=sv, half=half,
                                        bk=bk, lnf0=lnf0: h.matmul(
                                P[bk][:, 0:512],
                                LNF[:, lnf0 + 256 * c + 128 * sv:
                                    lnf0 + 256 * c + 128 * sv + 128],
                                WAB[:, 3 * E * c + 2 * E + 512 * half:
                                     3 * E * c + 2 * E + 512 * half + 512],
                                start=(c == 0), stop=(c == EC - 1)),
                                waits=([(s_dv, ev_lnf_all), (s_ld, ev_ldall)]
                                       + last.get(("bank", bk), [])) if c == 0 else [],
                                inc=(s_pe if c == EC - 1 else None))
                        ev = em.op("vector", lambda h, sv=sv, half=half, bk=bk,
                                   vt0=vt0: h.tensor_copy(
                            VT[:, vt0 + 1024 * sv + 512 * half:
                               vt0 + 1024 * sv + 512 * half + 512], P[bk][:, 0:512]),
                            waits=[(s_pe, evt)] + last.get(("VT", par), []), inc=s_dv)
                        last[("bank", bk)] = [(s_dv, ev)]
                        ev_v.append(ev)
                last[("VT", par)] = []
                ev_v_all = ev_v[-1]
                ev_y = []
                for hh in range(H):
                    hp = hh % 2
                    poff = 64 * (hh % 2)
                    chq = hh // 2
                    ev_sc = []
                    for isub in range(2):
                        bk = 2 + isub
                        ev = em.op("tensor", lambda h, hh=hh, isub=isub, bk=bk,
                                   poff=poff, chq=chq, qf0=qf0, kf0=kf0: h.matmul(
                            P[bk][:, 0:256],
                            QF[poff:poff + 64, qf0 + 256 * chq + 128 * isub:
                               qf0 + 256 * chq + 128 * isub + 128],
                            KF[poff:poff + 64, kf0 + 256 * chq:kf0 + 256 * chq + 256],
                            start=True, stop=True),
                            waits=[(s_dv, max(ev_qk[chq], ev_qk[8 + chq]))]
                                  + last.get(("bank", bk), []),
                            inc=s_pe)
                        ev_sc.append(ev)
                    for isub in range(2):
                        bk = 2 + isub
                        em.op("vector", lambda h, isub=isub, bk=bk: h.reduce_max(
                            SST[:, isub:isub + 1], P[bk][:, 0:256], axis=AXX,
                            negate=True),
                            waits=[(s_pe, ev_sc[isub])] + last.get(("SSTa", isub), []))
                    ev_nm = em.op("vector", lambda h: h.drain(), inc=s_dv)
                    for isub in range(2):
                        bk = 2 + isub
                        em.op("scalar", lambda h, isub=isub, bk=bk: h.activation(
                            ESC[:, 256 * isub:256 * isub + 256], P[bk][:, 0:256],
                            AF.Exp, bias=SST[:, isub:isub + 1], scale=1.0,
                            accum_out=SST[:, 2 + isub:3 + isub]),
                            waits=[(s_dv, ev_nm)] + last.get(("ESC", isub), []))
                    ev_exp = em.op("scalar", lambda h: h.drain(), inc=s_ac)
                    last[("bank", 2)] = [(s_ac, ev_exp)]
                    last[("bank", 3)] = [(s_ac, ev_exp)]
                    for isub in range(2):
                        em.op("vector", lambda h, isub=isub: h.reciprocal(
                            SST[:, 4 + isub:5 + isub], SST[:, 2 + isub:3 + isub]),
                            waits=[(s_ac, ev_exp)])
                    ev_rc = em.op("vector", lambda h: h.drain(), inc=s_dv)
                    ev_eb = []
                    for isub in range(2):
                        ev = em.op("scalar", lambda h, isub=isub: h.activation(
                            EB[:, 256 * isub:256 * isub + 256],
                            ESC[:, 256 * isub:256 * isub + 256],
                            AF.Copy, scale=SST[:, 4 + isub:5 + isub]),
                            waits=[(s_dv, ev_rc)] + last.get(("EB", isub), []),
                            inc=s_ac)
                        last[("SSTa", isub)] = [(s_ac, ev)]
                        last[("ESC", isub)] = [(s_ac, ev)]
                        ev_eb.append(ev)
                    evt = None
                    for isub in range(2):
                        for jj in range(2):
                            evt = em.op("tensor", lambda h, isub=isub, jj=jj: h.matmul(
                                P[6][:, 256 * jj + 128 * isub:256 * jj + 128 * isub + 128],
                                EB[:, 256 * isub + 128 * jj:256 * isub + 128 * jj + 128],
                                ID[:, :], start=True, stop=True),
                                waits=([(s_ac, ev_eb[isub])]
                                       + (last.get(("bank", 6), []) if (isub == 0 and jj == 0) else [])),
                                inc=(s_pe if (isub == 1 and jj == 1) else None))
                    ev_etb = em.op("scalar", lambda h, hp=hp: h.activation(
                        ETB[:, 512 * hp:512 * hp + 512], P[6][:, 0:512], AF.Copy),
                        waits=[(s_pe, evt)] + last.get(("ETB", hp), []), inc=s_ac)
                    last[("bank", 6)] = [(s_ac, ev_etb)]
                    last[("EB", 0)] = [(s_pe, evt)]
                    last[("EB", 1)] = [(s_pe, evt)]
                    evt = None
                    for jsub in range(2):
                        evt = em.op("tensor", lambda h, hh=hh, jsub=jsub, hp=hp,
                                    poff=poff, vt0=vt0: h.matmul(
                            P[7][poff:poff + 64, 0:256],
                            VT[:, vt0 + 1024 * jsub + 64 * hh:
                               vt0 + 1024 * jsub + 64 * hh + 64],
                            ETB[:, 512 * hp + 256 * jsub:512 * hp + 256 * jsub + 256],
                            start=(jsub == 0), stop=(jsub == 1),
                            tile_position=(0, poff)),
                            waits=([(s_ac, ev_etb), (s_dv, ev_v_all)]
                                   + last.get(("bank", 7), [])) if jsub == 0 else [],
                            inc=(s_pe if jsub == 1 else None))
                    ev = em.op("vector", lambda h, hh=hh, poff=poff, chq=chq,
                               yf0=yf0: h.tensor_copy(
                        YF[poff:poff + 64, yf0 + 256 * chq:yf0 + 256 * chq + 256],
                        P[7][poff:poff + 64, 0:256]),
                        waits=[(s_pe, evt)] + last.get(("YF", par), []), inc=s_dv)
                    last[("bank", 7)] = [(s_dv, ev)]
                    last[("ETB", hp)] = [(s_pe, evt)]
                    ev_y.append(ev)
                last[("YF", par)] = []
                ev_y_all = ev_y[-1]
                for s in range(2):
                    for half in range(2):
                        bk = (2 * s + half) % 2
                        evt = None
                        for c in range(EC):
                            evt = em.op("tensor", lambda h, c=c, s=s, half=half,
                                        bk=bk, yf0=yf0: h.matmul(
                                P[bk][:, 0:512],
                                YF[:, yf0 + 256 * c + 128 * s:
                                   yf0 + 256 * c + 128 * s + 128],
                                WPB[:, E * c + 512 * half:E * c + 512 * half + 512],
                                start=(c == 0), stop=(c == EC - 1)),
                                waits=([(s_dv, ev_y_all), (s_ld, ev_ldall)]
                                       + last.get(("bank", bk), [])) if c == 0 else [],
                                inc=(s_pe if c == EC - 1 else None))
                        ev = em.op("vector", lambda h, s=s, half=half, bk=bk,
                                   w=w: h.tensor_tensor(
                            XR[:, E * (2 * w + s) + 512 * half:
                               E * (2 * w + s) + 512 * half + 512],
                            XR[:, E * (2 * w + s) + 512 * half:
                               E * (2 * w + s) + 512 * half + 512],
                            P[bk][:, 0:512], ALU.add),
                            waits=[(s_pe, evt)], inc=s_dv)
                        last[("bank", bk)] = [(s_dv, ev)]
                last[("LNT", par)] = [(s_dv, s_dv.v)]
                if DEBUG and w == 0:
                    with nc.sbuf_tensor("DBGF", [128, 2048], F32) as DBGF:
                        def dump(name, srcap, n):
                            ev = em.op("vector", lambda h, srcap=srcap, n=n:
                                       h.tensor_copy(DBGF[:, 0:n], srcap),
                                       waits=[(s_dv, s_dv.v), (s_ac, s_ac.v),
                                              (s_pe, s_pe.v), (s_pl, s_pl.v),
                                              (s_st, s_st.v)],
                                       inc=s_dv)
                            dma("sync", dbg[name][:, 0:n], DBGF[:, 0:n],
                                waits=[(s_dv, ev)], sem=s_st)
                        dump("dlnt", LNT[:, 0:2048], 2048)
                        dump("dlnf", LNF[:, 0:2048], 2048)
                        dump("dqf", QF[:, 0:2048], 2048)
                        dump("dkf", KF[:, 0:2048], 2048)
                        dump("dvt", VT[:, 0:2048], 2048)
                        dump("desc", ESC[:, 0:512], 512)
                        dump("detb", ETB[:, 0:1024], 1024)
                        dump("dyf", YF[:, 0:2048], 2048)
                        dump("dstat", STAT[:, :], 128)
                        em.flush(nc)
            em.flush(nc)

        s_w1s = [Sem(es, nc, f"s_w1s{i}") for i in range(2)]
        s_w2s = [Sem(es, nc, f"s_w2s{i}") for i in range(2)]
        s_hl = Sem(es, nc, "s_hl")
        ev_spills = s_st.v

        def emit_ffn(idx):
            with nc.sbuf_tensor(f"XQF{idx}", [128, EC * 2052], BF16) as XQF, \
                 nc.sbuf_tensor(f"LNS{idx}", [128, 2 * E], F32) as LNS, \
                 nc.sbuf_tensor(f"XQS{idx}", [128, 2 * E], BF16) as XQS, \
                 nc.sbuf_tensor(f"AG16{idx}", [16, E], BF16) as AG16, \
                 nc.sbuf_tensor(f"W1S{idx}", [128, 2 * 4096], BF16) as W1S, \
                 nc.sbuf_tensor(f"W2S{idx}", [128, 2 * 4096], BF16) as W2S, \
                 nc.sbuf_tensor(f"G{idx}", [128, FC * 256], F32) as G, \
                 nc.sbuf_tensor(f"CT{idx}", [128, 2 * 512], F32) as CT, \
                 nc.sbuf_tensor(f"GS{idx}", [128, 2 * 256], F32) as GS, \
                 nc.sbuf_tensor(f"GQ{idx}", [128, FC * 256], BF16) as GQ, \
                 nc.sbuf_tensor(f"MB{idx}", [128, 256], F32) as MB, \
                 nc.sbuf_tensor(f"S2R{idx}", [1, 256], F32) as S2R, \
                 nc.sbuf_tensor(f"S2B{idx}", [128, 256], F32) as S2B, \
                 nc.sbuf_tensor(f"QST{idx}", [128, 64], F32) as QST:

                last = {}
                ev_stats = emit_stage_stats(1e-5, ev_ready=_EV_LDALL)
                em.op("gpsimd", lambda h: h.memset(
                    V(XQF, 2052)[:, :, 0:1], 0.0))
                ev_guard0 = em.op("gpsimd", lambda h: h.memset(
                    V(XQF, 2052)[:, :, 2049:2052], 0.0), inc=s_pl)

                ev_xqs = {}
                for tt in range(NT):
                    par = tt % 2
                    ev_ln = em.op("vector", lambda h, tt=tt, par=par: h.tensor_scalar(
                        LNS[:, par * E:(par + 1) * E], XR[:, E * tt:E * (tt + 1)],
                        STAT[:, 32 + tt:33 + tt], STAT[:, 48 + tt:49 + tt],
                        ALU.subtract, ALU.mult),
                        waits=[(s_dv, ev_stats)], inc=s_dv)
                    ev_mx = em.op("vector", lambda h, tt=tt, par=par: h.reduce_max(
                        QST[:, tt:tt + 1], LNS[:, par * E:(par + 1) * E], axis=AXX,
                        apply_absolute_value=True),
                        waits=[(s_dv, ev_ln)], inc=s_dv)
                    ev_mc = em.op("vector", lambda h, tt=tt: h.tensor_scalar(
                        QST[:, 16 + tt:17 + tt], QST[:, tt:tt + 1], 1e-5, None,
                        ALU.max), waits=[(s_dv, ev_mx)], inc=s_dv)
                    em.op("vector", lambda h, tt=tt: h.reciprocal(
                        QST[:, 32 + tt:33 + tt], QST[:, 16 + tt:17 + tt]),
                        waits=[(s_dv, ev_mc)])
                    ev_ri = em.op("vector", lambda h: h.drain(), inc=s_dv)
                    em.op("vector", lambda h, tt=tt, par=par: h.tensor_scalar(
                        LNS[:, par * E:(par + 1) * E], LNS[:, par * E:(par + 1) * E],
                        QST[:, 32 + tt:33 + tt], 127.0, ALU.mult, ALU.mult),
                        waits=[(s_dv, ev_ri)] + last.get(("XQS", par), []))
                    em.op("vector", lambda h, par=par: h.tensor_scalar(
                        LNS[:, par * E:(par + 1) * E], LNS[:, par * E:(par + 1) * E],
                        MAGIC, -MAGIC, ALU.add, ALU.add))
                    ev_xi = em.op("vector", lambda h, tt=tt, par=par: h.tensor_scalar(
                        XQS[:, par * E:(par + 1) * E], LNS[:, par * E:(par + 1) * E],
                        QST[:, 16 + tt:17 + tt], 1.0 / 127.0, ALU.mult, ALU.mult),
                        inc=s_dv)
                    ev_xqs[tt] = ev_xi
                    if tt == 0:
                        ev_h0 = dma("sync", ag_in[idx][0:1, :], XQS[0:1, 0:E],
                                    waits=[(s_dv, ev_xi)], sem=s_hl)
                    if tt == NT - 1:
                        ev_h1 = dma("sync", ag_in[idx][1:2, :], XQS[127:128, par * E:(par + 1) * E],
                                    waits=[(s_dv, ev_xi)], sem=s_hl)
                    for grp in range(2):
                        bk = grp
                        evt = None
                        for jj in range(4):
                            j = grp * 4 + jj
                            evt = em.op("tensor", lambda h, j=j, jj=jj, bk=bk, par=par: h.matmul(
                                P[bk][:, jj * 128:(jj + 1) * 128],
                                XQS[:, par * E + j * 128:par * E + (j + 1) * 128],
                                ID[:, :], start=True, stop=True),
                                waits=([(s_dv, ev_xi)] + last.get(("bank", bk), []))
                                      if jj == 0 else [],
                                inc=(s_pe if jj == 3 else None))
                        ev = em.op("vector", lambda h, tt=tt, grp=grp, bk=bk: h.tensor_copy(
                            V(XQF, 2052)[:, grp * 4:grp * 4 + 4, 1 + 128 * tt:1 + 128 * tt + 128],
                            P[bk][:, 0:512].rearrange("p (c w) -> p c w", w=128)),
                            waits=[(s_pe, evt)], inc=s_dv)
                        last[("bank", bk)] = [(s_dv, ev)]
                    last[("XQS", par)] = [(s_pe, s_pe.v)] + (
                        [(s_hl, ev_h0)] if tt == 0 else [])
                ev_ag = em.op("gpsimd", lambda h: h.collective_compute(
                    "AllGather", ALU.bypass, replica_groups=[list(range(NC))],
                    ins=[ag_in[idx].ap().opt()], outs=[ag_out[idx].ap().opt()]),
                    waits=[(s_hl, s_hl.v)], inc=s_cc)
                ev_ag16 = em.op("gpsimd", lambda h: h.dma_start(
                    out=AG16[:, :], in_=ag_out[idx][:, :]),
                    waits=[(s_cc, ev_ag)], inc=s_ld, inc_n=16)
                evg = None
                for side in range(2):
                    for j in range(8):
                        evg = em.op("tensor", lambda h, side=side, j=j: h.matmul(
                            P[2][:, 8 * side + j:8 * side + j + 1],
                            AG16[:, 128 * j:128 * j + 128], SELc[:, side:side + 1],
                            start=True, stop=True),
                            waits=[(s_ld, s_ld.v), (s_pl, ev_guard0)]
                                  if (side == 0 and j == 0) else [],
                            inc=(s_pe if (side == 1 and j == 7) else None))
                ev_gl = em.op("vector", lambda h: h.tensor_copy(
                    V(XQF, 2052)[:, :, 0:1],
                    P[2][:, 0:8].rearrange("p (c w) -> p c w", w=1)),
                    waits=[(s_pe, evg)], inc=s_dv)
                ev_gr = em.op("vector", lambda h: h.tensor_copy(
                    V(XQF, 2052)[:, :, 2049:2050],
                    P[2][:, 8:16].rearrange("p (c w) -> p c w", w=1)),
                    inc=s_dv)
                last[("bank", 2)] = [(s_dv, ev_gr)]
                ev_xqf = s_dv.v
                if FFN_PART <= 1:
                    em.flush(nc)
                    return

                spill1 = w1q_sp if idx == 0 or True else None
                def w1_dma(g):
                    slot = g % 2
                    wv = [(s_pe, last.get(("w1use", slot), 0)), (s_st, ev_spills)]
                    return em.op("sync", lambda h, g=g, slot=slot: h.dma_start(
                        out=V(W1S, 512)[:, slot * 8:slot * 8 + 8, :],
                        in_=w1q_sp.ap()[:, 512 * g:512 * (g + 1)].rearrange(
                            "(j p) f -> p j f", p=128)),
                        waits=[w for w in wv if w[1] > 0], inc=s_w1s[slot], inc_n=16)

                def w2_dma(g):
                    slot = g % 2
                    wv = [(s_pe, last.get(("w2use", slot), 0)), (s_st, ev_spills)]
                    return em.op("sync", lambda h, g=g, slot=slot: h.dma_start(
                        out=V(W2S, 1024)[:, slot * 4:slot * 4 + 4, :],
                        in_=w2q_sp.ap()[512 * g:512 * (g + 1), :].rearrange(
                            "(j p) f -> p j f", p=128)),
                        waits=[w for w in wv if w[1] > 0], inc=s_w2s[slot], inc_n=16)

                ev_gs = {}
                ev_mm2 = {}
                ev_gq = {}

                def pass1(k):
                    ev_w1 = {}
                    ev_w1[0] = w1_dma(0)
                    ev_w1[1] = w1_dma(1)
                    for c in range(FC):
                        g = c // 4
                        slot = g % 2
                        par = c % 2
                        bk = par
                        evt = None
                        for d in range(EC):
                            evt = em.op("tensor", lambda h, c=c, d=d, k=k, slot=slot,
                                        bk=bk: h.matmul(
                                P[bk][:, 0:258],
                                V(W1S, 4096)[:, slot, 512 * d + 128 * (c % 4):
                                             512 * d + 128 * (c % 4) + 128],
                                XQF[:, 2052 * d + 256 * k:2052 * d + 256 * k + 258],
                                start=(d == 0), stop=(d == EC - 1)),
                                waits=([(s_w1s[slot], ev_w1[g]), (s_dv, ev_xqf)]
                                       + last.get(("bank", bk), [])) if d == 0 else [],
                                inc=(s_pe if d == EC - 1 else None))
                        if c % 4 == 3:
                            last[("w1use", slot)] = evt
                            if g + 2 <= 7:
                                ev_w1[g + 2] = w1_dma(g + 2)
                        ev_t0 = em.op("scalar", lambda h, c=c, par=par, bk=bk: h.activation(
                            CT[:, 512 * par:512 * par + 256], P[bk][:, 0:256],
                            AF.Copy, scale=CWM[:, 3 * c:3 * c + 1]),
                            waits=[(s_pe, evt)] + last.get(("CTp", par), []), inc=s_ac)
                        ev_t2 = em.op("scalar", lambda h, c=c, par=par, bk=bk: h.activation(
                            CT[:, 512 * par + 256:512 * par + 512], P[bk][:, 2:258],
                            AF.Copy, scale=CWM[:, 3 * c + 2:3 * c + 3]),
                            inc=s_ac)
                        ev_st = em.op("vector", lambda h, c=c, par=par, bk=bk: h.scalar_tensor_tensor(
                            CT[:, 512 * par:512 * par + 256], P[bk][:, 1:257],
                            CWM[:, 3 * c + 1:3 * c + 2],
                            CT[:, 512 * par:512 * par + 256], ALU.mult, ALU.add),
                            waits=[(s_ac, ev_t2)], inc=s_dv)
                        last[("bank", bk)] = [(s_dv, ev_st), (s_ac, ev_t2)]
                        ev_pl = em.op("gpsimd", lambda h, par=par: h.tensor_tensor(
                            CT[:, 512 * par:512 * par + 256],
                            CT[:, 512 * par:512 * par + 256],
                            CT[:, 512 * par + 256:512 * par + 512], ALU.add),
                            waits=[(s_dv, ev_st), (s_ac, ev_t2)], inc=s_pl)
                        ev_si = em.op("scalar", lambda h, c=c, par=par: h.activation(
                            G[:, 256 * c:256 * c + 256], CT[:, 512 * par:512 * par + 256],
                            AF.Silu, bias=CBc[:, c:c + 1], scale=1.0),
                            waits=[(s_pl, ev_pl)] + ([(s_dv, ev_gs[c])] if c in ev_gs else []),
                            inc=s_ac)
                        last[("CTp", par)] = [(s_ac, ev_si), (s_pl, ev_pl)]

                def pass15(k):
                    ev_mb = em.op("vector", lambda h: h.reduce_max(
                        MB[:, :].rearrange("p (t u) -> p t u", u=1),
                        G[:, :].rearrange("p (c t) -> p t c", c=FC),
                        axis=AXX, apply_absolute_value=True),
                        waits=[(s_ac, s_ac.v)] + last.get(("MBr", 0), []), inc=s_dv)
                    evt = None
                    for sM in range(2):
                        evt = em.op("tensor", lambda h, sM=sM: h.matmul(
                            P[2][:, 128 * sM:128 * sM + 128], MB[:, 128 * sM:128 * sM + 128],
                            IDF[:, :], start=True, stop=True),
                            waits=([(s_dv, ev_mb)] + last.get(("bank", 2), []))
                                  if sM == 0 else [],
                            inc=(s_pe if sM == 1 else None))
                    last[("MBr", 0)] = [(s_pe, evt)]
                    ev_mt = None
                    for sM in range(2):
                        ev_mt = em.op("vector", lambda h, sM=sM: h.reduce_max(
                            QST[:, 48 + sM:49 + sM], P[2][:, 128 * sM:128 * sM + 128],
                            axis=AXX), waits=[(s_pe, evt)], inc=s_dv)
                    ev_mc2 = em.op("vector", lambda h: h.tensor_scalar(
                        QST[:, 50:52], QST[:, 48:50], 1e-5, None, ALU.max),
                        waits=[(s_dv, ev_mt)], inc=s_dv)
                    ev_u2 = em.op("vector", lambda h: h.tensor_scalar(
                        QST[:, 52:54], QST[:, 50:52], M2B[:, 0:1], 1.0 / 127.0,
                        ALU.mult, ALU.mult), waits=[(s_dv, ev_mc2)], inc=s_dv)
                    em.op("vector", lambda h: h.reciprocal(
                        QST[:, 54:56], QST[:, 50:52]),
                        waits=[(s_dv, ev_mc2)])
                    ev_rc2 = em.op("vector", lambda h: h.drain(), inc=s_dv)
                    ev_s2c = em.op("vector", lambda h: h.tensor_scalar(
                        QST[:, 56:58], QST[:, 54:56], 127.0, None, ALU.mult),
                        waits=[(s_dv, ev_rc2)], inc=s_dv)
                    evt = None
                    for sM in range(2):
                        evt = em.op("tensor", lambda h, sM=sM: h.matmul(
                            P[3][0:1, 128 * sM:128 * sM + 128],
                            QST[:, 56 + sM:57 + sM], IDF[:, :],
                            start=True, stop=True),
                            waits=([(s_dv, ev_s2c)] + last.get(("bank", 3), []))
                                  if sM == 0 else [],
                            inc=(s_pe if sM == 1 else None))
                    ev_s2r = em.op("vector", lambda h: h.tensor_copy(
                        S2R[0:1, :], P[3][0:1, 0:256]),
                        waits=[(s_pe, evt)], inc=s_dv)
                    last[("bank", 3)] = [(s_dv, ev_s2r)]
                    ev_bc = em.op("tensor", lambda h: h.matmul(
                        P[2][:, 128:384], ONE1F[0:1, :], S2R[0:1, :],
                        start=True, stop=True),
                        waits=[(s_dv, ev_s2r), (s_dv, ev_mt)], inc=s_pe)
                    ev_s2b = em.op("vector", lambda h: h.tensor_copy(
                        S2B[:, :], P[2][:, 128:384]),
                        waits=[(s_pe, ev_bc)], inc=s_dv)
                    last[("bank", 2)] = [(s_dv, ev_s2b)]
                    return ev_s2b, ev_u2

                def quant(k, ev_s2b):
                    prev_mm2 = dict(ev_mm2)
                    ev_gs.clear()
                    ev_gq.clear()
                    for c in range(FC):
                        par = c % 2
                        ev_g = em.op("vector", lambda h, c=c, par=par: h.tensor_tensor(
                            GS[:, 256 * par:256 * par + 256], G[:, 256 * c:256 * c + 256],
                            S2B[:, :], ALU.mult),
                            waits=[(s_dv, ev_s2b)]
                                  + ([(s_pe, prev_mm2[c])] if c in prev_mm2 else []),
                            inc=s_dv)
                        ev_gs[c] = ev_g
                        ev_gq[c] = em.op("vector", lambda h, c=c, par=par: h.tensor_scalar(
                            GQ[:, 256 * c:256 * c + 256], GS[:, 256 * par:256 * par + 256],
                            MAGIC, -MAGIC, ALU.add, ALU.add), inc=s_dv)

                def pass2(k, ev_u2):
                    ev_w2 = {}
                    ev_w2[0] = w2_dma(0)
                    ev_w2[1] = w2_dma(1)
                    ev_mm2.clear()
                    for c in range(FC):
                        g = c // 4
                        slot = g % 2
                        evt = None
                        for sM in range(2):
                            for eh in range(2):
                                bi = 4 + 2 * sM + eh
                                evt = em.op("tensor", lambda h, c=c, sM=sM, eh=eh,
                                            slot=slot, bi=bi: h.matmul(
                                    P[bi][:, 0:512],
                                    GQ[:, 256 * c + 128 * sM:256 * c + 128 * sM + 128],
                                    V(W2S, 4096)[:, slot, 1024 * (c % 4) + 512 * eh:
                                                 1024 * (c % 4) + 512 * eh + 512],
                                    start=(c == 0), stop=(c == FC - 1)),
                                    waits=([(s_dv, ev_gq[c]), (s_w2s[slot], ev_w2[g])]
                                           + (last.get(("bank", bi), []) if c == 0 else [])),
                                    inc=(s_pe if (sM == 1 and eh == 1) else None))
                        ev_mm2[c] = evt
                        if c % 4 == 3:
                            last[("w2use", slot)] = evt
                            if g + 2 <= 7:
                                ev_w2[g + 2] = w2_dma(g + 2)
                    for sM in range(2):
                        for eh in range(2):
                            bi = 4 + 2 * sM + eh
                            ev = em.op("vector", lambda h, k=k, sM=sM, eh=eh, bi=bi: h.scalar_tensor_tensor(
                                XR[:, E * (2 * k + sM) + 512 * eh:E * (2 * k + sM) + 512 * eh + 512],
                                P[bi][:, 0:512], QST[:, 52 + sM:53 + sM],
                                XR[:, E * (2 * k + sM) + 512 * eh:E * (2 * k + sM) + 512 * eh + 512],
                                ALU.mult, ALU.add),
                                waits=[(s_pe, ev_mm2[FC - 1]), (s_dv, ev_u2)], inc=s_dv)
                            last[("bank", bi)] = [(s_dv, ev)]

                pass1(0)
                for k in range(8):
                    ev_s2b, ev_u2 = pass15(k)
                    quant(k, ev_s2b)
                    if k + 1 < 8:
                        pass1(k + 1)
                    pass2(k, ev_u2)

                em.flush(nc)

        if STAGES >= 2:
            emit_ffn(0)

        def emit_attnb():
            with nc.sbuf_tensor("LN1F", [128, EC * 2048], BF16) as LN1F:
                emit_stage_stats(1e-5, extra_rho=True)
                ev_rho = s_dv.v
                with nc.sbuf_tensor("WKV", [128, 2 * EC * E], BF16) as WKV, \
                     nc.sbuf_tensor("QP", [128, EC * 2048], BF16) as QP, \
                     nc.sbuf_tensor("LNT2", [128, E], BF16) as LNT2, \
                     nc.sbuf_tensor("KPT", [128, E], BF16) as KPT, \
                     nc.sbuf_tensor("VTT", [128, E], BF16) as VTT, \
                     nc.sbuf_tensor("AD1T", [128, E], F32) as AD1T, \
                     nc.sbuf_tensor("ATT", [128, E], BF16) as ATT, \
                     nc.sbuf_tensor("ATF", [128, E], BF16) as ATF, \
                     nc.sbuf_tensor("DENR", [128, 36], F32) as DENR, \
                     nc.sbuf_tensor("KVSB", [64, H * 65], F32) as KVSB, \
                     nc.sbuf_tensor("KVSS", [128, H * 65], BF16) as KVSS:
                    last = {}
                    ev_wk = em.op("gpsimd", lambda h: h.dma_start(
                        out=V(WKV, E)[:, 0:EC, :],
                        in_=wk_d.ap().rearrange("(c p) e -> p c e", p=128)),
                        inc=s_ld, inc_n=16)
                    ev_wv = em.op("gpsimd", lambda h: h.dma_start(
                        out=V(WKV, E)[:, EC:2 * EC, :],
                        in_=wv_d.ap().rearrange("(c p) e -> p c e", p=128)),
                        inc=s_ld, inc_n=16)
                    ev_wkv = s_ld.v
                    ev_l1 = {}
                    for tt in range(NT):
                        ev_ln = em.op("vector", lambda h, tt=tt: h.tensor_scalar(
                            LNT2[:, :], XR[:, E * tt:E * (tt + 1)],
                            STAT[:, 32 + tt:33 + tt], STAT[:, 64 + tt:65 + tt],
                            ALU.subtract, ALU.mult),
                            waits=[(s_dv, ev_rho)] + last.get(("LNT2",), []), inc=s_dv)
                        for grp in range(2):
                            bk = grp
                            evt = None
                            for jj in range(4):
                                j = grp * 4 + jj
                                evt = em.op("tensor", lambda h, j=j, jj=jj, bk=bk: h.matmul(
                                    P[bk][:, jj * 128:(jj + 1) * 128],
                                    LNT2[:, j * 128:(j + 1) * 128],
                                    ID[:, :], start=True, stop=True),
                                    waits=([(s_dv, ev_ln)] + last.get(("bank", bk), []))
                                          if jj == 0 else [],
                                    inc=(s_pe if jj == 3 else None))
                            ev = None
                            for jj in range(4):
                                j = grp * 4 + jj
                                ev = em.op("vector", lambda h, tt=tt, j=j, jj=jj, bk=bk: h.tensor_scalar(
                                    LN1F[:, 2048 * j + 128 * tt:2048 * j + 128 * tt + 128],
                                    P[bk][:, 128 * jj:128 * jj + 128],
                                    GAMc[:, j:j + 1], BETc[:, j:j + 1],
                                    ALU.mult, ALU.add),
                                    waits=[(s_pe, evt)] if jj == 0 else [],
                                    inc=(s_dv if jj == 3 else None))
                            last[("bank", bk)] = [(s_dv, ev)]
                        last[("LNT2",)] = [(s_pe, s_pe.v)]
                        ev_l1[tt] = s_dv.v
                    if C_PART <= 1:
                        em.flush(nc)
                        return
                    for tt in range(NT):
                        evk = []
                        for (bk, wof) in ((2, 0), (3, EC)):
                            for half in range(2):
                                evt = None
                                for c in range(EC):
                                    evt = em.op("tensor", lambda h, c=c, tt=tt, half=half,
                                                bk=bk, wof=wof: h.matmul(
                                        P[bk][:, 512 * half:512 * half + 512] if False else
                                        P[bk][:, 0:512],
                                        LN1F[:, 2048 * c + 128 * tt:2048 * c + 128 * tt + 128],
                                        WKV[:, E * (wof + c) + 512 * half:
                                            E * (wof + c) + 512 * half + 512],
                                        start=(c == 0), stop=(c == EC - 1)),
                                        waits=([(s_dv, ev_l1[tt]), (s_ld, ev_wkv)]
                                               + last.get(("bank", bk), [])) if c == 0 else [],
                                        inc=(s_pe if c == EC - 1 else None))
                                if bk == 2:
                                    ev = em.op("vector", lambda h, half=half: h.tensor_scalar(
                                        KPT[:, 512 * half:512 * half + 512], P[2][:, 0:512],
                                        0.0, 1e-3, ALU.max, ALU.add),
                                        waits=[(s_pe, evt)] + last.get(("KPT",), []),
                                        inc=s_dv)
                                else:
                                    ev = em.op("vector", lambda h, half=half: h.tensor_copy(
                                        VTT[:, 512 * half:512 * half + 512], P[3][:, 0:512]),
                                        waits=[(s_pe, evt)] + last.get(("VTT",), []),
                                        inc=s_dv)
                                last[("bank", bk)] = [(s_dv, ev)]
                                evk.append(ev)
                        ev_kv = s_dv.v
                        evt = None
                        for hh in range(H):
                            bk = 6 + hh // 8
                            em.op("tensor", lambda h, hh=hh, bk=bk, tt=tt: h.matmul(
                                P[bk][0:64, 64 * (hh % 8):64 * (hh % 8) + 64],
                                KPT[:, 64 * hh:64 * hh + 64],
                                VTT[:, 64 * hh:64 * hh + 64],
                                start=(tt == 0 and hh % 8 == 0),
                                stop=(tt == NT - 1 and hh % 8 == 7),
                                skip_group_check=True),
                                waits=[(s_dv, ev_kv)] if hh == 0 else [])
                            evt = em.op("tensor", lambda h, hh=hh, tt=tt: h.matmul(
                                P[4][0:64, hh:hh + 1],
                                KPT[:, 64 * hh:64 * hh + 64], ONECB[:, 0:1],
                                start=(tt == 0 and hh == 0),
                                stop=(tt == NT - 1 and hh == H - 1),
                                skip_group_check=True),
                                inc=(s_pe if hh == H - 1 else None))
                        last[("KPT",)] = [(s_pe, evt)]
                        last[("VTT",)] = [(s_pe, evt)]
                    ev_kvs_done = s_pe.v
                    if C_PART <= 2:
                        em.flush(nc)
                        return
                    ev_wq = em.op("gpsimd", lambda h: h.dma_start(
                        out=V(WKV, E)[:, 0:EC, :],
                        in_=wq_d.ap().rearrange("(c p) e -> p c e", p=128)),
                        waits=[(s_pe, ev_kvs_done)], inc=s_ld, inc_n=16)
                    ev_wo = em.op("gpsimd", lambda h: h.dma_start(
                        out=V(WKV, E)[:, EC:2 * EC, :],
                        in_=wo_d.ap().rearrange("(c p) e -> p c e", p=128)),
                        inc=s_ld, inc_n=16)
                    for tt in range(NT):
                        for grp in range(2):
                            bk = 2 + grp
                            evt = None
                            for jj in range(4):
                                j = grp * 4 + jj
                                ev0 = None
                                for c in range(EC):
                                    evt = em.op("tensor", lambda h, c=c, j=j, jj=jj, tt=tt,
                                                bk=bk: h.matmul(
                                        P[bk][:, 128 * jj:128 * jj + 128],
                                        WKV[:, E * c + 128 * j:E * c + 128 * j + 128],
                                        LN1F[:, 2048 * c + 128 * tt:2048 * c + 128 * tt + 128],
                                        start=(c == 0), stop=(c == EC - 1)),
                                        waits=([(s_ld, ev_wq)] + last.get(("bank", bk), []))
                                              if (jj == 0 and c == 0) else [],
                                        inc=(s_pe if (jj == 3 and c == EC - 1) else None))
                            ev = em.op("vector", lambda h, tt=tt, grp=grp, bk=bk: h.tensor_scalar(
                                V(QP, 2048)[:, grp * 4:grp * 4 + 4, 128 * tt:128 * tt + 128],
                                P[bk][:, 0:512].rearrange("p (c w) -> p c w", w=128),
                                0.0, 1e-3, ALU.max, ALU.add),
                                waits=[(s_pe, evt)], inc=s_dv)
                            last[("bank", bk)] = [(s_dv, ev)]
                    ev_qp = s_dv.v
                    evs = []
                    for hh in range(H):
                        bk = 6 + hh // 8
                        ev = em.op("vector", lambda h, hh=hh, bk=bk: h.tensor_copy(
                            KVSB[:, 65 * hh:65 * hh + 64],
                            P[bk][0:64, 64 * (hh % 8):64 * (hh % 8) + 64]),
                            waits=[(s_pe, ev_kvs_done)] if hh == 0 else [], inc=None)
                    ev_ks = em.op("vector", lambda h: h.tensor_copy(
                        KVSB[:, :].rearrange("p (x u) -> p x u", u=65)[:, :, 64:65],
                        P[4][0:64, 0:16].rearrange("p (x u) -> p x u", u=1)),
                        inc=s_dv)
                    ev_kvsb = em.op("sync", lambda h: h.dma_start(
                        out=ar_in.ap(), in_=KVSB[:, :]),
                        waits=[(s_dv, ev_ks)], inc=s_st, inc_n=16)
                    ev_ar = em.op("gpsimd", lambda h: h.collective_compute(
                        "AllReduce", ALU.add,
                        replica_groups=[[0, 1, 2, 3], [4, 5, 6, 7]],
                        ins=[ar_in.ap().opt()], outs=[ar_out.ap().opt()]),
                        waits=[(s_st, ev_kvsb)], inc=s_cc)
                    em.op("gpsimd", lambda h: h.dma_start(
                        out=KVSS[0:64, :], in_=ar_out.ap()),
                        waits=[(s_cc, ev_ar)], inc=s_ld, inc_n=16)
                    em.op("gpsimd", lambda h: h.dma_start(
                        out=KVSS[64:128, :], in_=ar_out.ap()),
                        inc=s_ld, inc_n=16)
                    ev_kvss_v = s_ld.v
                    last[("bank", 4)] = [(s_dv, ev_ks)]
                    last[("bank", 6)] = [(s_dv, ev_ks)]
                    last[("bank", 7)] = [(s_dv, ev_ks)]
                    if C_PART <= 3:
                        em.flush(nc)
                        return
                    if DEBUG_C:
                        dma("gpsimd", dbg["cln1f"][:, :], LN1F[:, 0:2048],
                            waits=[(s_dv, s_dv.v), (s_pe, s_pe.v)], sem=s_st)
                        dma("gpsimd", dbg["cqp"][:, :], QP[:, 0:2048],
                            waits=[(s_dv, s_dv.v)], sem=s_st)
                        dma("sync", dbg["ckvsb"][:, :], KVSB[:, :],
                            waits=[(s_dv, s_dv.v)], sem=s_st)
                        dma("gpsimd", dbg["ckvss"][:, :], KVSS[:, :],
                            waits=[(s_ld, s_ld.v)], sem=s_st)
                    for tt in range(NT):
                        ev_nd = [None] * 4
                        for hh in range(H):
                            bk = hh % 4
                            evt = em.op("tensor", lambda h, hh=hh, tt=tt, bk=bk: h.matmul(
                                P[bk][:, 68 * (hh // 4):68 * (hh // 4) + 65],
                                QP[64 * (hh % 2):64 * (hh % 2) + 64,
                                   2048 * (hh // 2) + 128 * tt:2048 * (hh // 2) + 128 * tt + 128],
                                KVSS[64 * (hh % 2):64 * (hh % 2) + 64, 65 * hh:65 * hh + 65],
                                start=True, stop=True),
                                waits=([(s_ld, ev_kvss_v), (s_dv, ev_qp)]
                                       + last.get(("bank", bk), [])) if hh < 4 else [],
                                inc=(s_pe if hh >= 12 else None))
                            if hh >= 12:
                                ev_nd[hh % 4] = evt
                        for b4 in range(4):
                            em.op("vector", lambda h, b4=b4: h.tensor_copy(
                                DENR[:, 4 * b4:4 * b4 + 4].rearrange("p (x u) -> p x u", u=1),
                                P[b4][:, 0:272].rearrange("p (x u) -> p x u", u=68)[:, :, 64:65]),
                                waits=[(s_pe, ev_nd[b4])] + (last.get(("DENR",), []) if b4 == 0 else []))
                        em.op("vector", lambda h: h.drain())
                        em.op("vector", lambda h: h.reciprocal(DENR[:, 16:32], DENR[:, 0:16]))
                        ev_den = em.op("vector", lambda h: h.drain(), inc=s_dv)
                        ev_att = None
                        for hh in range(H):
                            bk = hh % 4
                            di = 16 + 4 * (hh % 4) + hh // 4
                            ev_att = em.op("vector", lambda h, hh=hh, bk=bk, di=di: h.tensor_scalar(
                                ATT[:, 64 * hh:64 * hh + 64],
                                P[bk][:, 68 * (hh // 4):68 * (hh // 4) + 64],
                                DENR[:, di:di + 1], None, ALU.mult),
                                waits=([(s_dv, ev_den)] + last.get(("ATT",), []))
                                      if hh == 0 else [],
                                inc=(s_dv if hh == H - 1 else None))
                        for b4 in range(4):
                            last[("bank", b4)] = [(s_dv, ev_att)]
                        last[("DENR",)] = [(s_dv, ev_att)]
                        if C2_SUB <= 1:
                            continue
                        for grp in range(2):
                            evt = None
                            for jj in range(4):
                                j = grp * 4 + jj
                                evt = em.op("tensor", lambda h, j=j, jj=jj: h.matmul(
                                    P[5][:, 128 * jj:128 * jj + 128],
                                    ATT[:, 128 * j:128 * j + 128], ID[:, :],
                                    start=True, stop=True),
                                    waits=([(s_dv, ev_att)] + last.get(("bank", 5), []))
                                          if jj == 0 else [],
                                    inc=(s_pe if jj == 3 else None))
                            ev = em.op("scalar", lambda h, grp=grp: h.activation(
                                V(ATF, 128)[:, grp * 4:grp * 4 + 4, :],
                                P[5][:, 0:512].rearrange("p (c w) -> p c w", w=128),
                                AF.Copy),
                                waits=[(s_pe, evt)] + (last.get(("ATF",), []) if grp == 0 else []),
                                inc=s_ac)
                            last[("bank", 5)] = [(s_ac, ev)]
                        ev_atf = s_ac.v
                        last[("ATT",)] = [(s_pe, s_pe.v)]
                        ev_wo_mm = []
                        for half in range(2):
                            bk = 6 + half
                            evt = None
                            for j in range(EC):
                                evt = em.op("tensor", lambda h, j=j, half=half, bk=bk: h.matmul(
                                    P[bk][:, 0:512],
                                    ATF[:, 128 * j:128 * j + 128],
                                    WKV[:, E * (EC + j) + 512 * half:
                                        E * (EC + j) + 512 * half + 512],
                                    start=(j == 0), stop=(j == EC - 1)),
                                    waits=([(s_ac, ev_atf), (s_ld, ev_wo)]
                                           + last.get(("bank", bk), [])) if j == 0 else [],
                                    inc=(s_pe if j == EC - 1 else None))
                            ev_wo_mm.append(evt)
                        ev_lo = em.op("vector", lambda h, tt=tt: h.tensor_scalar(
                            AD1T[:, :], XR[:, E * tt:E * (tt + 1)],
                            STAT[:, 32 + tt:33 + tt], STAT[:, 48 + tt:49 + tt],
                            ALU.subtract, ALU.mult),
                            waits=[(s_dv, s_dv.v)] + last.get(("AD1T",), []), inc=s_dv)
                        for half in range(2):
                            bk = 6 + half
                            ev = em.op("vector", lambda h, half=half, bk=bk: h.tensor_tensor(
                                AD1T[:, 512 * half:512 * half + 512],
                                AD1T[:, 512 * half:512 * half + 512],
                                P[bk][:, 0:512], ALU.add),
                                waits=[(s_pe, ev_wo_mm[half]), (s_dv, ev_lo)], inc=s_dv)
                            last[("bank", bk)] = [(s_dv, ev)]
                        ev_add1 = s_dv.v
                        if C2_SUB <= 2:
                            last[("AD1T",)] = [(s_dv, ev_add1)]
                            continue
                        qb = 8 * (tt % 2)
                        ev_s = em.op("vector", lambda h, qb=qb: h.reduce_sum(
                            QST2[:, qb + 0:qb + 1], AD1T[:, :], axis=AXX),
                            waits=[(s_dv, ev_add1)] + last.get(("QST2", tt % 2), []),
                            inc=s_dv)
                        ev_q2 = em.op("scalar", lambda h, qb=qb: h.activation(
                            SQT[:, :], AD1T[:, :], AF.Square, accum_out=QST2[:, qb + 1:qb + 2]),
                            waits=[(s_dv, ev_add1)])
                        ev_q2 = em.op("scalar", lambda h: h.drain(), inc=s_ac)
                        em.op("vector", lambda h, qb=qb: h.tensor_scalar(
                            QST2[:, qb + 2:qb + 3], QST2[:, qb + 0:qb + 1], 1.0 / E, None, ALU.mult),
                            waits=[(s_dv, ev_s)], inc=s_dv)
                        ev_m2x = s_dv.v
                        ev_ms = em.op("scalar", lambda h, qb=qb: h.activation(
                            QST2[:, qb + 3:qb + 4], QST2[:, qb + 2:qb + 3], AF.Square),
                            waits=[(s_dv, ev_m2x)], inc=s_ac)
                        em.op("vector", lambda h, qb=qb: h.tensor_scalar(
                            QST2[:, qb + 4:qb + 5], QST2[:, qb + 1:qb + 2], 1.0 / E, None, ALU.mult),
                            waits=[(s_ac, ev_q2)])
                        em.op("vector", lambda h, qb=qb: h.tensor_tensor(
                            QST2[:, qb + 4:qb + 5], QST2[:, qb + 4:qb + 5], QST2[:, qb + 3:qb + 4],
                            ALU.subtract),
                            waits=[(s_ac, ev_ms)])
                        ev_v2 = em.op("vector", lambda h, qb=qb: h.tensor_scalar(
                            QST2[:, qb + 5:qb + 6], QST2[:, qb + 4:qb + 5], 1e-6, None, ALU.add),
                            inc=s_dv)
                        ev_sd2 = em.op("scalar", lambda h, qb=qb: h.activation(
                            QST2[:, qb + 6:qb + 7], QST2[:, qb + 5:qb + 6], AF.Sqrt),
                            waits=[(s_dv, ev_v2)], inc=s_ac)
                        em.op("vector", lambda h, qb=qb: h.reciprocal(
                            QST2[:, qb + 7:qb + 8], QST2[:, qb + 6:qb + 7]),
                            waits=[(s_ac, ev_sd2)])
                        ev_r2 = em.op("vector", lambda h: h.drain(), inc=s_dv)
                        last[("QST2", tt % 2)] = [(s_ac, ev_sd2)]
                        ev_ln2 = em.op("vector", lambda h, qb=qb: h.tensor_scalar(
                            LNT2[:, :], AD1T[:, :], QST2[:, qb + 2:qb + 3], QST2[:, qb + 7:qb + 8],
                            ALU.subtract, ALU.mult),
                            waits=[(s_dv, ev_r2)] + last.get(("LNT2",), []), inc=s_dv)
                        for grp in range(2):
                            bk = grp
                            evt = None
                            for jj in range(4):
                                j = grp * 4 + jj
                                evt = em.op("tensor", lambda h, j=j, jj=jj, bk=bk: h.matmul(
                                    P[bk][:, jj * 128:(jj + 1) * 128],
                                    LNT2[:, j * 128:(j + 1) * 128],
                                    ID[:, :], start=True, stop=True),
                                    waits=([(s_dv, ev_ln2)] + last.get(("bank", bk), []))
                                          if jj == 0 else [],
                                    inc=(s_pe if jj == 3 else None))
                            ev = None
                            for jj in range(4):
                                j = grp * 4 + jj
                                ev = em.op("vector", lambda h, tt=tt, j=j, jj=jj, bk=bk: h.tensor_scalar(
                                    LN1F[:, 2048 * j + 128 * tt:2048 * j + 128 * tt + 128],
                                    P[bk][:, 128 * jj:128 * jj + 128],
                                    GAMc[:, j:j + 1], BETc[:, j:j + 1],
                                    ALU.mult, ALU.add),
                                    waits=[(s_pe, evt)] if jj == 0 else [],
                                    inc=(s_dv if jj == 3 else None))
                            last[("bank", bk)] = [(s_dv, ev)]
                        last[("LNT2",)] = [(s_pe, s_pe.v)]
                        if DEBUG_C and tt == 12:
                            dma("sync", dbg["cad9"][:, :], AD1T[:, :],
                                waits=[(s_dv, s_dv.v)], sem=s_st)
                            em.op("vector", lambda h: h.engine_nop(),
                                  waits=[(s_st, s_st.v)])
                        if DEBUG_C and tt == 12:
                            dma("gpsimd", dbg["catt"][:, :], ATT[:, :],
                                waits=[(s_dv, s_dv.v), (s_pe, s_pe.v)], sem=s_st)
                            dma("sync", dbg["cad1"][:, :], AD1T[:, :],
                                waits=[(s_dv, s_dv.v)], sem=s_st)
                            dma("sync", dbg["cden"][:, :], DENR[:, :],
                                waits=[(s_dv, s_dv.v)], sem=s_st)
                            dma("gpsimd", dbg["cln2f"][:, :], LN1F[:, 0:2048],
                                waits=[(s_dv, s_dv.v)], sem=s_st)
                            em.op("vector", lambda h: h.engine_nop(),
                                  waits=[(s_st, s_st.v)])
                        if DEBUG_C and tt == 151:
                            dma("gpsimd", dbg["cln2t15"][:, :], LN1F[:, 0:2048],
                                waits=[(s_dv, s_dv.v), (s_pe, s_pe.v)], sem=s_st)
                            em.op("vector", lambda h: h.engine_nop(),
                                  waits=[(s_st, s_st.v)])
                            em.op("gpsimd", lambda h: h.engine_nop(),
                                  waits=[(s_st, s_st.v)])
                        ev_xu = em.op("vector", lambda h, tt=tt: h.tensor_tensor(
                            XR[:, E * tt:E * (tt + 1)], XR[:, E * tt:E * (tt + 1)],
                            AD1T[:, :], ALU.add), inc=s_dv)
                        last[("AD1T",)] = [(s_dv, ev_xu), (s_ac, ev_q2)]
                        if C2_SER:
                            em.op("vector", lambda h: h.drain(),
                                  waits=[(s_ac, s_ac.v), (s_pe, s_pe.v), (s_pl, s_pl.v)],
                                  inc=s_dv)
                            em.op("scalar", lambda h: h.drain(),
                                  waits=[(s_dv, s_dv.v)], inc=s_ac)
                            em.op("tensor", lambda h: h.nop(),
                                  waits=[(s_dv, s_dv.v), (s_ac, s_ac.v)])
                if C_PART <= 4:
                    em.flush(nc)
                    return
                em.flush(nc)
                with nc.sbuf_tensor("HG", [128, FC * 512], BF16) as HG, \
                     nc.sbuf_tensor("OF", [128, EC * 512], BF16) as OF, \
                     nc.sbuf_tensor("BW1S", [128, 2 * 4096], BF16) as BW1S, \
                     nc.sbuf_tensor("BW2S", [128, 2 * 4096], BF16) as BW2S:
                    last = {}
                    s_b1 = [Sem(es, nc, f"s_b1{i}") for i in range(2)]
                    s_b2 = [Sem(es, nc, f"s_b2{i}") for i in range(2)]

                    def b1_dma(g):
                        slot = g % 2
                        wv = [(s_pe, last.get(("b1use", slot), 0)), (s_st, ev_spills)]
                        return em.op("sync", lambda h, g=g, slot=slot: h.dma_start(
                            out=V(BW1S, 512)[:, slot * 8:slot * 8 + 8, :],
                            in_=bw1_sp.ap()[:, 512 * g:512 * (g + 1)].rearrange(
                                "(j p) f -> p j f", p=128)),
                            waits=[w for w in wv if w[1] > 0], inc=s_b1[slot], inc_n=16)

                    def b2_dma(g):
                        slot = g % 2
                        wv = [(s_pe, last.get(("b2use", slot), 0)), (s_st, ev_spills)]
                        return em.op("sync", lambda h, g=g, slot=slot: h.dma_start(
                            out=V(BW2S, 1024)[:, slot * 4:slot * 4 + 4, :],
                            in_=bw2_sp.ap()[512 * g:512 * (g + 1), :].rearrange(
                                "(j p) f -> p j f", p=128)),
                            waits=[w for w in wv if w[1] > 0], inc=s_b2[slot], inc_n=16)

                    for tq in range(4):
                        ev_b1 = {}
                        ev_b1[0] = b1_dma(0)
                        ev_b1[1] = b1_dma(1)
                        for c in range(FC):
                            g = c // 4
                            slot = g % 2
                            bk = c % 2
                            evt = None
                            for j in range(EC):
                                evt = em.op("tensor", lambda h, c=c, j=j, tq=tq, slot=slot,
                                            bk=bk: h.matmul(
                                    P[bk][:, 0:512],
                                    V(BW1S, 512)[:, slot * 8 + j, 128 * (c % 4):
                                                 128 * (c % 4) + 128],
                                    LN1F[:, 2048 * j + 512 * tq:2048 * j + 512 * tq + 512],
                                    start=(j == 0), stop=(j == EC - 1)),
                                    waits=([(s_b1[slot], ev_b1[g]), (s_dv, s_dv.v)]
                                           + last.get(("bank", bk), [])) if j == 0 else [],
                                    inc=(s_pe if j == EC - 1 else None))
                            if c % 4 == 3:
                                last[("b1use", slot)] = evt
                                if g + 2 <= 7:
                                    ev_b1[g + 2] = b1_dma(g + 2)
                            ev = em.op("scalar", lambda h, c=c, bk=bk: h.activation(
                                HG[:, 512 * c:512 * c + 512], P[bk][:, 0:512],
                                AF.Gelu, bias=B1c[:, c:c + 1], scale=1.0),
                                waits=[(s_pe, evt)] + last.get(("HGw", c % 2), []),
                                inc=s_ac)
                            last[("bank", bk)] = [(s_ac, ev)]
                        ev_hg = s_ac.v
                        ev_b2 = {}
                        ev_b2[0] = b2_dma(0)
                        ev_b2[1] = b2_dma(1)
                        ev_ph2 = None
                        for c in range(FC):
                            g = c // 4
                            slot = g % 2
                            for j in range(EC):
                                ev_ph2 = em.op("tensor", lambda h, c=c, j=j, slot=slot: h.matmul(
                                    P[j][:, 0:512],
                                    V(BW2S, 1024)[:, slot * 4 + (c % 4), 128 * j:128 * j + 128],
                                    HG[:, 512 * c:512 * c + 512],
                                    start=(c == 0), stop=(c == FC - 1)),
                                    waits=([(s_b2[slot], ev_b2[g]), (s_ac, ev_hg)]
                                           + sum((last.get(("bank", bb), []) for bb in range(8)), []))
                                          if (c == 0 and j == 0) else [],
                                    inc=(s_pe if (c % 4 == 3 and j == EC - 1) else None))
                            if c % 4 == 3:
                                last[("b2use", slot)] = ev_ph2
                                if g + 2 <= 7:
                                    ev_b2[g + 2] = b2_dma(g + 2)
                        ev_of = []
                        for j in range(EC):
                            ev = em.op("scalar", lambda h, j=j: h.activation(
                                OF[:, 512 * j:512 * j + 512], P[j][:, 0:512],
                                AF.Gelu, bias=B2c[:, j:j + 1], scale=1.0),
                                waits=[(s_pe, ev_ph2)] + (last.get(("OFw",), []) if j == 0 else []),
                                inc=s_ac)
                            ev_of.append(ev)
                        for j in range(EC):
                            last[("bank", j)] = [(s_ac, ev_of[-1])]
                        for sq in range(4):
                            for grp in range(2):
                                bk = grp
                                evt = None
                                for jj in range(4):
                                    j = grp * 4 + jj
                                    evt = em.op("tensor", lambda h, j=j, jj=jj, sq=sq,
                                                bk=bk: h.matmul(
                                        P[bk][:, 128 * jj:128 * jj + 128],
                                        OF[:, 512 * j + 128 * sq:512 * j + 128 * sq + 128],
                                        ID[:, :], start=True, stop=True),
                                        waits=([(s_ac, ev_of[-1])] + last.get(("bank", bk), []))
                                              if jj == 0 else [],
                                        inc=(s_pe if jj == 3 else None))
                                tile = 4 * tq + sq
                                ev = em.op("vector", lambda h, tile=tile, grp=grp, bk=bk: h.tensor_tensor(
                                    XR[:, E * tile + 512 * grp:E * tile + 512 * grp + 512],
                                    XR[:, E * tile + 512 * grp:E * tile + 512 * grp + 512],
                                    P[bk][:, 0:512], ALU.add),
                                    waits=[(s_pe, evt)], inc=s_dv)
                                last[("bank", bk)] = [(s_dv, ev)]
                        last[("OFw",)] = [(s_pe, s_pe.v)]
                        last[("HGw", 0)] = [(s_pe, s_pe.v)]
                        last[("HGw", 1)] = [(s_pe, s_pe.v)]
                        if C3_SER:
                            em.op("vector", lambda h: h.drain(),
                                  waits=[(s_ac, s_ac.v), (s_pe, s_pe.v), (s_pl, s_pl.v)],
                                  inc=s_dv)
                            em.op("scalar", lambda h: h.drain(),
                                  waits=[(s_dv, s_dv.v)], inc=s_ac)
                            em.op("tensor", lambda h: h.nop(),
                                  waits=[(s_dv, s_dv.v), (s_ac, s_ac.v)])
                em.flush(nc)

        if STAGES >= 3:
            with nc.sbuf_tensor("QST2g", [128, 16], F32) as QST2:
                emit_attnb()
        if STAGES >= 4:
            emit_ffn(1)

        ev_out = dma("sync", out_d.ap().rearrange("(i p) f -> p i f", p=128),
                     V(XR, E), waits=[(s_dv, s_dv.v), (s_ac, s_ac.v), (s_pe, s_pe.v)],
                     sem=s_st)
        em.op("sync", lambda h: h.nop(), waits=[(s_st, ev_out)])
        em.flush(nc)

    return nc


def host_layout(inputs):
    xfull = np.ascontiguousarray(inputs["L0_attn_a_x"], dtype=np.float32)
    lay = dict(
        wattnT=np.ascontiguousarray(inputs["L0_attn_a_w_attn"].T),
        wprojT=np.ascontiguousarray(inputs["L0_attn_a_w_proj"].T),
        w1T=np.ascontiguousarray(inputs["L0_ffn_w1"].T),
        w2T=np.ascontiguousarray(inputs["L0_ffn_w2"].T),
        cw=np.ascontiguousarray(
            inputs["L0_ffn_conv_w"].reshape(F, 3).reshape(FC, 128, 3)
            .transpose(1, 0, 2).reshape(128, 96)),
        cb=np.ascontiguousarray(inputs["L0_ffn_conv_b"].reshape(FC, 128).T),
        gam=np.ascontiguousarray(inputs["L0_attn_b_gamma1"].reshape(EC, 128).T),
        bet=np.ascontiguousarray(inputs["L0_attn_b_beta1"].reshape(EC, 128).T),
        b1c=np.ascontiguousarray(inputs["L0_attn_b_b1"].reshape(FC, 128).T),
        b2c=np.ascontiguousarray(inputs["L0_attn_b_b2"].reshape(EC, 128).T),
        wqm=np.ascontiguousarray(inputs["L0_attn_b_wq"].reshape(E, E)),
        wkm=np.ascontiguousarray(inputs["L0_attn_b_wk"].reshape(E, E)),
        wvm=np.ascontiguousarray(inputs["L0_attn_b_wv"].reshape(E, E)),
        wom=np.ascontiguousarray(inputs["L0_attn_b_wo"].reshape(E, E)),
        bw1=np.ascontiguousarray(inputs["L0_attn_b_w1"]),
        bw2=np.ascontiguousarray(inputs["L0_attn_b_w2"]),
    )
    import ml_dtypes
    in_maps = []
    for c in range(NC):
        b, q = c // 4, c % 4
        sel = np.zeros((16, 2), ml_dtypes.bfloat16)
        if q > 0:
            sel[2 * (c - 1) + 1, 0] = 1
        if q < 3:
            sel[2 * (c + 1), 1] = 1
        m = dict(lay)
        m["x"] = np.ascontiguousarray(xfull[b, q * T:(q + 1) * T, :])
        m["sel"] = sel
        in_maps.append(m)
    return in_maps


def kernel(**inputs):
    in_maps = host_layout(inputs)
    nc = build_nc()
    res = run_bass_kernel_spmd(nc, in_maps, core_ids=list(range(NC)))
    outf = np.empty((2, 8192, E), np.float32)
    for c in range(NC):
        b, q = c // 4, c % 4
        outf[b, q * T:(q + 1) * T, :] = res.results[c]["out"]
    return outf


# version 04
# speedup vs baseline: 1.1812x; 1.0687x over previous
import numpy as np
from contextlib import ExitStack
import concourse.bass as bass
import concourse.mybir as mybir
from concourse.bass_utils import run_bass_kernel_spmd

F32 = mybir.dt.float32
BF16 = mybir.dt.bfloat16
AF = mybir.ActivationFunctionType
ALU = mybir.AluOpType
AXX = mybir.AxisListType.X

MAGIC = float(np.float32(1.5 * 2 ** 23))
STAGES = 4
FFN_PART = 4
FFN_P1 = 9
C_PART = 9
C2_SUB = 9
C2_SER = False
C3_SER = False
DEBUG = False
DEBUG_B = False
DEBUG_C = False

T = 2048
NT = 16
NW = 8
E = 1024
EC = 8
F = 4096
FC = 32
H = 16
NC = 8


class Sem:
    def __init__(self, es, nc, name):
        self.h = es.enter_context(nc.semaphore(name))
        self.v = 0

    def bump(self, n=1):
        self.v += n
        return self.v


class Em:
    ENGINES = ("tensor", "vector", "scalar", "gpsimd", "sync")

    def __init__(self, nc):
        self.nc = nc
        self.q = {e: [] for e in self.ENGINES}

    def op(self, eng, fn, waits=(), inc=None, inc_n=1):
        waits = [(s, v) for (s, v) in waits if v > 0]
        for s, v in waits:
            assert v <= s.v, f"wait on future event {v} > {s.v}"
        ev = inc.bump(inc_n) if inc is not None else None

        def g(h, fn=fn, waits=waits, inc=inc, inc_n=inc_n):
            for s, v in waits:
                h.wait_ge(s.h, v)
            ins = fn(h)
            if inc is not None:
                ins.then_inc(inc.h, inc_n)
        self.q[eng].append(g)
        return ev

    def flush(self, nc):
        with nc.Block() as block:
            for eng in self.ENGINES:
                lst = self.q[eng]
                if not lst:
                    continue
                getattr(block, eng)(lambda h, lst=lst: [f(h) for f in lst])
        self.q = {e: [] for e in self.ENGINES}


def build_nc():
    nc = bass.Bass("TRN2", target_bir_lowering=False, debug=False)
    x_in = nc.dram_tensor("x", [T, E], F32, kind="ExternalInput")
    wattnT = nc.dram_tensor("wattnT", [E, 3 * E], F32, kind="ExternalInput")
    wprojT = nc.dram_tensor("wprojT", [E, E], F32, kind="ExternalInput")
    w1T = nc.dram_tensor("w1T", [E, F], F32, kind="ExternalInput")
    w2T = nc.dram_tensor("w2T", [F, E], F32, kind="ExternalInput")
    cw_d = nc.dram_tensor("cw", [128, 96], F32, kind="ExternalInput")
    cb_d = nc.dram_tensor("cb", [128, 32], F32, kind="ExternalInput")
    gam_d = nc.dram_tensor("gam", [128, 8], F32, kind="ExternalInput")
    bet_d = nc.dram_tensor("bet", [128, 8], F32, kind="ExternalInput")
    b1_d = nc.dram_tensor("b1c", [128, 32], F32, kind="ExternalInput")
    b2_d = nc.dram_tensor("b2c", [128, 8], F32, kind="ExternalInput")
    sel_d = nc.dram_tensor("sel", [16, 2], BF16, kind="ExternalInput")
    wq_d = nc.dram_tensor("wqm", [E, E], F32, kind="ExternalInput")
    wk_d = nc.dram_tensor("wkm", [E, E], F32, kind="ExternalInput")
    wv_d = nc.dram_tensor("wvm", [E, E], F32, kind="ExternalInput")
    wo_d = nc.dram_tensor("wom", [E, E], F32, kind="ExternalInput")
    bw1_d = nc.dram_tensor("bw1", [E, F], F32, kind="ExternalInput")
    bw2_d = nc.dram_tensor("bw2", [F, E], F32, kind="ExternalInput")
    out_d = nc.dram_tensor("out", [T, E], F32, kind="ExternalOutput")
    dbg = {}
    if DEBUG_C:
        for nm, shape in [("cxr9", [128, 1024]), ("cln2c0", [128, 2048]),
                          ("cln2t15", [128, 2048]), ("cln2c1", [128, 2048]),
                          ("cad9", [128, 1024]),
                          ("cln1f", [128, 2048]), ("cqp", [128, 2048]),
                          ("ckvsb", [64, 1040]), ("ckvss", [128, 1040]),
                          ("catt", [128, 1024]), ("cad1", [128, 1024]),
                          ("cden", [128, 36]), ("cln2f", [128, 2048])]:
            dbg[nm] = nc.dram_tensor(nm, shape, F32, kind="ExternalOutput")
    if DEBUG_B:
        dbg["bw1q"] = nc.dram_tensor("bw1q", [E, F], F32, kind="ExternalOutput")
        dbg["bm1"] = nc.dram_tensor("bm1", [128, 2], F32, kind="ExternalOutput")
        for nm, shape in [("bxqf", [128, 8 * 2052]), ("bg", [128, 32 * 256]),
                          ("bmb", [128, 256]), ("bqst", [128, 64]),
                          ("bs2b", [128, 256]), ("bgq", [128, 512]),
                          ("bstat", [128, 128])]:
            dbg[nm] = nc.dram_tensor(nm, shape, F32, kind="ExternalOutput")
    if DEBUG:
        for nm, shape in [("dlnt", [128, 2048]), ("dlnf", [128, 2048]),
                          ("dqf", [128, 2048]), ("dkf", [128, 2048]),
                          ("dvt", [128, 2048]), ("desc", [128, 512]),
                          ("detb", [128, 1024]), ("dyf", [128, 2048]),
                          ("dstat", [128, 128])]:
            dbg[nm] = nc.dram_tensor(nm, shape, F32, kind="ExternalOutput")

    w1q_sp = nc.dram_tensor("w1q_sp", [E, F], BF16)
    w2q_sp = nc.dram_tensor("w2q_sp", [F, E], BF16)
    bw1_sp = nc.dram_tensor("bw1_sp", [E, F], BF16)
    bw2_sp = nc.dram_tensor("bw2_sp", [F, E], BF16)
    ag_in = [nc.dram_tensor(f"ag_in{i}", [2, E], BF16) for i in range(2)]
    ag_out = [nc.dram_tensor(f"ag_out{i}", [NC * 2, E], BF16) for i in range(2)]
    ar_in = nc.dram_tensor("ar_in", [64, H * 65], F32)
    ar_out = nc.dram_tensor("ar_out", [64, H * 65], F32)

    es = ExitStack()
    sbuf = lambda n, s, d: es.enter_context(nc.sbuf_tensor(n, s, d))
    with es:
        ID = sbuf("ID", [128, 128], BF16)
        IDF = sbuf("IDF", [128, 128], F32)
        ONE1F = sbuf("ONE1F", [1, 128], F32)
        ONECB = sbuf("ONECB", [128, 1], BF16)
        CWM = sbuf("CWM", [128, 96], F32)
        CBc = sbuf("CBc", [128, 32], F32)
        GAMc = sbuf("GAMc", [128, 8], F32)
        BETc = sbuf("BETc", [128, 8], F32)
        B1c = sbuf("B1c", [128, 32], F32)
        B2c = sbuf("B2c", [128, 8], F32)
        SELc = sbuf("SELc", [16, 2], BF16)
        STAT = sbuf("STAT", [128, 128], F32)
        M1B = sbuf("M1B", [128, 2], F32)
        M2B = sbuf("M2B", [128, 2], F32)
        SQT = sbuf("SQT", [128, E], F32)

        P = [es.enter_context(nc.psum_tensor(f"ps{i}", [128, 512], F32))
             for i in range(8)]

        s_ld = Sem(es, nc, "s_ld")
        s_st = Sem(es, nc, "s_st")
        s_pe = Sem(es, nc, "s_pe")
        s_dv = Sem(es, nc, "s_dv")
        s_ac = Sem(es, nc, "s_ac")
        s_pl = Sem(es, nc, "s_pl")
        s_cc = Sem(es, nc, "s_cc")

        em = Em(nc)

        def dma(eng, dst, src, waits=(), sem=s_ld):
            return em.op(eng, lambda h: h.dma_start(out=dst, in_=src),
                         waits=waits, inc=sem, inc_n=16)

        def V(t, width):
            return t[:, :].rearrange("p (j w) -> p j w", w=width)

        dma("sync", CWM[:, :], cw_d[:, :])
        dma("sync", CBc[:, :], cb_d[:, :])
        dma("sync", GAMc[:, :], gam_d[:, :])
        dma("sync", BETc[:, :], bet_d[:, :])
        dma("sync", B1c[:, :], b1_d[:, :])
        dma("sync", B2c[:, :], b2_d[:, :])
        ev_consts = dma("sync", SELc[:, :], sel_d[:, :])
        em.op("gpsimd", lambda h: h.memset(ID[:, :], 0.0))
        em.op("gpsimd", lambda h: h.affine_select(
            out=ID[:, :], in_=ID[:, :], compare_op=ALU.not_equal, fill=1.0,
            base=0, pattern=[[-1, 128]], channel_multiplier=1))
        em.op("gpsimd", lambda h: h.memset(IDF[:, :], 0.0))
        em.op("gpsimd", lambda h: h.affine_select(
            out=IDF[:, :], in_=IDF[:, :], compare_op=ALU.not_equal, fill=1.0,
            base=0, pattern=[[-1, 128]], channel_multiplier=1))
        em.op("gpsimd", lambda h: h.memset(ONE1F[:, :], 1.0))
        ev_id = em.op("gpsimd", lambda h: h.memset(ONECB[:, :], 1.0), inc=s_pl)

        if STAGES >= 2:
            with nc.sbuf_tensor("WTMP", [128, EC * F], F32) as WTMP, \
                 nc.sbuf_tensor("WQT", [128, EC * F], BF16) as WQT:

                def quantize_weight(dram_w, rows, cols, spill, mB):
                    nchunk = rows // 128
                    ncols = nchunk * cols
                    ev_w = dma("sync",
                               WTMP[:, 0:ncols].rearrange("p (j f) -> p j f", f=cols),
                               dram_w.ap().rearrange("(j p) f -> p j f", p=128))
                    nsl = 8
                    sl = ncols // nsl
                    for g in range(nsl):
                        em.op("vector", lambda h, g=g, sl=sl: h.reduce_sum(
                            STAT[:, 96 + g:97 + g], WTMP[:, g * sl:(g + 1) * sl],
                            axis=AXX, apply_absolute_value=True),
                            waits=[(s_ld, ev_w)] if g == 0 else [],
                            inc=(s_dv if g == nsl - 1 else None))
                    _evp = s_dv.v
                    em.op("vector", lambda h: h.reduce_sum(
                        STAT[:, 104:105], STAT[:, 96:104], axis=AXX),
                        waits=[(s_dv, _evp)], inc=s_dv)
                    ev_part = s_dv.v
                    ev_tr = em.op("tensor", lambda h: h.matmul(
                        P[0][0:1, 0:128], STAT[:, 104:105], IDF[:, :],
                        start=True, stop=True),
                        waits=[(s_dv, ev_part), (s_pl, ev_id)], inc=s_pe)
                    ev_tsum = em.op("vector", lambda h: h.reduce_sum(
                        STAT[0:1, 105:106], P[0][0:1, 0:128], axis=AXX),
                        waits=[(s_pe, ev_tr)], inc=s_dv)
                    em.op("vector", lambda h, rc=float(rows * cols): h.tensor_scalar(
                        STAT[0:1, 106:107], STAT[0:1, 105:106], 1.0 / rc, 1e-5,
                        ALU.mult, ALU.max), waits=[(s_dv, ev_tsum)], inc=s_dv)
                    ev_m = s_dv.v
                    ev_bc = em.op("tensor", lambda h: h.matmul(
                        P[1][:, 0:1], ONE1F[0:1, :], STAT[0:1, 106:107],
                        start=True, stop=True),
                        waits=[(s_dv, ev_m)], inc=s_pe)
                    ev_mb0 = em.op("vector", lambda h: h.tensor_copy(
                        mB[:, 0:1], P[1][:, 0:1]),
                        waits=[(s_pe, ev_bc)], inc=s_dv)
                    em.op("vector", lambda h: h.reciprocal(mB[:, 1:2], mB[:, 0:1]),
                          waits=[(s_dv, ev_mb0)])
                    em.op("vector", lambda h: h.drain(), inc=s_dv)
                    sl2 = ncols // 8
                    ev_scl = s_dv.v
                    for g in range(8):
                        a, b = g * sl2, (g + 1) * sl2
                        ev_p1 = em.op("scalar", lambda h, a=a, b=b: h.activation(
                            WTMP[:, a:b], WTMP[:, a:b], AF.Copy, scale=mB[:, 1:2],
                            bias=MAGIC), waits=[(s_dv, ev_scl)], inc=s_ac)
                        em.op("vector", lambda h, a=a, b=b: h.tensor_scalar(
                            WTMP[:, a:b], WTMP[:, a:b], -MAGIC, 1.0,
                            ALU.add, ALU.min), waits=[(s_ac, ev_p1)])
                        em.op("vector", lambda h, a=a, b=b: h.tensor_scalar(
                            WQT[:, a:b], WTMP[:, a:b], -1.0, None, ALU.max),
                            inc=s_dv)
                    ev_q = s_dv.v
                    return dma("sync", spill.ap().rearrange("(j p) f -> p j f", p=128),
                               WQT[:, 0:ncols].rearrange("p (j f) -> p j f", f=cols),
                               waits=[(s_dv, ev_q)], sem=s_st)

                quantize_weight(w1T, E, F, w1q_sp, M1B)
                em.op("vector", lambda h: h.tensor_scalar(
                    CWM[:, :], CWM[:, :], M1B[:, 0:1], None, ALU.mult),
                    waits=[(s_ld, ev_consts)])
                quantize_weight(w2T, F, E, w2q_sp, M2B)

                if STAGES >= 3:
                    def cast_weight(dram_w, rows, cols, spill):
                        nchunk = rows // 128
                        ncols = nchunk * cols
                        ev_w = dma("sync",
                                   WTMP[:, 0:ncols].rearrange("p (j f) -> p j f", f=cols),
                                   dram_w.ap().rearrange("(j p) f -> p j f", p=128))
                        sl = ncols // 8
                        for g in range(8):
                            a, b = g * sl, (g + 1) * sl
                            em.op("vector", lambda h, a=a, b=b: h.tensor_copy(
                                WQT[:, a:b], WTMP[:, a:b]),
                                waits=([(s_ld, ev_w)] if g == 0 else []),
                                inc=(s_dv if g == 7 else None))
                        return dma("sync",
                                   spill.ap().rearrange("(j p) f -> p j f", p=128),
                                   WQT[:, 0:ncols].rearrange("p (j f) -> p j f", f=cols),
                                   waits=[(s_dv, s_dv.v)], sem=s_st)

                    cast_weight(bw1_d, E, F, bw1_sp)
                    cast_weight(bw2_d, F, E, bw2_sp)

                em.flush(nc)

        XR = sbuf("XR", [128, NT * E], F32)
        ev_x = dma("sync", V(XR, E), x_in.ap().rearrange("(i p) f -> p i f", p=128))

        def emit_stage_stats(eps, extra_rho=False, src=None, ev_ready=None):
            for tt in range(NT):
                em.op("vector", lambda h, tt=tt: h.reduce_sum(
                    STAT[:, tt:tt + 1], (src or XR)[:, E * tt:E * (tt + 1)], axis=AXX),
                    waits=[(s_ld, ev_ready or ev_x)] if tt == 0 else [],
                    inc=(s_dv if tt == NT - 1 else None))
                em.op("scalar", lambda h, tt=tt: h.activation(
                    SQT[:, :], (src or XR)[:, E * tt:E * (tt + 1)], AF.Square,
                    accum_out=STAT[:, 16 + tt:17 + tt]),
                    waits=[(s_ld, ev_ready or ev_x)] if tt == 0 else [])
            em.op("scalar", lambda h: h.drain(), inc=s_ac)
            ev_sums = s_dv.v
            ev_sq = s_ac.v
            ev_mean = em.op("vector", lambda h: h.tensor_scalar(
                STAT[:, 32:48], STAT[:, 0:16], 1.0 / E, None, ALU.mult),
                waits=[(s_dv, ev_sums)], inc=s_dv)
            ev_msq = em.op("scalar", lambda h: h.activation(
                STAT[:, 80:96], STAT[:, 32:48], AF.Square),
                waits=[(s_dv, ev_mean)], inc=s_ac)
            ev_ssq = em.op("vector", lambda h: h.tensor_scalar(
                STAT[:, 48:64], STAT[:, 16:32], 1.0 / E, None, ALU.mult),
                waits=[(s_ac, ev_sq)], inc=s_dv)
            ev_var = em.op("vector", lambda h: h.tensor_tensor(
                STAT[:, 48:64], STAT[:, 48:64], STAT[:, 80:96], ALU.subtract),
                waits=[(s_ac, ev_msq), (s_dv, ev_ssq)], inc=s_dv)
            ev_ve = em.op("vector", lambda h, eps=eps: h.tensor_scalar(
                STAT[:, 80:96], STAT[:, 48:64], eps, None, ALU.add),
                waits=[(s_dv, ev_var)], inc=s_dv)
            ev_sd = em.op("scalar", lambda h: h.activation(
                STAT[:, 96:112], STAT[:, 80:96], AF.Sqrt),
                waits=[(s_dv, ev_ve)], inc=s_ac)
            em.op("vector", lambda h: h.reciprocal(
                STAT[:, 112:128], STAT[:, 96:112]),
                waits=[(s_ac, ev_sd)])
            ev_r = em.op("vector", lambda h: h.drain(), inc=s_dv)
            ev_cp = em.op("vector", lambda h: h.tensor_copy(
                STAT[:, 48:64], STAT[:, 112:128]),
                waits=[(s_dv, ev_r)], inc=s_dv)
            if extra_rho:
                ev1 = em.op("scalar", lambda h: h.activation(
                    STAT[:, 80:96], STAT[:, 48:64], AF.Square),
                    waits=[(s_dv, ev_cp)], inc=s_ac)
                em.op("vector", lambda h: h.reciprocal(
                    STAT[:, 96:112], STAT[:, 80:96]),
                    waits=[(s_ac, ev1)])
                ev2 = em.op("vector", lambda h: h.drain(), inc=s_dv)
                ev3 = em.op("vector", lambda h, eps=eps: h.tensor_scalar(
                    STAT[:, 96:112], STAT[:, 96:112], -eps, None, ALU.add),
                    waits=[(s_dv, ev2)], inc=s_dv)
                ev4 = em.op("vector", lambda h: h.tensor_tensor(
                    STAT[:, 96:112], STAT[:, 96:112], STAT[:, 80:96], ALU.mult),
                    waits=[(s_dv, ev3)], inc=s_dv)
                ev5 = em.op("vector", lambda h: h.tensor_scalar(
                    STAT[:, 96:112], STAT[:, 96:112], 1e-6, None, ALU.add),
                    waits=[(s_dv, ev4)], inc=s_dv)
                ev6 = em.op("scalar", lambda h: h.activation(
                    STAT[:, 96:112], STAT[:, 96:112], AF.Sqrt),
                    waits=[(s_dv, ev5)], inc=s_ac)
                em.op("vector", lambda h: h.reciprocal(
                    STAT[:, 112:128], STAT[:, 96:112]),
                    waits=[(s_ac, ev6)])
                ev7 = em.op("vector", lambda h: h.drain(), inc=s_dv)
                ev_cp = em.op("vector", lambda h: h.tensor_tensor(
                    STAT[:, 64:80], STAT[:, 112:128], STAT[:, 48:64], ALU.mult),
                    waits=[(s_dv, ev7)], inc=s_dv)
            return ev_cp

        with nc.sbuf_tensor("WAB", [128, EC * 3 * E], BF16) as WAB, \
             nc.sbuf_tensor("WPB", [128, EC * E], BF16) as WPB, \
             nc.sbuf_tensor("LNT", [128, 2 * 2 * E], BF16) as LNT, \
             nc.sbuf_tensor("LNF", [128, 2 * EC * 256], BF16) as LNF, \
             nc.sbuf_tensor("QF", [128, 2 * 2048], BF16) as QF, \
             nc.sbuf_tensor("KF", [128, 2 * 2048], BF16) as KF, \
             nc.sbuf_tensor("VT", [128, 2 * 2048], BF16) as VT, \
             nc.sbuf_tensor("ESC", [128, 4 * 256], F32) as ESC, \
             nc.sbuf_tensor("EB", [128, 4 * 256], BF16) as EB, \
             nc.sbuf_tensor("ETB", [128, 2 * 512], BF16) as ETB, \
             nc.sbuf_tensor("YF", [128, 2 * 2048], BF16) as YF, \
             nc.sbuf_tensor("SST", [128, 12], F32) as SST:

            ev_wa = em.op("gpsimd", lambda h: h.dma_start(
                out=V(WAB, 3 * E), in_=wattnT.ap().rearrange("(c p) e -> p c e", p=128)),
                inc=s_ld, inc_n=16)
            ev_wp = em.op("gpsimd", lambda h: h.dma_start(
                out=V(WPB, E), in_=wprojT.ap().rearrange("(c p) e -> p c e", p=128)),
                inc=s_ld, inc_n=16)
            ev_ldall = s_ld.v
            globals()["_EV_LDALL"] = ev_ldall

            emit_stage_stats(1e-5, ev_ready=ev_ldall)
            ev_stats = s_dv.v

            last = {}

            for w in range(NW):
                par = w % 2
                lnt0 = par * 2 * E
                lnf0 = par * EC * 256
                qf0 = kf0 = vt0 = yf0 = par * 2048
                ev_lnt = []
                for s in range(2):
                    tt = 2 * w + s
                    ev = em.op("vector", lambda h, tt=tt, s=s, lnt0=lnt0: h.tensor_scalar(
                        LNT[:, lnt0 + s * E:lnt0 + (s + 1) * E],
                        XR[:, E * tt:E * (tt + 1)],
                        STAT[:, 32 + tt:33 + tt], STAT[:, 48 + tt:49 + tt],
                        ALU.subtract, ALU.mult),
                        waits=[(s_dv, ev_stats)] + last.get(("LNT", par), []),
                        inc=s_dv)
                    ev_lnt.append(ev)
                ev_lnf = []
                for s in range(2):
                    for grp in range(2):
                        bk = grp % 2
                        evt = None
                        for jj in range(4):
                            j = grp * 4 + jj
                            evt = em.op("tensor", lambda h, s=s, j=j, jj=jj, bk=bk,
                                        lnt0=lnt0: h.matmul(
                                P[bk][:, jj * 128:(jj + 1) * 128],
                                LNT[:, lnt0 + s * E + j * 128:lnt0 + s * E + (j + 1) * 128],
                                ID[:, :], start=True, stop=True),
                                waits=([(s_dv, ev_lnt[s]), (s_pl, ev_id)]
                                       + last.get(("bank", bk), []))
                                      if jj == 0 else [],
                                inc=(s_pe if jj == 3 else None))
                        ev = em.op("vector", lambda h, s=s, grp=grp, bk=bk,
                                   lnf0=lnf0: h.tensor_copy(
                            V(LNF, 256)[:, lnf0 // 256 + grp * 4:
                                        lnf0 // 256 + grp * 4 + 4, 128 * s:128 * s + 128],
                            P[bk][:, 0:512].rearrange("p (c w) -> p c w", w=128)),
                            waits=[(s_pe, evt)], inc=s_dv)
                        last[("bank", bk)] = [(s_dv, ev)]
                        ev_lnf.append(ev)
                ev_lnf_all = ev_lnf[-1]
                last[("LNT", par)] = [(s_pe, s_pe.v)]
                ev_qk = []
                for j in range(16):
                    bk = 2 + (j % 2)
                    isq = j < 8
                    off = (128 * j) if isq else (E + 128 * (j - 8))
                    evt = None
                    for c in range(EC):
                        evt = em.op("tensor", lambda h, c=c, off=off, bk=bk,
                                    lnf0=lnf0: h.matmul(
                            P[bk][:, 0:256],
                            WAB[:, 3 * E * c + off:3 * E * c + off + 128],
                            LNF[:, lnf0 + 256 * c:lnf0 + 256 * c + 256],
                            start=(c == 0), stop=(c == EC - 1)),
                            waits=([(s_dv, ev_lnf_all), (s_ld, ev_ldall)]
                                   + last.get(("bank", bk), [])) if c == 0 else [],
                            inc=(s_pe if c == EC - 1 else None))
                    if isq:
                        ev = em.op("vector", lambda h, j=j, bk=bk, qf0=qf0: h.tensor_scalar(
                            QF[:, qf0 + 256 * j:qf0 + 256 * j + 256], P[bk][:, 0:256],
                            0.125, None, ALU.mult),
                            waits=[(s_pe, evt)] + last.get(("QF", par), []), inc=s_dv)
                    else:
                        ev = em.op("vector", lambda h, j=j, bk=bk, kf0=kf0: h.tensor_copy(
                            KF[:, kf0 + 256 * (j - 8):kf0 + 256 * (j - 8) + 256],
                            P[bk][:, 0:256]),
                            waits=[(s_pe, evt)] + last.get(("KF", par), []), inc=s_dv)
                    last[("bank", bk)] = [(s_dv, ev)]
                    ev_qk.append(ev)
                last[("QF", par)] = []
                last[("KF", par)] = []
                ev_v = []
                for sv in range(2):
                    for half in range(2):
                        bk = 4 + ((2 * sv + half) % 2)
                        evt = None
                        for c in range(EC):
                            evt = em.op("tensor", lambda h, c=c, sv=sv, half=half,
                                        bk=bk, lnf0=lnf0: h.matmul(
                                P[bk][:, 0:512],
                                LNF[:, lnf0 + 256 * c + 128 * sv:
                                    lnf0 + 256 * c + 128 * sv + 128],
                                WAB[:, 3 * E * c + 2 * E + 512 * half:
                                     3 * E * c + 2 * E + 512 * half + 512],
                                start=(c == 0), stop=(c == EC - 1)),
                                waits=([(s_dv, ev_lnf_all), (s_ld, ev_ldall)]
                                       + last.get(("bank", bk), [])) if c == 0 else [],
                                inc=(s_pe if c == EC - 1 else None))
                        ev = em.op("vector", lambda h, sv=sv, half=half, bk=bk,
                                   vt0=vt0: h.tensor_copy(
                            VT[:, vt0 + 1024 * sv + 512 * half:
                               vt0 + 1024 * sv + 512 * half + 512], P[bk][:, 0:512]),
                            waits=[(s_pe, evt)] + last.get(("VT", par), []), inc=s_dv)
                        last[("bank", bk)] = [(s_dv, ev)]
                        ev_v.append(ev)
                last[("VT", par)] = []
                ev_v_all = ev_v[-1]
                ev_y = []
                for pr in range(H // 2):
                    h0 = 2 * pr
                    chq = pr
                    ev_sc = {}
                    for dh in range(2):
                        hh = h0 + dh
                        poff = 64 * (hh % 2)
                        for isub in range(2):
                            bk = 2 + 2 * dh + isub
                            ev_sc[2 * dh + isub] = em.op(
                                "tensor", lambda h, hh=hh, isub=isub, bk=bk,
                                poff=poff, chq=chq, qf0=qf0, kf0=kf0: h.matmul(
                                    P[bk][:, 0:256],
                                    QF[poff:poff + 64, qf0 + 256 * chq + 128 * isub:
                                       qf0 + 256 * chq + 128 * isub + 128],
                                    KF[poff:poff + 64, kf0 + 256 * chq:kf0 + 256 * chq + 256],
                                    start=True, stop=True),
                                waits=[(s_dv, max(ev_qk[chq], ev_qk[8 + chq]))]
                                      + last.get(("bank", bk), []),
                                inc=s_pe)
                    for sl in range(4):
                        em.op("vector", lambda h, sl=sl: h.reduce_max(
                            SST[:, sl:sl + 1], P[2 + sl][:, 0:256], axis=AXX,
                            negate=True),
                            waits=[(s_pe, ev_sc[sl])] + (last.get(("SSTa",), [])
                                                         if sl == 0 else []))
                    ev_nm = em.op("vector", lambda h: h.drain(), inc=s_dv)
                    for sl in range(4):
                        em.op("scalar", lambda h, sl=sl: h.activation(
                            ESC[:, 256 * sl:256 * sl + 256], P[2 + sl][:, 0:256],
                            AF.Exp, bias=SST[:, sl:sl + 1], scale=1.0,
                            accum_out=SST[:, 4 + sl:5 + sl]),
                            waits=[(s_dv, ev_nm)] + (last.get(("ESC",), [])
                                                     if sl == 0 else []))
                    ev_exp = em.op("scalar", lambda h: h.drain(), inc=s_ac)
                    for sl in range(4):
                        last[("bank", 2 + sl)] = [(s_ac, ev_exp)]
                    for sl in range(4):
                        em.op("vector", lambda h, sl=sl: h.reciprocal(
                            SST[:, 8 + sl:9 + sl], SST[:, 4 + sl:5 + sl]),
                            waits=[(s_ac, ev_exp)] if sl == 0 else [])
                    ev_rc = em.op("vector", lambda h: h.drain(), inc=s_dv)
                    last[("SSTa",)] = [(s_dv, ev_rc)]
                    ev_eb = {}
                    for sl in range(4):
                        ev_eb[sl] = em.op("scalar", lambda h, sl=sl: h.activation(
                            EB[:, 256 * sl:256 * sl + 256],
                            ESC[:, 256 * sl:256 * sl + 256],
                            AF.Copy, scale=SST[:, 8 + sl:9 + sl]),
                            waits=[(s_dv, ev_rc)] + (last.get(("EB",), [])
                                                     if sl == 0 else []),
                            inc=s_ac)
                    last[("ESC",)] = [(s_ac, ev_eb[3])]
                    for dh in range(2):
                        hh = h0 + dh
                        hp = hh % 2
                        poff = 64 * (hh % 2)
                        evt = None
                        for isub in range(2):
                            sl = 2 * dh + isub
                            for jj in range(2):
                                evt = em.op("tensor", lambda h, sl=sl, isub=isub,
                                            jj=jj: h.matmul(
                                    P[6][:, 256 * jj + 128 * isub:
                                         256 * jj + 128 * isub + 128],
                                    EB[:, 256 * sl + 128 * jj:256 * sl + 128 * jj + 128],
                                    ID[:, :], start=True, stop=True),
                                    waits=([(s_ac, ev_eb[sl])]
                                           + (last.get(("bank", 6), [])
                                              if (isub == 0 and jj == 0) else [])),
                                    inc=(s_pe if (isub == 1 and jj == 1) else None))
                        ev_etb = em.op("scalar", lambda h, hp=hp: h.activation(
                            ETB[:, 512 * hp:512 * hp + 512], P[6][:, 0:512], AF.Copy),
                            waits=[(s_pe, evt)] + last.get(("ETB", hp), []), inc=s_ac)
                        last[("bank", 6)] = [(s_ac, ev_etb)]
                        last[("EB",)] = [(s_pe, evt)]
                        evt = None
                        for jsub in range(2):
                            evt = em.op("tensor", lambda h, hh=hh, jsub=jsub, hp=hp,
                                        poff=poff, vt0=vt0: h.matmul(
                                P[7][poff:poff + 64, 0:256],
                                VT[:, vt0 + 1024 * jsub + 64 * hh:
                                   vt0 + 1024 * jsub + 64 * hh + 64],
                                ETB[:, 512 * hp + 256 * jsub:512 * hp + 256 * jsub + 256],
                                start=(jsub == 0), stop=(jsub == 1),
                                tile_position=(0, poff)),
                                waits=([(s_ac, ev_etb), (s_dv, ev_v_all)]
                                       + last.get(("bank", 7), [])) if jsub == 0 else [],
                                inc=(s_pe if jsub == 1 else None))
                        ev = em.op("vector", lambda h, hh=hh, poff=poff, chq=chq,
                                   yf0=yf0: h.tensor_copy(
                            YF[poff:poff + 64, yf0 + 256 * chq:yf0 + 256 * chq + 256],
                            P[7][poff:poff + 64, 0:256]),
                            waits=[(s_pe, evt)] + last.get(("YF", par), []), inc=s_dv)
                        last[("bank", 7)] = [(s_dv, ev)]
                        last[("ETB", hp)] = [(s_pe, evt)]
                        ev_y.append(ev)
                last[("YF", par)] = []
                ev_y_all = ev_y[-1]

                for s in range(2):
                    for half in range(2):
                        bk = (2 * s + half) % 2
                        evt = None
                        for c in range(EC):
                            evt = em.op("tensor", lambda h, c=c, s=s, half=half,
                                        bk=bk, yf0=yf0: h.matmul(
                                P[bk][:, 0:512],
                                YF[:, yf0 + 256 * c + 128 * s:
                                   yf0 + 256 * c + 128 * s + 128],
                                WPB[:, E * c + 512 * half:E * c + 512 * half + 512],
                                start=(c == 0), stop=(c == EC - 1)),
                                waits=([(s_dv, ev_y_all), (s_ld, ev_ldall)]
                                       + last.get(("bank", bk), [])) if c == 0 else [],
                                inc=(s_pe if c == EC - 1 else None))
                        ev = em.op("vector", lambda h, s=s, half=half, bk=bk,
                                   w=w: h.tensor_tensor(
                            XR[:, E * (2 * w + s) + 512 * half:
                               E * (2 * w + s) + 512 * half + 512],
                            XR[:, E * (2 * w + s) + 512 * half:
                               E * (2 * w + s) + 512 * half + 512],
                            P[bk][:, 0:512], ALU.add),
                            waits=[(s_pe, evt)], inc=s_dv)
                        last[("bank", bk)] = [(s_dv, ev)]
                last[("LNT", par)] = [(s_dv, s_dv.v)]
                if DEBUG and w == 0:
                    with nc.sbuf_tensor("DBGF", [128, 2048], F32) as DBGF:
                        def dump(name, srcap, n):
                            ev = em.op("vector", lambda h, srcap=srcap, n=n:
                                       h.tensor_copy(DBGF[:, 0:n], srcap),
                                       waits=[(s_dv, s_dv.v), (s_ac, s_ac.v),
                                              (s_pe, s_pe.v), (s_pl, s_pl.v),
                                              (s_st, s_st.v)],
                                       inc=s_dv)
                            dma("sync", dbg[name][:, 0:n], DBGF[:, 0:n],
                                waits=[(s_dv, ev)], sem=s_st)
                        dump("dlnt", LNT[:, 0:2048], 2048)
                        dump("dlnf", LNF[:, 0:2048], 2048)
                        dump("dqf", QF[:, 0:2048], 2048)
                        dump("dkf", KF[:, 0:2048], 2048)
                        dump("dvt", VT[:, 0:2048], 2048)
                        dump("desc", ESC[:, 0:512], 512)
                        dump("detb", ETB[:, 0:1024], 1024)
                        dump("dyf", YF[:, 0:2048], 2048)
                        dump("dstat", STAT[:, :], 128)
                        em.flush(nc)
            em.flush(nc)

        s_w1s = [Sem(es, nc, f"s_w1s{i}") for i in range(2)]
        s_w2s = [Sem(es, nc, f"s_w2s{i}") for i in range(2)]
        s_hl = Sem(es, nc, "s_hl")
        ev_spills = s_st.v

        def emit_ffn(idx):
            with nc.sbuf_tensor(f"XQF{idx}", [128, EC * 2052], BF16) as XQF, \
                 nc.sbuf_tensor(f"LNS{idx}", [128, 2 * E], F32) as LNS, \
                 nc.sbuf_tensor(f"XQS{idx}", [128, 2 * E], BF16) as XQS, \
                 nc.sbuf_tensor(f"AG16{idx}", [16, E], BF16) as AG16, \
                 nc.sbuf_tensor(f"W1S{idx}", [128, 2 * 4096], BF16) as W1S, \
                 nc.sbuf_tensor(f"W2S{idx}", [128, 2 * 4096], BF16) as W2S, \
                 nc.sbuf_tensor(f"G{idx}", [128, FC * 256], F32) as G, \
                 nc.sbuf_tensor(f"CT{idx}", [128, 2 * 512], F32) as CT, \
                 nc.sbuf_tensor(f"GS{idx}", [128, 2 * 256], F32) as GS, \
                 nc.sbuf_tensor(f"GQ{idx}", [128, FC * 256], BF16) as GQ, \
                 nc.sbuf_tensor(f"MB{idx}", [128, 512], F32) as MB, \
                 nc.sbuf_tensor(f"S2R{idx}", [1, 256], F32) as S2R, \
                 nc.sbuf_tensor(f"S2B{idx}", [128, 256], F32) as S2B, \
                 nc.sbuf_tensor(f"QST{idx}", [128, 64], F32) as QST:

                last = {}
                ev_stats = emit_stage_stats(1e-5, ev_ready=_EV_LDALL)
                em.op("gpsimd", lambda h: h.memset(
                    V(XQF, 2052)[:, :, 0:1], 0.0))
                ev_guard0 = em.op("gpsimd", lambda h: h.memset(
                    V(XQF, 2052)[:, :, 2049:2052], 0.0), inc=s_pl)

                ev_xqs = {}
                for tt in range(NT):
                    par = tt % 2
                    ev_ln = em.op("vector", lambda h, tt=tt, par=par: h.tensor_scalar(
                        LNS[:, par * E:(par + 1) * E], XR[:, E * tt:E * (tt + 1)],
                        STAT[:, 32 + tt:33 + tt], STAT[:, 48 + tt:49 + tt],
                        ALU.subtract, ALU.mult),
                        waits=[(s_dv, ev_stats)], inc=s_dv)
                    ev_mx = em.op("vector", lambda h, tt=tt, par=par: h.reduce_max(
                        QST[:, tt:tt + 1], LNS[:, par * E:(par + 1) * E], axis=AXX,
                        apply_absolute_value=True),
                        waits=[(s_dv, ev_ln)], inc=s_dv)
                    ev_mc = em.op("vector", lambda h, tt=tt: h.tensor_scalar(
                        QST[:, 16 + tt:17 + tt], QST[:, tt:tt + 1], 1e-5, None,
                        ALU.max), waits=[(s_dv, ev_mx)], inc=s_dv)
                    em.op("vector", lambda h, tt=tt: h.reciprocal(
                        QST[:, 32 + tt:33 + tt], QST[:, 16 + tt:17 + tt]),
                        waits=[(s_dv, ev_mc)])
                    ev_ri = em.op("vector", lambda h: h.drain(), inc=s_dv)
                    em.op("vector", lambda h, tt=tt, par=par: h.tensor_scalar(
                        LNS[:, par * E:(par + 1) * E], LNS[:, par * E:(par + 1) * E],
                        QST[:, 32 + tt:33 + tt], 127.0, ALU.mult, ALU.mult),
                        waits=[(s_dv, ev_ri)] + last.get(("XQS", par), []))
                    em.op("vector", lambda h, par=par: h.tensor_scalar(
                        LNS[:, par * E:(par + 1) * E], LNS[:, par * E:(par + 1) * E],
                        MAGIC, -MAGIC, ALU.add, ALU.add))
                    ev_xi = em.op("vector", lambda h, tt=tt, par=par: h.tensor_scalar(
                        XQS[:, par * E:(par + 1) * E], LNS[:, par * E:(par + 1) * E],
                        QST[:, 16 + tt:17 + tt], 1.0 / 127.0, ALU.mult, ALU.mult),
                        inc=s_dv)
                    ev_xqs[tt] = ev_xi
                    if tt == 0:
                        ev_h0 = dma("sync", ag_in[idx][0:1, :], XQS[0:1, 0:E],
                                    waits=[(s_dv, ev_xi)], sem=s_hl)
                    if tt == NT - 1:
                        ev_h1 = dma("sync", ag_in[idx][1:2, :], XQS[127:128, par * E:(par + 1) * E],
                                    waits=[(s_dv, ev_xi)], sem=s_hl)
                    for grp in range(2):
                        bk = grp
                        evt = None
                        for jj in range(4):
                            j = grp * 4 + jj
                            evt = em.op("tensor", lambda h, j=j, jj=jj, bk=bk, par=par: h.matmul(
                                P[bk][:, jj * 128:(jj + 1) * 128],
                                XQS[:, par * E + j * 128:par * E + (j + 1) * 128],
                                ID[:, :], start=True, stop=True),
                                waits=([(s_dv, ev_xi)] + last.get(("bank", bk), []))
                                      if jj == 0 else [],
                                inc=(s_pe if jj == 3 else None))
                        ev = em.op("vector", lambda h, tt=tt, grp=grp, bk=bk: h.tensor_copy(
                            V(XQF, 2052)[:, grp * 4:grp * 4 + 4, 1 + 128 * tt:1 + 128 * tt + 128],
                            P[bk][:, 0:512].rearrange("p (c w) -> p c w", w=128)),
                            waits=[(s_pe, evt)], inc=s_dv)
                        last[("bank", bk)] = [(s_dv, ev)]
                    last[("XQS", par)] = [(s_pe, s_pe.v)] + (
                        [(s_hl, ev_h0)] if tt == 0 else [])
                ev_ag = em.op("gpsimd", lambda h: h.collective_compute(
                    "AllGather", ALU.bypass, replica_groups=[list(range(NC))],
                    ins=[ag_in[idx].ap().opt()], outs=[ag_out[idx].ap().opt()]),
                    waits=[(s_hl, s_hl.v)], inc=s_cc)
                ev_ag16 = em.op("gpsimd", lambda h: h.dma_start(
                    out=AG16[:, :], in_=ag_out[idx][:, :]),
                    waits=[(s_cc, ev_ag)], inc=s_ld, inc_n=16)
                evg = None
                for side in range(2):
                    for j in range(8):
                        evg = em.op("tensor", lambda h, side=side, j=j: h.matmul(
                            P[2][:, 8 * side + j:8 * side + j + 1],
                            AG16[:, 128 * j:128 * j + 128], SELc[:, side:side + 1],
                            start=True, stop=True),
                            waits=[(s_ld, s_ld.v), (s_pl, ev_guard0)]
                                  if (side == 0 and j == 0) else [],
                            inc=(s_pe if (side == 1 and j == 7) else None))
                ev_gl = em.op("vector", lambda h: h.tensor_copy(
                    V(XQF, 2052)[:, :, 0:1],
                    P[2][:, 0:8].rearrange("p (c w) -> p c w", w=1)),
                    waits=[(s_pe, evg)], inc=s_dv)
                ev_gr = em.op("vector", lambda h: h.tensor_copy(
                    V(XQF, 2052)[:, :, 2049:2050],
                    P[2][:, 8:16].rearrange("p (c w) -> p c w", w=1)),
                    inc=s_dv)
                last[("bank", 2)] = [(s_dv, ev_gr)]
                ev_xqf = s_dv.v
                if FFN_PART <= 1:
                    em.flush(nc)
                    return

                spill1 = w1q_sp if idx == 0 or True else None
                def w1_dma(g):
                    slot = g % 2
                    wv = [(s_pe, last.get(("w1use", slot), 0)), (s_st, ev_spills)]
                    return em.op("sync", lambda h, g=g, slot=slot: h.dma_start(
                        out=V(W1S, 512)[:, slot * 8:slot * 8 + 8, :],
                        in_=w1q_sp.ap()[:, 512 * g:512 * (g + 1)].rearrange(
                            "(j p) f -> p j f", p=128)),
                        waits=[w for w in wv if w[1] > 0], inc=s_w1s[slot], inc_n=16)

                def w2_dma(g):
                    slot = g % 2
                    wv = [(s_pe, last.get(("w2use", slot), 0)), (s_st, ev_spills)]
                    return em.op("sync", lambda h, g=g, slot=slot: h.dma_start(
                        out=V(W2S, 1024)[:, slot * 4:slot * 4 + 4, :],
                        in_=w2q_sp.ap()[512 * g:512 * (g + 1), :].rearrange(
                            "(j p) f -> p j f", p=128)),
                        waits=[w for w in wv if w[1] > 0], inc=s_w2s[slot], inc_n=16)

                ev_gs = {}
                ev_mm2 = {}
                ev_gq = {}

                def pass1(k):
                    ev_w1 = {}
                    ev_w1[0] = w1_dma(0)
                    ev_w1[1] = w1_dma(1)
                    em.op("gpsimd", lambda h: h.memset(MB[:, 0:512], 0.0),
                          waits=last.get(("MBacc",), []) + last.get(("MBr", 0), []),
                          inc=s_pl)
                    for c in range(FC):
                        g = c // 4
                        slot = g % 2
                        par = c % 2
                        bk = par
                        evt = None
                        for d in range(EC):
                            evt = em.op("tensor", lambda h, c=c, d=d, k=k, slot=slot,
                                        bk=bk: h.matmul(
                                P[bk][:, 0:258],
                                V(W1S, 4096)[:, slot, 512 * d + 128 * (c % 4):
                                             512 * d + 128 * (c % 4) + 128],
                                XQF[:, 2052 * d + 256 * k:2052 * d + 256 * k + 258],
                                start=(d == 0), stop=(d == EC - 1)),
                                waits=([(s_w1s[slot], ev_w1[g]), (s_dv, ev_xqf)]
                                       + last.get(("bank", bk), [])) if d == 0 else [],
                                inc=(s_pe if d == EC - 1 else None))
                        if c % 4 == 3:
                            last[("w1use", slot)] = evt
                            if g + 2 <= 7:
                                ev_w1[g + 2] = w1_dma(g + 2)
                        ev_t0 = em.op("scalar", lambda h, c=c, par=par, bk=bk: h.activation(
                            CT[:, 512 * par:512 * par + 256], P[bk][:, 0:256],
                            AF.Copy, scale=CWM[:, 3 * c:3 * c + 1]),
                            waits=[(s_pe, evt)] + last.get(("CTp", par), []), inc=s_ac)
                        ev_t2 = em.op("scalar", lambda h, c=c, par=par, bk=bk: h.activation(
                            CT[:, 512 * par + 256:512 * par + 512], P[bk][:, 2:258],
                            AF.Copy, scale=CWM[:, 3 * c + 2:3 * c + 3]),
                            inc=s_ac)
                        ev_st = em.op("vector", lambda h, c=c, par=par, bk=bk: h.scalar_tensor_tensor(
                            CT[:, 512 * par:512 * par + 256], P[bk][:, 1:257],
                            CWM[:, 3 * c + 1:3 * c + 2],
                            CT[:, 512 * par:512 * par + 256], ALU.mult, ALU.add),
                            waits=[(s_ac, ev_t2)], inc=s_dv)
                        last[("bank", bk)] = [(s_dv, ev_st), (s_ac, ev_t2)]
                        ev_pl = em.op("gpsimd", lambda h, par=par: h.tensor_tensor(
                            CT[:, 512 * par:512 * par + 256],
                            CT[:, 512 * par:512 * par + 256],
                            CT[:, 512 * par + 256:512 * par + 512], ALU.add),
                            waits=[(s_dv, ev_st), (s_ac, ev_t2)], inc=s_pl)
                        ev_si = em.op("scalar", lambda h, c=c, par=par: h.activation(
                            G[:, 256 * c:256 * c + 256], CT[:, 512 * par:512 * par + 256],
                            AF.Silu, bias=CBc[:, c:c + 1], scale=1.0),
                            waits=[(s_pl, ev_pl)] + ([(s_dv, ev_gs[c])] if c in ev_gs else []),
                            inc=s_ac)
                        em.op("vector", lambda h, c=c: h.tensor_tensor(
                            MB[:, 0:256], MB[:, 0:256], G[:, 256 * c:256 * c + 256],
                            ALU.max),
                            waits=[(s_ac, ev_si)] + ([(s_pl, s_pl.v)] if c == 0 else []))
                        ev_mm = em.op("vector", lambda h, c=c: h.tensor_tensor(
                            MB[:, 256:512], MB[:, 256:512], G[:, 256 * c:256 * c + 256],
                            ALU.min), inc=(s_dv if c == FC - 1 else None))
                        last[("CTp", par)] = [(s_ac, ev_si), (s_pl, ev_pl)]
                    last[("Macc",)] = [(s_dv, s_dv.v)]

                def pass15(k):
                    ev_mb = em.op("vector", lambda h: h.scalar_tensor_tensor(
                        MB[:, 0:256], MB[:, 256:512], -1.0, MB[:, 0:256],
                        ALU.mult, ALU.max),
                        waits=last.get(("Macc",), []) + last.get(("MBr", 0), []),
                        inc=s_dv)
                    evt = None
                    for sM in range(2):
                        evt = em.op("tensor", lambda h, sM=sM: h.matmul(
                            P[2][:, 128 * sM:128 * sM + 128], MB[:, 128 * sM:128 * sM + 128],
                            IDF[:, :], start=True, stop=True),
                            waits=([(s_dv, ev_mb)] + last.get(("bank", 2), []))
                                  if sM == 0 else [],
                            inc=(s_pe if sM == 1 else None))
                    last[("MBr", 0)] = [(s_pe, evt)]
                    last[("MBacc",)] = [(s_dv, ev_mb)]
                    ev_mt = None
                    for sM in range(2):
                        ev_mt = em.op("vector", lambda h, sM=sM: h.reduce_max(
                            QST[:, 48 + sM:49 + sM], P[2][:, 128 * sM:128 * sM + 128],
                            axis=AXX), waits=[(s_pe, evt)], inc=s_dv)
                    ev_mc2 = em.op("vector", lambda h: h.tensor_scalar(
                        QST[:, 50:52], QST[:, 48:50], 1e-5, None, ALU.max),
                        waits=[(s_dv, ev_mt)], inc=s_dv)
                    ev_u2 = em.op("vector", lambda h: h.tensor_scalar(
                        QST[:, 52:54], QST[:, 50:52], M2B[:, 0:1], 1.0 / 127.0,
                        ALU.mult, ALU.mult), waits=[(s_dv, ev_mc2)], inc=s_dv)
                    em.op("vector", lambda h: h.reciprocal(
                        QST[:, 54:56], QST[:, 50:52]),
                        waits=[(s_dv, ev_mc2)])
                    ev_rc2 = em.op("vector", lambda h: h.drain(), inc=s_dv)
                    ev_s2c = em.op("vector", lambda h: h.tensor_scalar(
                        QST[:, 56:58], QST[:, 54:56], 127.0, None, ALU.mult),
                        waits=[(s_dv, ev_rc2)], inc=s_dv)
                    evt = None
                    for sM in range(2):
                        evt = em.op("tensor", lambda h, sM=sM: h.matmul(
                            P[3][0:1, 128 * sM:128 * sM + 128],
                            QST[:, 56 + sM:57 + sM], IDF[:, :],
                            start=True, stop=True),
                            waits=([(s_dv, ev_s2c)] + last.get(("bank", 3), []))
                                  if sM == 0 else [],
                            inc=(s_pe if sM == 1 else None))
                    ev_s2r = em.op("vector", lambda h: h.tensor_copy(
                        S2R[0:1, :], P[3][0:1, 0:256]),
                        waits=[(s_pe, evt)], inc=s_dv)
                    last[("bank", 3)] = [(s_dv, ev_s2r)]
                    ev_bc = em.op("tensor", lambda h: h.matmul(
                        P[2][:, 128:384], ONE1F[0:1, :], S2R[0:1, :],
                        start=True, stop=True),
                        waits=[(s_dv, ev_s2r), (s_dv, ev_mt)], inc=s_pe)
                    ev_s2b = em.op("vector", lambda h: h.tensor_copy(
                        S2B[:, :], P[2][:, 128:384]),
                        waits=[(s_pe, ev_bc)], inc=s_dv)
                    last[("bank", 2)] = [(s_dv, ev_s2b)]
                    return ev_s2b, ev_u2

                def quant(k, ev_s2b):
                    prev_mm2 = dict(ev_mm2)
                    ev_gs.clear()
                    ev_gq.clear()
                    for c in range(FC):
                        par = c % 2
                        ev_g = em.op("vector", lambda h, c=c, par=par: h.tensor_tensor(
                            GS[:, 256 * par:256 * par + 256], G[:, 256 * c:256 * c + 256],
                            S2B[:, :], ALU.mult),
                            waits=[(s_dv, ev_s2b)]
                                  + ([(s_pe, prev_mm2[c])] if c in prev_mm2 else []),
                            inc=s_dv)
                        ev_gs[c] = ev_g
                        ev_gq[c] = em.op("vector", lambda h, c=c, par=par: h.tensor_scalar(
                            GQ[:, 256 * c:256 * c + 256], GS[:, 256 * par:256 * par + 256],
                            MAGIC, -MAGIC, ALU.add, ALU.add), inc=s_dv)

                def pass2(k, ev_u2):
                    ev_w2 = {}
                    ev_w2[0] = w2_dma(0)
                    ev_w2[1] = w2_dma(1)
                    ev_mm2.clear()
                    for c in range(FC):
                        g = c // 4
                        slot = g % 2
                        evt = None
                        for sM in range(2):
                            for eh in range(2):
                                bi = 4 + 2 * sM + eh
                                evt = em.op("tensor", lambda h, c=c, sM=sM, eh=eh,
                                            slot=slot, bi=bi: h.matmul(
                                    P[bi][:, 0:512],
                                    GQ[:, 256 * c + 128 * sM:256 * c + 128 * sM + 128],
                                    V(W2S, 4096)[:, slot, 1024 * (c % 4) + 512 * eh:
                                                 1024 * (c % 4) + 512 * eh + 512],
                                    start=(c == 0), stop=(c == FC - 1)),
                                    waits=([(s_dv, ev_gq[c]), (s_w2s[slot], ev_w2[g])]
                                           + (last.get(("bank", bi), []) if c == 0 else [])),
                                    inc=(s_pe if (sM == 1 and eh == 1) else None))
                        ev_mm2[c] = evt
                        if c % 4 == 3:
                            last[("w2use", slot)] = evt
                            if g + 2 <= 7:
                                ev_w2[g + 2] = w2_dma(g + 2)
                    for sM in range(2):
                        for eh in range(2):
                            bi = 4 + 2 * sM + eh
                            ev = em.op("vector", lambda h, k=k, sM=sM, eh=eh, bi=bi: h.scalar_tensor_tensor(
                                XR[:, E * (2 * k + sM) + 512 * eh:E * (2 * k + sM) + 512 * eh + 512],
                                P[bi][:, 0:512], QST[:, 52 + sM:53 + sM],
                                XR[:, E * (2 * k + sM) + 512 * eh:E * (2 * k + sM) + 512 * eh + 512],
                                ALU.mult, ALU.add),
                                waits=[(s_pe, ev_mm2[FC - 1]), (s_dv, ev_u2)], inc=s_dv)
                            last[("bank", bi)] = [(s_dv, ev)]

                pass1(0)
                for k in range(8):
                    ev_s2b, ev_u2 = pass15(k)
                    quant(k, ev_s2b)
                    if k + 1 < 8:
                        pass1(k + 1)
                    pass2(k, ev_u2)

                em.flush(nc)

        if STAGES >= 2:
            emit_ffn(0)

        def emit_attnb():
            with nc.sbuf_tensor("LN1F", [128, EC * 2048], BF16) as LN1F:
                emit_stage_stats(1e-5, extra_rho=True)
                ev_rho = s_dv.v
                with nc.sbuf_tensor("WKV", [128, 2 * EC * E], BF16) as WKV, \
                     nc.sbuf_tensor("QP", [128, EC * 2048], BF16) as QP, \
                     nc.sbuf_tensor("LNT2", [128, E], BF16) as LNT2, \
                     nc.sbuf_tensor("KPT", [128, E], BF16) as KPT, \
                     nc.sbuf_tensor("VTT", [128, E], BF16) as VTT, \
                     nc.sbuf_tensor("AD1T", [128, E], F32) as AD1T, \
                     nc.sbuf_tensor("ATT", [128, E], BF16) as ATT, \
                     nc.sbuf_tensor("ATF", [128, E], BF16) as ATF, \
                     nc.sbuf_tensor("DENR", [128, 36], F32) as DENR, \
                     nc.sbuf_tensor("KVSB", [64, H * 65], F32) as KVSB, \
                     nc.sbuf_tensor("KVSS", [128, H * 65], BF16) as KVSS:
                    last = {}
                    ev_wk = em.op("gpsimd", lambda h: h.dma_start(
                        out=V(WKV, E)[:, 0:EC, :],
                        in_=wk_d.ap().rearrange("(c p) e -> p c e", p=128)),
                        inc=s_ld, inc_n=16)
                    ev_wv = em.op("gpsimd", lambda h: h.dma_start(
                        out=V(WKV, E)[:, EC:2 * EC, :],
                        in_=wv_d.ap().rearrange("(c p) e -> p c e", p=128)),
                        inc=s_ld, inc_n=16)
                    ev_wkv = s_ld.v
                    ev_l1 = {}
                    for tt in range(NT):
                        ev_ln = em.op("vector", lambda h, tt=tt: h.tensor_scalar(
                            LNT2[:, :], XR[:, E * tt:E * (tt + 1)],
                            STAT[:, 32 + tt:33 + tt], STAT[:, 64 + tt:65 + tt],
                            ALU.subtract, ALU.mult),
                            waits=[(s_dv, ev_rho)] + last.get(("LNT2",), []), inc=s_dv)
                        for grp in range(2):
                            bk = grp
                            evt = None
                            for jj in range(4):
                                j = grp * 4 + jj
                                evt = em.op("tensor", lambda h, j=j, jj=jj, bk=bk: h.matmul(
                                    P[bk][:, jj * 128:(jj + 1) * 128],
                                    LNT2[:, j * 128:(j + 1) * 128],
                                    ID[:, :], start=True, stop=True),
                                    waits=([(s_dv, ev_ln)] + last.get(("bank", bk), []))
                                          if jj == 0 else [],
                                    inc=(s_pe if jj == 3 else None))
                            ev = None
                            for jj in range(4):
                                j = grp * 4 + jj
                                ev = em.op("vector", lambda h, tt=tt, j=j, jj=jj, bk=bk: h.tensor_scalar(
                                    LN1F[:, 2048 * j + 128 * tt:2048 * j + 128 * tt + 128],
                                    P[bk][:, 128 * jj:128 * jj + 128],
                                    GAMc[:, j:j + 1], BETc[:, j:j + 1],
                                    ALU.mult, ALU.add),
                                    waits=[(s_pe, evt)] if jj == 0 else [],
                                    inc=(s_dv if jj == 3 else None))
                            last[("bank", bk)] = [(s_dv, ev)]
                        last[("LNT2",)] = [(s_pe, s_pe.v)]
                        ev_l1[tt] = s_dv.v
                    if C_PART <= 1:
                        em.flush(nc)
                        return
                    for tt in range(NT):
                        evk = []
                        for (bk, wof) in ((2, 0), (3, EC)):
                            for half in range(2):
                                evt = None
                                for c in range(EC):
                                    evt = em.op("tensor", lambda h, c=c, tt=tt, half=half,
                                                bk=bk, wof=wof: h.matmul(
                                        P[bk][:, 512 * half:512 * half + 512] if False else
                                        P[bk][:, 0:512],
                                        LN1F[:, 2048 * c + 128 * tt:2048 * c + 128 * tt + 128],
                                        WKV[:, E * (wof + c) + 512 * half:
                                            E * (wof + c) + 512 * half + 512],
                                        start=(c == 0), stop=(c == EC - 1)),
                                        waits=([(s_dv, ev_l1[tt]), (s_ld, ev_wkv)]
                                               + last.get(("bank", bk), [])) if c == 0 else [],
                                        inc=(s_pe if c == EC - 1 else None))
                                if bk == 2:
                                    ev = em.op("vector", lambda h, half=half: h.tensor_scalar(
                                        KPT[:, 512 * half:512 * half + 512], P[2][:, 0:512],
                                        0.0, 1e-3, ALU.max, ALU.add),
                                        waits=[(s_pe, evt)] + last.get(("KPT",), []),
                                        inc=s_dv)
                                else:
                                    ev = em.op("vector", lambda h, half=half: h.tensor_copy(
                                        VTT[:, 512 * half:512 * half + 512], P[3][:, 0:512]),
                                        waits=[(s_pe, evt)] + last.get(("VTT",), []),
                                        inc=s_dv)
                                last[("bank", bk)] = [(s_dv, ev)]
                                evk.append(ev)
                        ev_kv = s_dv.v
                        evt = None
                        for hh in range(H):
                            bk = 6 + hh // 8
                            em.op("tensor", lambda h, hh=hh, bk=bk, tt=tt: h.matmul(
                                P[bk][0:64, 64 * (hh % 8):64 * (hh % 8) + 64],
                                KPT[:, 64 * hh:64 * hh + 64],
                                VTT[:, 64 * hh:64 * hh + 64],
                                start=(tt == 0 and hh % 8 == 0),
                                stop=(tt == NT - 1 and hh % 8 == 7),
                                skip_group_check=True),
                                waits=[(s_dv, ev_kv)] if hh == 0 else [])
                            evt = em.op("tensor", lambda h, hh=hh, tt=tt: h.matmul(
                                P[4][0:64, hh:hh + 1],
                                KPT[:, 64 * hh:64 * hh + 64], ONECB[:, 0:1],
                                start=(tt == 0 and hh == 0),
                                stop=(tt == NT - 1 and hh == H - 1),
                                skip_group_check=True),
                                inc=(s_pe if hh == H - 1 else None))
                        last[("KPT",)] = [(s_pe, evt)]
                        last[("VTT",)] = [(s_pe, evt)]
                    ev_kvs_done = s_pe.v
                    if C_PART <= 2:
                        em.flush(nc)
                        return
                    ev_wq = em.op("gpsimd", lambda h: h.dma_start(
                        out=V(WKV, E)[:, 0:EC, :],
                        in_=wq_d.ap().rearrange("(c p) e -> p c e", p=128)),
                        waits=[(s_pe, ev_kvs_done)], inc=s_ld, inc_n=16)
                    ev_wo = em.op("gpsimd", lambda h: h.dma_start(
                        out=V(WKV, E)[:, EC:2 * EC, :],
                        in_=wo_d.ap().rearrange("(c p) e -> p c e", p=128)),
                        inc=s_ld, inc_n=16)
                    for tt in range(NT):
                        for grp in range(2):
                            bk = 2 + grp
                            evt = None
                            for jj in range(4):
                                j = grp * 4 + jj
                                ev0 = None
                                for c in range(EC):
                                    evt = em.op("tensor", lambda h, c=c, j=j, jj=jj, tt=tt,
                                                bk=bk: h.matmul(
                                        P[bk][:, 128 * jj:128 * jj + 128],
                                        WKV[:, E * c + 128 * j:E * c + 128 * j + 128],
                                        LN1F[:, 2048 * c + 128 * tt:2048 * c + 128 * tt + 128],
                                        start=(c == 0), stop=(c == EC - 1)),
                                        waits=([(s_ld, ev_wq)] + last.get(("bank", bk), []))
                                              if (jj == 0 and c == 0) else [],
                                        inc=(s_pe if (jj == 3 and c == EC - 1) else None))
                            ev = em.op("vector", lambda h, tt=tt, grp=grp, bk=bk: h.tensor_scalar(
                                V(QP, 2048)[:, grp * 4:grp * 4 + 4, 128 * tt:128 * tt + 128],
                                P[bk][:, 0:512].rearrange("p (c w) -> p c w", w=128),
                                0.0, 1e-3, ALU.max, ALU.add),
                                waits=[(s_pe, evt)], inc=s_dv)
                            last[("bank", bk)] = [(s_dv, ev)]
                    ev_qp = s_dv.v
                    evs = []
                    for hh in range(H):
                        bk = 6 + hh // 8
                        ev = em.op("vector", lambda h, hh=hh, bk=bk: h.tensor_copy(
                            KVSB[:, 65 * hh:65 * hh + 64],
                            P[bk][0:64, 64 * (hh % 8):64 * (hh % 8) + 64]),
                            waits=[(s_pe, ev_kvs_done)] if hh == 0 else [], inc=None)
                    ev_ks = em.op("vector", lambda h: h.tensor_copy(
                        KVSB[:, :].rearrange("p (x u) -> p x u", u=65)[:, :, 64:65],
                        P[4][0:64, 0:16].rearrange("p (x u) -> p x u", u=1)),
                        inc=s_dv)
                    ev_kvsb = em.op("sync", lambda h: h.dma_start(
                        out=ar_in.ap(), in_=KVSB[:, :]),
                        waits=[(s_dv, ev_ks)], inc=s_st, inc_n=16)
                    ev_ar = em.op("gpsimd", lambda h: h.collective_compute(
                        "AllReduce", ALU.add,
                        replica_groups=[[0, 1, 2, 3], [4, 5, 6, 7]],
                        ins=[ar_in.ap().opt()], outs=[ar_out.ap().opt()]),
                        waits=[(s_st, ev_kvsb)], inc=s_cc)
                    em.op("gpsimd", lambda h: h.dma_start(
                        out=KVSS[0:64, :], in_=ar_out.ap()),
                        waits=[(s_cc, ev_ar)], inc=s_ld, inc_n=16)
                    em.op("gpsimd", lambda h: h.dma_start(
                        out=KVSS[64:128, :], in_=ar_out.ap()),
                        inc=s_ld, inc_n=16)
                    ev_kvss_v = s_ld.v
                    last[("bank", 4)] = [(s_dv, ev_ks)]
                    last[("bank", 6)] = [(s_dv, ev_ks)]
                    last[("bank", 7)] = [(s_dv, ev_ks)]
                    if C_PART <= 3:
                        em.flush(nc)
                        return
                    if DEBUG_C:
                        dma("gpsimd", dbg["cln1f"][:, :], LN1F[:, 0:2048],
                            waits=[(s_dv, s_dv.v), (s_pe, s_pe.v)], sem=s_st)
                        dma("gpsimd", dbg["cqp"][:, :], QP[:, 0:2048],
                            waits=[(s_dv, s_dv.v)], sem=s_st)
                        dma("sync", dbg["ckvsb"][:, :], KVSB[:, :],
                            waits=[(s_dv, s_dv.v)], sem=s_st)
                        dma("gpsimd", dbg["ckvss"][:, :], KVSS[:, :],
                            waits=[(s_ld, s_ld.v)], sem=s_st)
                    for tt in range(NT):
                        ev_nd = [None] * 4
                        for hh in range(H):
                            bk = hh % 4
                            evt = em.op("tensor", lambda h, hh=hh, tt=tt, bk=bk: h.matmul(
                                P[bk][:, 68 * (hh // 4):68 * (hh // 4) + 65],
                                QP[64 * (hh % 2):64 * (hh % 2) + 64,
                                   2048 * (hh // 2) + 128 * tt:2048 * (hh // 2) + 128 * tt + 128],
                                KVSS[64 * (hh % 2):64 * (hh % 2) + 64, 65 * hh:65 * hh + 65],
                                start=True, stop=True),
                                waits=([(s_ld, ev_kvss_v), (s_dv, ev_qp)]
                                       + last.get(("bank", bk), [])) if hh < 4 else [],
                                inc=(s_pe if hh >= 12 else None))
                            if hh >= 12:
                                ev_nd[hh % 4] = evt
                        for b4 in range(4):
                            em.op("vector", lambda h, b4=b4: h.tensor_copy(
                                DENR[:, 4 * b4:4 * b4 + 4].rearrange("p (x u) -> p x u", u=1),
                                P[b4][:, 0:272].rearrange("p (x u) -> p x u", u=68)[:, :, 64:65]),
                                waits=[(s_pe, ev_nd[b4])] + (last.get(("DENR",), []) if b4 == 0 else []))
                        em.op("vector", lambda h: h.drain())
                        em.op("vector", lambda h: h.reciprocal(DENR[:, 16:32], DENR[:, 0:16]))
                        ev_den = em.op("vector", lambda h: h.drain(), inc=s_dv)
                        ev_att = None
                        for hh in range(H):
                            bk = hh % 4
                            di = 16 + 4 * (hh % 4) + hh // 4
                            ev_att = em.op("vector", lambda h, hh=hh, bk=bk, di=di: h.tensor_scalar(
                                ATT[:, 64 * hh:64 * hh + 64],
                                P[bk][:, 68 * (hh // 4):68 * (hh // 4) + 64],
                                DENR[:, di:di + 1], None, ALU.mult),
                                waits=([(s_dv, ev_den)] + last.get(("ATT",), []))
                                      if hh == 0 else [],
                                inc=(s_dv if hh == H - 1 else None))
                        for b4 in range(4):
                            last[("bank", b4)] = [(s_dv, ev_att)]
                        last[("DENR",)] = [(s_dv, ev_att)]
                        if C2_SUB <= 1:
                            continue
                        for grp in range(2):
                            evt = None
                            for jj in range(4):
                                j = grp * 4 + jj
                                evt = em.op("tensor", lambda h, j=j, jj=jj: h.matmul(
                                    P[5][:, 128 * jj:128 * jj + 128],
                                    ATT[:, 128 * j:128 * j + 128], ID[:, :],
                                    start=True, stop=True),
                                    waits=([(s_dv, ev_att)] + last.get(("bank", 5), []))
                                          if jj == 0 else [],
                                    inc=(s_pe if jj == 3 else None))
                            ev = em.op("scalar", lambda h, grp=grp: h.activation(
                                V(ATF, 128)[:, grp * 4:grp * 4 + 4, :],
                                P[5][:, 0:512].rearrange("p (c w) -> p c w", w=128),
                                AF.Copy),
                                waits=[(s_pe, evt)] + (last.get(("ATF",), []) if grp == 0 else []),
                                inc=s_ac)
                            last[("bank", 5)] = [(s_ac, ev)]
                        ev_atf = s_ac.v
                        last[("ATT",)] = [(s_pe, s_pe.v)]
                        ev_wo_mm = []
                        for half in range(2):
                            bk = 6 + half
                            evt = None
                            for j in range(EC):
                                evt = em.op("tensor", lambda h, j=j, half=half, bk=bk: h.matmul(
                                    P[bk][:, 0:512],
                                    ATF[:, 128 * j:128 * j + 128],
                                    WKV[:, E * (EC + j) + 512 * half:
                                        E * (EC + j) + 512 * half + 512],
                                    start=(j == 0), stop=(j == EC - 1)),
                                    waits=([(s_ac, ev_atf), (s_ld, ev_wo)]
                                           + last.get(("bank", bk), [])) if j == 0 else [],
                                    inc=(s_pe if j == EC - 1 else None))
                            ev_wo_mm.append(evt)
                        ev_lo = em.op("vector", lambda h, tt=tt: h.tensor_scalar(
                            AD1T[:, :], XR[:, E * tt:E * (tt + 1)],
                            STAT[:, 32 + tt:33 + tt], STAT[:, 48 + tt:49 + tt],
                            ALU.subtract, ALU.mult),
                            waits=[(s_dv, s_dv.v)] + last.get(("AD1T",), []), inc=s_dv)
                        for half in range(2):
                            bk = 6 + half
                            ev = em.op("vector", lambda h, half=half, bk=bk: h.tensor_tensor(
                                AD1T[:, 512 * half:512 * half + 512],
                                AD1T[:, 512 * half:512 * half + 512],
                                P[bk][:, 0:512], ALU.add),
                                waits=[(s_pe, ev_wo_mm[half]), (s_dv, ev_lo)], inc=s_dv)
                            last[("bank", bk)] = [(s_dv, ev)]
                        ev_add1 = s_dv.v
                        if C2_SUB <= 2:
                            last[("AD1T",)] = [(s_dv, ev_add1)]
                            continue
                        qb = 8 * (tt % 2)
                        ev_s = em.op("vector", lambda h, qb=qb: h.reduce_sum(
                            QST2[:, qb + 0:qb + 1], AD1T[:, :], axis=AXX),
                            waits=[(s_dv, ev_add1)] + last.get(("QST2", tt % 2), []),
                            inc=s_dv)
                        ev_q2 = em.op("scalar", lambda h, qb=qb: h.activation(
                            SQT[:, :], AD1T[:, :], AF.Square, accum_out=QST2[:, qb + 1:qb + 2]),
                            waits=[(s_dv, ev_add1)])
                        ev_q2 = em.op("scalar", lambda h: h.drain(), inc=s_ac)
                        em.op("vector", lambda h, qb=qb: h.tensor_scalar(
                            QST2[:, qb + 2:qb + 3], QST2[:, qb + 0:qb + 1], 1.0 / E, None, ALU.mult),
                            waits=[(s_dv, ev_s)], inc=s_dv)
                        ev_m2x = s_dv.v
                        ev_ms = em.op("scalar", lambda h, qb=qb: h.activation(
                            QST2[:, qb + 3:qb + 4], QST2[:, qb + 2:qb + 3], AF.Square),
                            waits=[(s_dv, ev_m2x)], inc=s_ac)
                        em.op("vector", lambda h, qb=qb: h.tensor_scalar(
                            QST2[:, qb + 4:qb + 5], QST2[:, qb + 1:qb + 2], 1.0 / E, None, ALU.mult),
                            waits=[(s_ac, ev_q2)])
                        em.op("vector", lambda h, qb=qb: h.tensor_tensor(
                            QST2[:, qb + 4:qb + 5], QST2[:, qb + 4:qb + 5], QST2[:, qb + 3:qb + 4],
                            ALU.subtract),
                            waits=[(s_ac, ev_ms)])
                        ev_v2 = em.op("vector", lambda h, qb=qb: h.tensor_scalar(
                            QST2[:, qb + 5:qb + 6], QST2[:, qb + 4:qb + 5], 1e-6, None, ALU.add),
                            inc=s_dv)
                        ev_sd2 = em.op("scalar", lambda h, qb=qb: h.activation(
                            QST2[:, qb + 6:qb + 7], QST2[:, qb + 5:qb + 6], AF.Sqrt),
                            waits=[(s_dv, ev_v2)], inc=s_ac)
                        em.op("vector", lambda h, qb=qb: h.reciprocal(
                            QST2[:, qb + 7:qb + 8], QST2[:, qb + 6:qb + 7]),
                            waits=[(s_ac, ev_sd2)])
                        ev_r2 = em.op("vector", lambda h: h.drain(), inc=s_dv)
                        last[("QST2", tt % 2)] = [(s_ac, ev_sd2)]
                        ev_ln2 = em.op("vector", lambda h, qb=qb: h.tensor_scalar(
                            LNT2[:, :], AD1T[:, :], QST2[:, qb + 2:qb + 3], QST2[:, qb + 7:qb + 8],
                            ALU.subtract, ALU.mult),
                            waits=[(s_dv, ev_r2)] + last.get(("LNT2",), []), inc=s_dv)
                        for grp in range(2):
                            bk = grp
                            evt = None
                            for jj in range(4):
                                j = grp * 4 + jj
                                evt = em.op("tensor", lambda h, j=j, jj=jj, bk=bk: h.matmul(
                                    P[bk][:, jj * 128:(jj + 1) * 128],
                                    LNT2[:, j * 128:(j + 1) * 128],
                                    ID[:, :], start=True, stop=True),
                                    waits=([(s_dv, ev_ln2)] + last.get(("bank", bk), []))
                                          if jj == 0 else [],
                                    inc=(s_pe if jj == 3 else None))
                            ev = None
                            for jj in range(4):
                                j = grp * 4 + jj
                                ev = em.op("vector", lambda h, tt=tt, j=j, jj=jj, bk=bk: h.tensor_scalar(
                                    LN1F[:, 2048 * j + 128 * tt:2048 * j + 128 * tt + 128],
                                    P[bk][:, 128 * jj:128 * jj + 128],
                                    GAMc[:, j:j + 1], BETc[:, j:j + 1],
                                    ALU.mult, ALU.add),
                                    waits=[(s_pe, evt)] if jj == 0 else [],
                                    inc=(s_dv if jj == 3 else None))
                            last[("bank", bk)] = [(s_dv, ev)]
                        last[("LNT2",)] = [(s_pe, s_pe.v)]
                        if DEBUG_C and tt == 12:
                            dma("sync", dbg["cad9"][:, :], AD1T[:, :],
                                waits=[(s_dv, s_dv.v)], sem=s_st)
                            em.op("vector", lambda h: h.engine_nop(),
                                  waits=[(s_st, s_st.v)])
                        if DEBUG_C and tt == 12:
                            dma("gpsimd", dbg["catt"][:, :], ATT[:, :],
                                waits=[(s_dv, s_dv.v), (s_pe, s_pe.v)], sem=s_st)
                            dma("sync", dbg["cad1"][:, :], AD1T[:, :],
                                waits=[(s_dv, s_dv.v)], sem=s_st)
                            dma("sync", dbg["cden"][:, :], DENR[:, :],
                                waits=[(s_dv, s_dv.v)], sem=s_st)
                            dma("gpsimd", dbg["cln2f"][:, :], LN1F[:, 0:2048],
                                waits=[(s_dv, s_dv.v)], sem=s_st)
                            em.op("vector", lambda h: h.engine_nop(),
                                  waits=[(s_st, s_st.v)])
                        if DEBUG_C and tt == 151:
                            dma("gpsimd", dbg["cln2t15"][:, :], LN1F[:, 0:2048],
                                waits=[(s_dv, s_dv.v), (s_pe, s_pe.v)], sem=s_st)
                            em.op("vector", lambda h: h.engine_nop(),
                                  waits=[(s_st, s_st.v)])
                            em.op("gpsimd", lambda h: h.engine_nop(),
                                  waits=[(s_st, s_st.v)])
                        ev_xu = em.op("vector", lambda h, tt=tt: h.tensor_tensor(
                            XR[:, E * tt:E * (tt + 1)], XR[:, E * tt:E * (tt + 1)],
                            AD1T[:, :], ALU.add), inc=s_dv)
                        last[("AD1T",)] = [(s_dv, ev_xu), (s_ac, ev_q2)]
                        if C2_SER:
                            em.op("vector", lambda h: h.drain(),
                                  waits=[(s_ac, s_ac.v), (s_pe, s_pe.v), (s_pl, s_pl.v)],
                                  inc=s_dv)
                            em.op("scalar", lambda h: h.drain(),
                                  waits=[(s_dv, s_dv.v)], inc=s_ac)
                            em.op("tensor", lambda h: h.nop(),
                                  waits=[(s_dv, s_dv.v), (s_ac, s_ac.v)])
                if C_PART <= 4:
                    em.flush(nc)
                    return
                em.flush(nc)
                with nc.sbuf_tensor("HG", [128, FC * 512], BF16) as HG, \
                     nc.sbuf_tensor("OF", [128, EC * 512], BF16) as OF, \
                     nc.sbuf_tensor("BW1S", [128, 2 * 4096], BF16) as BW1S, \
                     nc.sbuf_tensor("BW2S", [128, 2 * 4096], BF16) as BW2S:
                    last = {}
                    s_b1 = [Sem(es, nc, f"s_b1{i}") for i in range(2)]
                    s_b2 = [Sem(es, nc, f"s_b2{i}") for i in range(2)]

                    def b1_dma(g):
                        slot = g % 2
                        wv = [(s_pe, last.get(("b1use", slot), 0)), (s_st, ev_spills)]
                        return em.op("sync", lambda h, g=g, slot=slot: h.dma_start(
                            out=V(BW1S, 512)[:, slot * 8:slot * 8 + 8, :],
                            in_=bw1_sp.ap()[:, 512 * g:512 * (g + 1)].rearrange(
                                "(j p) f -> p j f", p=128)),
                            waits=[w for w in wv if w[1] > 0], inc=s_b1[slot], inc_n=16)

                    def b2_dma(g):
                        slot = g % 2
                        wv = [(s_pe, last.get(("b2use", slot), 0)), (s_st, ev_spills)]
                        return em.op("sync", lambda h, g=g, slot=slot: h.dma_start(
                            out=V(BW2S, 1024)[:, slot * 4:slot * 4 + 4, :],
                            in_=bw2_sp.ap()[512 * g:512 * (g + 1), :].rearrange(
                                "(j p) f -> p j f", p=128)),
                            waits=[w for w in wv if w[1] > 0], inc=s_b2[slot], inc_n=16)

                    for tq in range(4):
                        ev_b1 = {}
                        ev_b1[0] = b1_dma(0)
                        ev_b1[1] = b1_dma(1)
                        for c in range(FC):
                            g = c // 4
                            slot = g % 2
                            bk = c % 2
                            evt = None
                            for j in range(EC):
                                evt = em.op("tensor", lambda h, c=c, j=j, tq=tq, slot=slot,
                                            bk=bk: h.matmul(
                                    P[bk][:, 0:512],
                                    V(BW1S, 512)[:, slot * 8 + j, 128 * (c % 4):
                                                 128 * (c % 4) + 128],
                                    LN1F[:, 2048 * j + 512 * tq:2048 * j + 512 * tq + 512],
                                    start=(j == 0), stop=(j == EC - 1)),
                                    waits=([(s_b1[slot], ev_b1[g]), (s_dv, s_dv.v)]
                                           + last.get(("bank", bk), [])) if j == 0 else [],
                                    inc=(s_pe if j == EC - 1 else None))
                            if c % 4 == 3:
                                last[("b1use", slot)] = evt
                                if g + 2 <= 7:
                                    ev_b1[g + 2] = b1_dma(g + 2)
                            ev = em.op("scalar", lambda h, c=c, bk=bk: h.activation(
                                HG[:, 512 * c:512 * c + 512], P[bk][:, 0:512],
                                AF.Gelu, bias=B1c[:, c:c + 1], scale=1.0),
                                waits=[(s_pe, evt)] + last.get(("HGw", c % 2), []),
                                inc=s_ac)
                            last[("bank", bk)] = [(s_ac, ev)]
                        ev_hg = s_ac.v
                        ev_b2 = {}
                        ev_b2[0] = b2_dma(0)
                        ev_b2[1] = b2_dma(1)
                        ev_ph2 = None
                        for c in range(FC):
                            g = c // 4
                            slot = g % 2
                            for j in range(EC):
                                ev_ph2 = em.op("tensor", lambda h, c=c, j=j, slot=slot: h.matmul(
                                    P[j][:, 0:512],
                                    V(BW2S, 1024)[:, slot * 4 + (c % 4), 128 * j:128 * j + 128],
                                    HG[:, 512 * c:512 * c + 512],
                                    start=(c == 0), stop=(c == FC - 1)),
                                    waits=([(s_b2[slot], ev_b2[g]), (s_ac, ev_hg)]
                                           + sum((last.get(("bank", bb), []) for bb in range(8)), []))
                                          if (c == 0 and j == 0) else [],
                                    inc=(s_pe if (c % 4 == 3 and j == EC - 1) else None))
                            if c % 4 == 3:
                                last[("b2use", slot)] = ev_ph2
                                if g + 2 <= 7:
                                    ev_b2[g + 2] = b2_dma(g + 2)
                        ev_of = []
                        for j in range(EC):
                            ev = em.op("scalar", lambda h, j=j: h.activation(
                                OF[:, 512 * j:512 * j + 512], P[j][:, 0:512],
                                AF.Gelu, bias=B2c[:, j:j + 1], scale=1.0),
                                waits=[(s_pe, ev_ph2)] + (last.get(("OFw",), []) if j == 0 else []),
                                inc=s_ac)
                            ev_of.append(ev)
                        for j in range(EC):
                            last[("bank", j)] = [(s_ac, ev_of[-1])]
                        for sq in range(4):
                            for grp in range(2):
                                bk = grp
                                evt = None
                                for jj in range(4):
                                    j = grp * 4 + jj
                                    evt = em.op("tensor", lambda h, j=j, jj=jj, sq=sq,
                                                bk=bk: h.matmul(
                                        P[bk][:, 128 * jj:128 * jj + 128],
                                        OF[:, 512 * j + 128 * sq:512 * j + 128 * sq + 128],
                                        ID[:, :], start=True, stop=True),
                                        waits=([(s_ac, ev_of[-1])] + last.get(("bank", bk), []))
                                              if jj == 0 else [],
                                        inc=(s_pe if jj == 3 else None))
                                tile = 4 * tq + sq
                                ev = em.op("vector", lambda h, tile=tile, grp=grp, bk=bk: h.tensor_tensor(
                                    XR[:, E * tile + 512 * grp:E * tile + 512 * grp + 512],
                                    XR[:, E * tile + 512 * grp:E * tile + 512 * grp + 512],
                                    P[bk][:, 0:512], ALU.add),
                                    waits=[(s_pe, evt)], inc=s_dv)
                                last[("bank", bk)] = [(s_dv, ev)]
                        last[("OFw",)] = [(s_pe, s_pe.v)]
                        last[("HGw", 0)] = [(s_pe, s_pe.v)]
                        last[("HGw", 1)] = [(s_pe, s_pe.v)]
                        if C3_SER:
                            em.op("vector", lambda h: h.drain(),
                                  waits=[(s_ac, s_ac.v), (s_pe, s_pe.v), (s_pl, s_pl.v)],
                                  inc=s_dv)
                            em.op("scalar", lambda h: h.drain(),
                                  waits=[(s_dv, s_dv.v)], inc=s_ac)
                            em.op("tensor", lambda h: h.nop(),
                                  waits=[(s_dv, s_dv.v), (s_ac, s_ac.v)])
                em.flush(nc)

        if STAGES >= 3:
            with nc.sbuf_tensor("QST2g", [128, 16], F32) as QST2:
                emit_attnb()
        if STAGES >= 4:
            emit_ffn(1)

        ev_out = dma("sync", out_d.ap().rearrange("(i p) f -> p i f", p=128),
                     V(XR, E), waits=[(s_dv, s_dv.v), (s_ac, s_ac.v), (s_pe, s_pe.v)],
                     sem=s_st)
        em.op("sync", lambda h: h.nop(), waits=[(s_st, ev_out)])
        em.flush(nc)

    return nc


def host_layout(inputs):
    xfull = np.ascontiguousarray(inputs["L0_attn_a_x"], dtype=np.float32)
    lay = dict(
        wattnT=np.ascontiguousarray(inputs["L0_attn_a_w_attn"].T),
        wprojT=np.ascontiguousarray(inputs["L0_attn_a_w_proj"].T),
        w1T=np.ascontiguousarray(inputs["L0_ffn_w1"].T),
        w2T=np.ascontiguousarray(inputs["L0_ffn_w2"].T),
        cw=np.ascontiguousarray(
            inputs["L0_ffn_conv_w"].reshape(F, 3).reshape(FC, 128, 3)
            .transpose(1, 0, 2).reshape(128, 96)),
        cb=np.ascontiguousarray(inputs["L0_ffn_conv_b"].reshape(FC, 128).T),
        gam=np.ascontiguousarray(inputs["L0_attn_b_gamma1"].reshape(EC, 128).T),
        bet=np.ascontiguousarray(inputs["L0_attn_b_beta1"].reshape(EC, 128).T),
        b1c=np.ascontiguousarray(inputs["L0_attn_b_b1"].reshape(FC, 128).T),
        b2c=np.ascontiguousarray(inputs["L0_attn_b_b2"].reshape(EC, 128).T),
        wqm=np.ascontiguousarray(inputs["L0_attn_b_wq"].reshape(E, E)),
        wkm=np.ascontiguousarray(inputs["L0_attn_b_wk"].reshape(E, E)),
        wvm=np.ascontiguousarray(inputs["L0_attn_b_wv"].reshape(E, E)),
        wom=np.ascontiguousarray(inputs["L0_attn_b_wo"].reshape(E, E)),
        bw1=np.ascontiguousarray(inputs["L0_attn_b_w1"]),
        bw2=np.ascontiguousarray(inputs["L0_attn_b_w2"]),
    )
    import ml_dtypes
    in_maps = []
    for c in range(NC):
        b, q = c // 4, c % 4
        sel = np.zeros((16, 2), ml_dtypes.bfloat16)
        if q > 0:
            sel[2 * (c - 1) + 1, 0] = 1
        if q < 3:
            sel[2 * (c + 1), 1] = 1
        m = dict(lay)
        m["x"] = np.ascontiguousarray(xfull[b, q * T:(q + 1) * T, :])
        m["sel"] = sel
        in_maps.append(m)
    return in_maps


def kernel(**inputs):
    in_maps = host_layout(inputs)
    nc = build_nc()
    res = run_bass_kernel_spmd(nc, in_maps, core_ids=list(range(NC)))
    outf = np.empty((2, 8192, E), np.float32)
    for c in range(NC):
        b, q = c // 4, c % 4
        outf[b, q * T:(q + 1) * T, :] = res.results[c]["out"]
    return outf


# version 05
# speedup vs baseline: 1.1817x; 1.0005x over previous
import numpy as np
from contextlib import ExitStack
import concourse.bass as bass
import concourse.mybir as mybir
from concourse.bass_utils import run_bass_kernel_spmd

F32 = mybir.dt.float32
BF16 = mybir.dt.bfloat16
AF = mybir.ActivationFunctionType
ALU = mybir.AluOpType
AXX = mybir.AxisListType.X

MAGIC = float(np.float32(1.5 * 2 ** 23))
STAGES = 4
FFN_PART = 4
FFN_P1 = 9
C_PART = 9
C2_SUB = 9
C2_SER = False
C3_SER = False
DEBUG = False
DEBUG_B = False
DEBUG_C = False

T = 2048
NT = 16
NW = 8
E = 1024
EC = 8
F = 4096
FC = 32
H = 16
NC = 8


class Sem:
    def __init__(self, es, nc, name):
        self.h = es.enter_context(nc.semaphore(name))
        self.v = 0

    def bump(self, n=1):
        self.v += n
        return self.v


class Em:
    ENGINES = ("tensor", "vector", "scalar", "gpsimd", "sync")

    def __init__(self, nc):
        self.nc = nc
        self.q = {e: [] for e in self.ENGINES}

    def op(self, eng, fn, waits=(), inc=None, inc_n=1):
        waits = [(s, v) for (s, v) in waits if v > 0]
        for s, v in waits:
            assert v <= s.v, f"wait on future event {v} > {s.v}"
        ev = inc.bump(inc_n) if inc is not None else None

        def g(h, fn=fn, waits=waits, inc=inc, inc_n=inc_n):
            for s, v in waits:
                h.wait_ge(s.h, v)
            ins = fn(h)
            if inc is not None:
                ins.then_inc(inc.h, inc_n)
        self.q[eng].append(g)
        return ev

    def flush(self, nc):
        with nc.Block() as block:
            for eng in self.ENGINES:
                lst = self.q[eng]
                if not lst:
                    continue
                getattr(block, eng)(lambda h, lst=lst: [f(h) for f in lst])
        self.q = {e: [] for e in self.ENGINES}


def build_nc():
    nc = bass.Bass("TRN2", target_bir_lowering=False, debug=False)
    x_in = nc.dram_tensor("x", [T, E], F32, kind="ExternalInput")
    wattnT = nc.dram_tensor("wattnT", [E, 3 * E], F32, kind="ExternalInput")
    wprojT = nc.dram_tensor("wprojT", [E, E], F32, kind="ExternalInput")
    w1T = nc.dram_tensor("w1T", [E, F], F32, kind="ExternalInput")
    w2T = nc.dram_tensor("w2T", [F, E], F32, kind="ExternalInput")
    cw_d = nc.dram_tensor("cw", [128, 96], F32, kind="ExternalInput")
    cb_d = nc.dram_tensor("cb", [128, 32], F32, kind="ExternalInput")
    gam_d = nc.dram_tensor("gam", [128, 8], F32, kind="ExternalInput")
    bet_d = nc.dram_tensor("bet", [128, 8], F32, kind="ExternalInput")
    b1_d = nc.dram_tensor("b1c", [128, 32], F32, kind="ExternalInput")
    b2_d = nc.dram_tensor("b2c", [128, 8], F32, kind="ExternalInput")
    sel_d = nc.dram_tensor("sel", [16, 2], BF16, kind="ExternalInput")
    wq_d = nc.dram_tensor("wqm", [E, E], F32, kind="ExternalInput")
    wk_d = nc.dram_tensor("wkm", [E, E], F32, kind="ExternalInput")
    wv_d = nc.dram_tensor("wvm", [E, E], F32, kind="ExternalInput")
    wo_d = nc.dram_tensor("wom", [E, E], F32, kind="ExternalInput")
    bw1_d = nc.dram_tensor("bw1", [E, F], F32, kind="ExternalInput")
    bw2_d = nc.dram_tensor("bw2", [F, E], F32, kind="ExternalInput")
    out_d = nc.dram_tensor("out", [T, E], F32, kind="ExternalOutput")
    dbg = {}
    if DEBUG_C:
        for nm, shape in [("cxr9", [128, 1024]), ("cln2c0", [128, 2048]),
                          ("cln2t15", [128, 2048]), ("cln2c1", [128, 2048]),
                          ("cad9", [128, 1024]),
                          ("cln1f", [128, 2048]), ("cqp", [128, 2048]),
                          ("ckvsb", [64, 1040]), ("ckvss", [128, 1040]),
                          ("catt", [128, 1024]), ("cad1", [128, 1024]),
                          ("cden", [128, 36]), ("cln2f", [128, 2048])]:
            dbg[nm] = nc.dram_tensor(nm, shape, F32, kind="ExternalOutput")
    if DEBUG_B:
        dbg["bw1q"] = nc.dram_tensor("bw1q", [E, F], F32, kind="ExternalOutput")
        dbg["bm1"] = nc.dram_tensor("bm1", [128, 2], F32, kind="ExternalOutput")
        for nm, shape in [("bxqf", [128, 8 * 2052]), ("bg", [128, 32 * 256]),
                          ("bmb", [128, 256]), ("bqst", [128, 64]),
                          ("bs2b", [128, 256]), ("bgq", [128, 512]),
                          ("bstat", [128, 128])]:
            dbg[nm] = nc.dram_tensor(nm, shape, F32, kind="ExternalOutput")
    if DEBUG:
        for nm, shape in [("dlnt", [128, 2048]), ("dlnf", [128, 2048]),
                          ("dqf", [128, 2048]), ("dkf", [128, 2048]),
                          ("dvt", [128, 2048]), ("desc", [128, 512]),
                          ("detb", [128, 1024]), ("dyf", [128, 2048]),
                          ("dstat", [128, 128])]:
            dbg[nm] = nc.dram_tensor(nm, shape, F32, kind="ExternalOutput")

    w1q_sp = nc.dram_tensor("w1q_sp", [E, F], BF16)
    w2q_sp = nc.dram_tensor("w2q_sp", [F, E], BF16)
    bw1_sp = nc.dram_tensor("bw1_sp", [E, F], BF16)
    bw2_sp = nc.dram_tensor("bw2_sp", [F, E], BF16)
    ag_in = [nc.dram_tensor(f"ag_in{i}", [2, E], BF16) for i in range(2)]
    ag_out = [nc.dram_tensor(f"ag_out{i}", [NC * 2, E], BF16) for i in range(2)]
    ar_in = nc.dram_tensor("ar_in", [64, H * 65], F32)
    ar_out = nc.dram_tensor("ar_out", [64, H * 65], F32)

    es = ExitStack()
    sbuf = lambda n, s, d: es.enter_context(nc.sbuf_tensor(n, s, d))
    with es:
        ID = sbuf("ID", [128, 128], BF16)
        IDF = sbuf("IDF", [128, 128], F32)
        ONE1F = sbuf("ONE1F", [1, 128], F32)
        ONECB = sbuf("ONECB", [128, 1], BF16)
        CWM = sbuf("CWM", [128, 96], F32)
        CBc = sbuf("CBc", [128, 32], F32)
        GAMc = sbuf("GAMc", [128, 8], F32)
        BETc = sbuf("BETc", [128, 8], F32)
        B1c = sbuf("B1c", [128, 32], F32)
        B2c = sbuf("B2c", [128, 8], F32)
        SELc = sbuf("SELc", [16, 2], BF16)
        STAT = sbuf("STAT", [128, 128], F32)
        M1B = sbuf("M1B", [128, 2], F32)
        M2B = sbuf("M2B", [128, 2], F32)
        SQT = sbuf("SQT", [128, E], F32)

        P = [es.enter_context(nc.psum_tensor(f"ps{i}", [128, 512], F32))
             for i in range(8)]

        s_ld = Sem(es, nc, "s_ld")
        s_st = Sem(es, nc, "s_st")
        s_pe = Sem(es, nc, "s_pe")
        s_dv = Sem(es, nc, "s_dv")
        s_ac = Sem(es, nc, "s_ac")
        s_pl = Sem(es, nc, "s_pl")
        s_cc = Sem(es, nc, "s_cc")

        em = Em(nc)

        def dma(eng, dst, src, waits=(), sem=s_ld):
            return em.op(eng, lambda h: h.dma_start(out=dst, in_=src),
                         waits=waits, inc=sem, inc_n=16)

        def V(t, width):
            return t[:, :].rearrange("p (j w) -> p j w", w=width)

        dma("sync", CWM[:, :], cw_d[:, :])
        dma("sync", CBc[:, :], cb_d[:, :])
        dma("sync", GAMc[:, :], gam_d[:, :])
        dma("sync", BETc[:, :], bet_d[:, :])
        dma("sync", B1c[:, :], b1_d[:, :])
        dma("sync", B2c[:, :], b2_d[:, :])
        ev_consts = dma("sync", SELc[:, :], sel_d[:, :])
        em.op("gpsimd", lambda h: h.memset(ID[:, :], 0.0))
        em.op("gpsimd", lambda h: h.affine_select(
            out=ID[:, :], in_=ID[:, :], compare_op=ALU.not_equal, fill=1.0,
            base=0, pattern=[[-1, 128]], channel_multiplier=1))
        em.op("gpsimd", lambda h: h.memset(IDF[:, :], 0.0))
        em.op("gpsimd", lambda h: h.affine_select(
            out=IDF[:, :], in_=IDF[:, :], compare_op=ALU.not_equal, fill=1.0,
            base=0, pattern=[[-1, 128]], channel_multiplier=1))
        em.op("gpsimd", lambda h: h.memset(ONE1F[:, :], 1.0))
        ev_id = em.op("gpsimd", lambda h: h.memset(ONECB[:, :], 1.0), inc=s_pl)

        if STAGES >= 2:
            with nc.sbuf_tensor("WTMP", [128, EC * F], F32) as WTMP, \
                 nc.sbuf_tensor("WQT", [128, EC * F], BF16) as WQT:

                def quantize_weight(dram_w, rows, cols, spill, mB):
                    nchunk = rows // 128
                    ncols = nchunk * cols
                    ev_w = dma("sync",
                               WTMP[:, 0:ncols].rearrange("p (j f) -> p j f", f=cols),
                               dram_w.ap().rearrange("(j p) f -> p j f", p=128))
                    nsl = 8
                    sl = ncols // nsl
                    for g in range(nsl):
                        em.op("vector", lambda h, g=g, sl=sl: h.reduce_sum(
                            STAT[:, 96 + g:97 + g], WTMP[:, g * sl:(g + 1) * sl],
                            axis=AXX, apply_absolute_value=True),
                            waits=[(s_ld, ev_w)] if g == 0 else [],
                            inc=(s_dv if g == nsl - 1 else None))
                    _evp = s_dv.v
                    em.op("vector", lambda h: h.reduce_sum(
                        STAT[:, 104:105], STAT[:, 96:104], axis=AXX),
                        waits=[(s_dv, _evp)], inc=s_dv)
                    ev_part = s_dv.v
                    ev_tr = em.op("tensor", lambda h: h.matmul(
                        P[0][0:1, 0:128], STAT[:, 104:105], IDF[:, :],
                        start=True, stop=True),
                        waits=[(s_dv, ev_part), (s_pl, ev_id)], inc=s_pe)
                    ev_tsum = em.op("vector", lambda h: h.reduce_sum(
                        STAT[0:1, 105:106], P[0][0:1, 0:128], axis=AXX),
                        waits=[(s_pe, ev_tr)], inc=s_dv)
                    em.op("vector", lambda h, rc=float(rows * cols): h.tensor_scalar(
                        STAT[0:1, 106:107], STAT[0:1, 105:106], 1.0 / rc, 1e-5,
                        ALU.mult, ALU.max), waits=[(s_dv, ev_tsum)], inc=s_dv)
                    ev_m = s_dv.v
                    ev_bc = em.op("tensor", lambda h: h.matmul(
                        P[1][:, 0:1], ONE1F[0:1, :], STAT[0:1, 106:107],
                        start=True, stop=True),
                        waits=[(s_dv, ev_m)], inc=s_pe)
                    ev_mb0 = em.op("vector", lambda h: h.tensor_copy(
                        mB[:, 0:1], P[1][:, 0:1]),
                        waits=[(s_pe, ev_bc)], inc=s_dv)
                    em.op("vector", lambda h: h.reciprocal(mB[:, 1:2], mB[:, 0:1]),
                          waits=[(s_dv, ev_mb0)])
                    em.op("vector", lambda h: h.drain(), inc=s_dv)
                    sl2 = ncols // 8
                    ev_scl = s_dv.v
                    for g in range(8):
                        a, b = g * sl2, (g + 1) * sl2
                        ev_p1 = em.op("scalar", lambda h, a=a, b=b: h.activation(
                            WTMP[:, a:b], WTMP[:, a:b], AF.Copy, scale=mB[:, 1:2],
                            bias=MAGIC), waits=[(s_dv, ev_scl)], inc=s_ac)
                        em.op("vector", lambda h, a=a, b=b: h.tensor_scalar(
                            WTMP[:, a:b], WTMP[:, a:b], -MAGIC, 1.0,
                            ALU.add, ALU.min), waits=[(s_ac, ev_p1)])
                        em.op("vector", lambda h, a=a, b=b: h.tensor_scalar(
                            WQT[:, a:b], WTMP[:, a:b], -1.0, None, ALU.max),
                            inc=s_dv)
                    ev_q = s_dv.v
                    return dma("sync", spill.ap().rearrange("(j p) f -> p j f", p=128),
                               WQT[:, 0:ncols].rearrange("p (j f) -> p j f", f=cols),
                               waits=[(s_dv, ev_q)], sem=s_st)

                quantize_weight(w1T, E, F, w1q_sp, M1B)
                em.op("vector", lambda h: h.tensor_scalar(
                    CWM[:, :], CWM[:, :], M1B[:, 0:1], None, ALU.mult),
                    waits=[(s_ld, ev_consts)])
                quantize_weight(w2T, F, E, w2q_sp, M2B)

                if STAGES >= 3:
                    dma("gpsimd", bw1_sp.ap(), bw1_d.ap(), sem=s_st)
                    dma("gpsimd", bw2_sp.ap(), bw2_d.ap(), sem=s_st)

                em.flush(nc)

        XR = sbuf("XR", [128, NT * E], F32)
        ev_x = dma("sync", V(XR, E), x_in.ap().rearrange("(i p) f -> p i f", p=128))

        def emit_stage_stats(eps, extra_rho=False, src=None, ev_ready=None):
            for tt in range(NT):
                em.op("vector", lambda h, tt=tt: h.reduce_sum(
                    STAT[:, tt:tt + 1], (src or XR)[:, E * tt:E * (tt + 1)], axis=AXX),
                    waits=[(s_ld, ev_ready or ev_x)] if tt == 0 else [],
                    inc=(s_dv if tt == NT - 1 else None))
                em.op("scalar", lambda h, tt=tt: h.activation(
                    SQT[:, :], (src or XR)[:, E * tt:E * (tt + 1)], AF.Square,
                    accum_out=STAT[:, 16 + tt:17 + tt]),
                    waits=[(s_ld, ev_ready or ev_x)] if tt == 0 else [])
            em.op("scalar", lambda h: h.drain(), inc=s_ac)
            ev_sums = s_dv.v
            ev_sq = s_ac.v
            ev_mean = em.op("vector", lambda h: h.tensor_scalar(
                STAT[:, 32:48], STAT[:, 0:16], 1.0 / E, None, ALU.mult),
                waits=[(s_dv, ev_sums)], inc=s_dv)
            ev_msq = em.op("scalar", lambda h: h.activation(
                STAT[:, 80:96], STAT[:, 32:48], AF.Square),
                waits=[(s_dv, ev_mean)], inc=s_ac)
            ev_ssq = em.op("vector", lambda h: h.tensor_scalar(
                STAT[:, 48:64], STAT[:, 16:32], 1.0 / E, None, ALU.mult),
                waits=[(s_ac, ev_sq)], inc=s_dv)
            ev_var = em.op("vector", lambda h: h.tensor_tensor(
                STAT[:, 48:64], STAT[:, 48:64], STAT[:, 80:96], ALU.subtract),
                waits=[(s_ac, ev_msq), (s_dv, ev_ssq)], inc=s_dv)
            ev_ve = em.op("vector", lambda h, eps=eps: h.tensor_scalar(
                STAT[:, 80:96], STAT[:, 48:64], eps, None, ALU.add),
                waits=[(s_dv, ev_var)], inc=s_dv)
            ev_sd = em.op("scalar", lambda h: h.activation(
                STAT[:, 96:112], STAT[:, 80:96], AF.Sqrt),
                waits=[(s_dv, ev_ve)], inc=s_ac)
            em.op("vector", lambda h: h.reciprocal(
                STAT[:, 112:128], STAT[:, 96:112]),
                waits=[(s_ac, ev_sd)])
            ev_r = em.op("vector", lambda h: h.drain(), inc=s_dv)
            ev_cp = em.op("vector", lambda h: h.tensor_copy(
                STAT[:, 48:64], STAT[:, 112:128]),
                waits=[(s_dv, ev_r)], inc=s_dv)
            if extra_rho:
                ev1 = em.op("scalar", lambda h: h.activation(
                    STAT[:, 80:96], STAT[:, 48:64], AF.Square),
                    waits=[(s_dv, ev_cp)], inc=s_ac)
                em.op("vector", lambda h: h.reciprocal(
                    STAT[:, 96:112], STAT[:, 80:96]),
                    waits=[(s_ac, ev1)])
                ev2 = em.op("vector", lambda h: h.drain(), inc=s_dv)
                ev3 = em.op("vector", lambda h, eps=eps: h.tensor_scalar(
                    STAT[:, 96:112], STAT[:, 96:112], -eps, None, ALU.add),
                    waits=[(s_dv, ev2)], inc=s_dv)
                ev4 = em.op("vector", lambda h: h.tensor_tensor(
                    STAT[:, 96:112], STAT[:, 96:112], STAT[:, 80:96], ALU.mult),
                    waits=[(s_dv, ev3)], inc=s_dv)
                ev5 = em.op("vector", lambda h: h.tensor_scalar(
                    STAT[:, 96:112], STAT[:, 96:112], 1e-6, None, ALU.add),
                    waits=[(s_dv, ev4)], inc=s_dv)
                ev6 = em.op("scalar", lambda h: h.activation(
                    STAT[:, 96:112], STAT[:, 96:112], AF.Sqrt),
                    waits=[(s_dv, ev5)], inc=s_ac)
                em.op("vector", lambda h: h.reciprocal(
                    STAT[:, 112:128], STAT[:, 96:112]),
                    waits=[(s_ac, ev6)])
                ev7 = em.op("vector", lambda h: h.drain(), inc=s_dv)
                ev_cp = em.op("vector", lambda h: h.tensor_tensor(
                    STAT[:, 64:80], STAT[:, 112:128], STAT[:, 48:64], ALU.mult),
                    waits=[(s_dv, ev7)], inc=s_dv)
            return ev_cp

        with nc.sbuf_tensor("WAB", [128, EC * 3 * E], BF16) as WAB, \
             nc.sbuf_tensor("WPB", [128, EC * E], BF16) as WPB, \
             nc.sbuf_tensor("LNT", [128, 2 * 2 * E], BF16) as LNT, \
             nc.sbuf_tensor("LNF", [128, 2 * EC * 256], BF16) as LNF, \
             nc.sbuf_tensor("QF", [128, 2 * 2048], BF16) as QF, \
             nc.sbuf_tensor("KF", [128, 2 * 2048], BF16) as KF, \
             nc.sbuf_tensor("VT", [128, 2 * 2048], BF16) as VT, \
             nc.sbuf_tensor("ESC", [128, 4 * 256], F32) as ESC, \
             nc.sbuf_tensor("EB", [128, 4 * 256], BF16) as EB, \
             nc.sbuf_tensor("ETB", [128, 2 * 512], BF16) as ETB, \
             nc.sbuf_tensor("YF", [128, 2 * 2048], BF16) as YF, \
             nc.sbuf_tensor("SST", [128, 12], F32) as SST:

            ev_wa = em.op("gpsimd", lambda h: h.dma_start(
                out=V(WAB, 3 * E), in_=wattnT.ap().rearrange("(c p) e -> p c e", p=128)),
                inc=s_ld, inc_n=16)
            ev_wp = em.op("gpsimd", lambda h: h.dma_start(
                out=V(WPB, E), in_=wprojT.ap().rearrange("(c p) e -> p c e", p=128)),
                inc=s_ld, inc_n=16)
            ev_ldall = s_ld.v
            globals()["_EV_LDALL"] = ev_ldall

            emit_stage_stats(1e-5, ev_ready=ev_ldall)
            ev_stats = s_dv.v

            last = {}

            for w in range(NW):
                par = w % 2
                lnt0 = par * 2 * E
                lnf0 = par * EC * 256
                qf0 = kf0 = vt0 = yf0 = par * 2048
                ev_lnt = []
                for s in range(2):
                    tt = 2 * w + s
                    ev = em.op("vector", lambda h, tt=tt, s=s, lnt0=lnt0: h.tensor_scalar(
                        LNT[:, lnt0 + s * E:lnt0 + (s + 1) * E],
                        XR[:, E * tt:E * (tt + 1)],
                        STAT[:, 32 + tt:33 + tt], STAT[:, 48 + tt:49 + tt],
                        ALU.subtract, ALU.mult),
                        waits=[(s_dv, ev_stats)] + last.get(("LNT", par), []),
                        inc=s_dv)
                    ev_lnt.append(ev)
                ev_lnf = []
                for s in range(2):
                    for grp in range(2):
                        bk = grp % 2
                        evt = None
                        for jj in range(4):
                            j = grp * 4 + jj
                            evt = em.op("tensor", lambda h, s=s, j=j, jj=jj, bk=bk,
                                        lnt0=lnt0: h.matmul(
                                P[bk][:, jj * 128:(jj + 1) * 128],
                                LNT[:, lnt0 + s * E + j * 128:lnt0 + s * E + (j + 1) * 128],
                                ID[:, :], start=True, stop=True),
                                waits=([(s_dv, ev_lnt[s]), (s_pl, ev_id)]
                                       + last.get(("bank", bk), []))
                                      if jj == 0 else [],
                                inc=(s_pe if jj == 3 else None))
                        ev = em.op("vector", lambda h, s=s, grp=grp, bk=bk,
                                   lnf0=lnf0: h.tensor_copy(
                            V(LNF, 256)[:, lnf0 // 256 + grp * 4:
                                        lnf0 // 256 + grp * 4 + 4, 128 * s:128 * s + 128],
                            P[bk][:, 0:512].rearrange("p (c w) -> p c w", w=128)),
                            waits=[(s_pe, evt)], inc=s_dv)
                        last[("bank", bk)] = [(s_dv, ev)]
                        ev_lnf.append(ev)
                ev_lnf_all = ev_lnf[-1]
                last[("LNT", par)] = [(s_pe, s_pe.v)]
                ev_qk = []
                for j in range(16):
                    bk = 2 + (j % 2)
                    isq = j < 8
                    off = (128 * j) if isq else (E + 128 * (j - 8))
                    evt = None
                    for c in range(EC):
                        evt = em.op("tensor", lambda h, c=c, off=off, bk=bk,
                                    lnf0=lnf0: h.matmul(
                            P[bk][:, 0:256],
                            WAB[:, 3 * E * c + off:3 * E * c + off + 128],
                            LNF[:, lnf0 + 256 * c:lnf0 + 256 * c + 256],
                            start=(c == 0), stop=(c == EC - 1)),
                            waits=([(s_dv, ev_lnf_all), (s_ld, ev_ldall)]
                                   + last.get(("bank", bk), [])) if c == 0 else [],
                            inc=(s_pe if c == EC - 1 else None))
                    if isq:
                        ev = em.op("vector", lambda h, j=j, bk=bk, qf0=qf0: h.tensor_scalar(
                            QF[:, qf0 + 256 * j:qf0 + 256 * j + 256], P[bk][:, 0:256],
                            0.125, None, ALU.mult),
                            waits=[(s_pe, evt)] + last.get(("QF", par), []), inc=s_dv)
                    else:
                        ev = em.op("vector", lambda h, j=j, bk=bk, kf0=kf0: h.tensor_copy(
                            KF[:, kf0 + 256 * (j - 8):kf0 + 256 * (j - 8) + 256],
                            P[bk][:, 0:256]),
                            waits=[(s_pe, evt)] + last.get(("KF", par), []), inc=s_dv)
                    last[("bank", bk)] = [(s_dv, ev)]
                    ev_qk.append(ev)
                last[("QF", par)] = []
                last[("KF", par)] = []
                ev_v = []
                for sv in range(2):
                    for half in range(2):
                        bk = 4 + ((2 * sv + half) % 2)
                        evt = None
                        for c in range(EC):
                            evt = em.op("tensor", lambda h, c=c, sv=sv, half=half,
                                        bk=bk, lnf0=lnf0: h.matmul(
                                P[bk][:, 0:512],
                                LNF[:, lnf0 + 256 * c + 128 * sv:
                                    lnf0 + 256 * c + 128 * sv + 128],
                                WAB[:, 3 * E * c + 2 * E + 512 * half:
                                     3 * E * c + 2 * E + 512 * half + 512],
                                start=(c == 0), stop=(c == EC - 1)),
                                waits=([(s_dv, ev_lnf_all), (s_ld, ev_ldall)]
                                       + last.get(("bank", bk), [])) if c == 0 else [],
                                inc=(s_pe if c == EC - 1 else None))
                        ev = em.op("vector", lambda h, sv=sv, half=half, bk=bk,
                                   vt0=vt0: h.tensor_copy(
                            VT[:, vt0 + 1024 * sv + 512 * half:
                               vt0 + 1024 * sv + 512 * half + 512], P[bk][:, 0:512]),
                            waits=[(s_pe, evt)] + last.get(("VT", par), []), inc=s_dv)
                        last[("bank", bk)] = [(s_dv, ev)]
                        ev_v.append(ev)
                last[("VT", par)] = []
                ev_v_all = ev_v[-1]
                ev_y = []
                for pr in range(H // 2):
                    h0 = 2 * pr
                    chq = pr
                    ev_sc = {}
                    for dh in range(2):
                        hh = h0 + dh
                        poff = 64 * (hh % 2)
                        for isub in range(2):
                            bk = 2 + 2 * dh + isub
                            ev_sc[2 * dh + isub] = em.op(
                                "tensor", lambda h, hh=hh, isub=isub, bk=bk,
                                poff=poff, chq=chq, qf0=qf0, kf0=kf0: h.matmul(
                                    P[bk][:, 0:256],
                                    QF[poff:poff + 64, qf0 + 256 * chq + 128 * isub:
                                       qf0 + 256 * chq + 128 * isub + 128],
                                    KF[poff:poff + 64, kf0 + 256 * chq:kf0 + 256 * chq + 256],
                                    start=True, stop=True),
                                waits=[(s_dv, max(ev_qk[chq], ev_qk[8 + chq]))]
                                      + last.get(("bank", bk), []),
                                inc=s_pe)
                    for sl in range(4):
                        em.op("vector", lambda h, sl=sl: h.reduce_max(
                            SST[:, sl:sl + 1], P[2 + sl][:, 0:256], axis=AXX,
                            negate=True),
                            waits=[(s_pe, ev_sc[sl])] + (last.get(("SSTa",), [])
                                                         if sl == 0 else []))
                    ev_nm = em.op("vector", lambda h: h.drain(), inc=s_dv)
                    for sl in range(4):
                        em.op("scalar", lambda h, sl=sl: h.activation(
                            ESC[:, 256 * sl:256 * sl + 256], P[2 + sl][:, 0:256],
                            AF.Exp, bias=SST[:, sl:sl + 1], scale=1.0,
                            accum_out=SST[:, 4 + sl:5 + sl]),
                            waits=[(s_dv, ev_nm)] + (last.get(("ESC",), [])
                                                     if sl == 0 else []))
                    ev_exp = em.op("scalar", lambda h: h.drain(), inc=s_ac)
                    for sl in range(4):
                        last[("bank", 2 + sl)] = [(s_ac, ev_exp)]
                    for sl in range(4):
                        em.op("vector", lambda h, sl=sl: h.reciprocal(
                            SST[:, 8 + sl:9 + sl], SST[:, 4 + sl:5 + sl]),
                            waits=[(s_ac, ev_exp)] if sl == 0 else [])
                    ev_rc = em.op("vector", lambda h: h.drain(), inc=s_dv)
                    last[("SSTa",)] = [(s_dv, ev_rc)]
                    ev_eb = {}
                    for sl in range(4):
                        ev_eb[sl] = em.op("scalar", lambda h, sl=sl: h.activation(
                            EB[:, 256 * sl:256 * sl + 256],
                            ESC[:, 256 * sl:256 * sl + 256],
                            AF.Copy, scale=SST[:, 8 + sl:9 + sl]),
                            waits=[(s_dv, ev_rc)] + (last.get(("EB",), [])
                                                     if sl == 0 else []),
                            inc=s_ac)
                    last[("ESC",)] = [(s_ac, ev_eb[3])]
                    for dh in range(2):
                        hh = h0 + dh
                        hp = hh % 2
                        poff = 64 * (hh % 2)
                        evt = None
                        for isub in range(2):
                            sl = 2 * dh + isub
                            for jj in range(2):
                                evt = em.op("tensor", lambda h, sl=sl, isub=isub,
                                            jj=jj: h.matmul(
                                    P[6][:, 256 * jj + 128 * isub:
                                         256 * jj + 128 * isub + 128],
                                    EB[:, 256 * sl + 128 * jj:256 * sl + 128 * jj + 128],
                                    ID[:, :], start=True, stop=True),
                                    waits=([(s_ac, ev_eb[sl])]
                                           + (last.get(("bank", 6), [])
                                              if (isub == 0 and jj == 0) else [])),
                                    inc=(s_pe if (isub == 1 and jj == 1) else None))
                        ev_etb = em.op("scalar", lambda h, hp=hp: h.activation(
                            ETB[:, 512 * hp:512 * hp + 512], P[6][:, 0:512], AF.Copy),
                            waits=[(s_pe, evt)] + last.get(("ETB", hp), []), inc=s_ac)
                        last[("bank", 6)] = [(s_ac, ev_etb)]
                        last[("EB",)] = [(s_pe, evt)]
                        evt = None
                        for jsub in range(2):
                            evt = em.op("tensor", lambda h, hh=hh, jsub=jsub, hp=hp,
                                        poff=poff, vt0=vt0: h.matmul(
                                P[7][poff:poff + 64, 0:256],
                                VT[:, vt0 + 1024 * jsub + 64 * hh:
                                   vt0 + 1024 * jsub + 64 * hh + 64],
                                ETB[:, 512 * hp + 256 * jsub:512 * hp + 256 * jsub + 256],
                                start=(jsub == 0), stop=(jsub == 1),
                                tile_position=(0, poff)),
                                waits=([(s_ac, ev_etb), (s_dv, ev_v_all)]
                                       + last.get(("bank", 7), [])) if jsub == 0 else [],
                                inc=(s_pe if jsub == 1 else None))
                        ev = em.op("vector", lambda h, hh=hh, poff=poff, chq=chq,
                                   yf0=yf0: h.tensor_copy(
                            YF[poff:poff + 64, yf0 + 256 * chq:yf0 + 256 * chq + 256],
                            P[7][poff:poff + 64, 0:256]),
                            waits=[(s_pe, evt)] + last.get(("YF", par), []), inc=s_dv)
                        last[("bank", 7)] = [(s_dv, ev)]
                        last[("ETB", hp)] = [(s_pe, evt)]
                        ev_y.append(ev)
                last[("YF", par)] = []
                ev_y_all = ev_y[-1]

                for s in range(2):
                    for half in range(2):
                        bk = (2 * s + half) % 2
                        evt = None
                        for c in range(EC):
                            evt = em.op("tensor", lambda h, c=c, s=s, half=half,
                                        bk=bk, yf0=yf0: h.matmul(
                                P[bk][:, 0:512],
                                YF[:, yf0 + 256 * c + 128 * s:
                                   yf0 + 256 * c + 128 * s + 128],
                                WPB[:, E * c + 512 * half:E * c + 512 * half + 512],
                                start=(c == 0), stop=(c == EC - 1)),
                                waits=([(s_dv, ev_y_all), (s_ld, ev_ldall)]
                                       + last.get(("bank", bk), [])) if c == 0 else [],
                                inc=(s_pe if c == EC - 1 else None))
                        ev = em.op("vector", lambda h, s=s, half=half, bk=bk,
                                   w=w: h.tensor_tensor(
                            XR[:, E * (2 * w + s) + 512 * half:
                               E * (2 * w + s) + 512 * half + 512],
                            XR[:, E * (2 * w + s) + 512 * half:
                               E * (2 * w + s) + 512 * half + 512],
                            P[bk][:, 0:512], ALU.add),
                            waits=[(s_pe, evt)], inc=s_dv)
                        last[("bank", bk)] = [(s_dv, ev)]
                last[("LNT", par)] = [(s_dv, s_dv.v)]
                if DEBUG and w == 0:
                    with nc.sbuf_tensor("DBGF", [128, 2048], F32) as DBGF:
                        def dump(name, srcap, n):
                            ev = em.op("vector", lambda h, srcap=srcap, n=n:
                                       h.tensor_copy(DBGF[:, 0:n], srcap),
                                       waits=[(s_dv, s_dv.v), (s_ac, s_ac.v),
                                              (s_pe, s_pe.v), (s_pl, s_pl.v),
                                              (s_st, s_st.v)],
                                       inc=s_dv)
                            dma("sync", dbg[name][:, 0:n], DBGF[:, 0:n],
                                waits=[(s_dv, ev)], sem=s_st)
                        dump("dlnt", LNT[:, 0:2048], 2048)
                        dump("dlnf", LNF[:, 0:2048], 2048)
                        dump("dqf", QF[:, 0:2048], 2048)
                        dump("dkf", KF[:, 0:2048], 2048)
                        dump("dvt", VT[:, 0:2048], 2048)
                        dump("desc", ESC[:, 0:512], 512)
                        dump("detb", ETB[:, 0:1024], 1024)
                        dump("dyf", YF[:, 0:2048], 2048)
                        dump("dstat", STAT[:, :], 128)
                        em.flush(nc)
            em.flush(nc)

        s_w1s = [Sem(es, nc, f"s_w1s{i}") for i in range(2)]
        s_w2s = [Sem(es, nc, f"s_w2s{i}") for i in range(2)]
        s_hl = Sem(es, nc, "s_hl")
        ev_spills = s_st.v

        def emit_ffn(idx):
            with nc.sbuf_tensor(f"XQF{idx}", [128, EC * 2052], BF16) as XQF, \
                 nc.sbuf_tensor(f"LNS{idx}", [128, 2 * E], F32) as LNS, \
                 nc.sbuf_tensor(f"XQS{idx}", [128, 2 * E], BF16) as XQS, \
                 nc.sbuf_tensor(f"AG16{idx}", [16, E], BF16) as AG16, \
                 nc.sbuf_tensor(f"W1S{idx}", [128, 2 * 4096], BF16) as W1S, \
                 nc.sbuf_tensor(f"W2S{idx}", [128, 2 * 4096], BF16) as W2S, \
                 nc.sbuf_tensor(f"G{idx}", [128, FC * 256], F32) as G, \
                 nc.sbuf_tensor(f"CT{idx}", [128, 2 * 512], F32) as CT, \
                 nc.sbuf_tensor(f"GS{idx}", [128, 2 * 256], F32) as GS, \
                 nc.sbuf_tensor(f"GQ{idx}", [128, FC * 256], BF16) as GQ, \
                 nc.sbuf_tensor(f"MB{idx}", [128, 512], F32) as MB, \
                 nc.sbuf_tensor(f"S2R{idx}", [1, 256], F32) as S2R, \
                 nc.sbuf_tensor(f"S2B{idx}", [128, 256], F32) as S2B, \
                 nc.sbuf_tensor(f"QST{idx}", [128, 64], F32) as QST:

                last = {}
                ev_stats = emit_stage_stats(1e-5, ev_ready=_EV_LDALL)
                em.op("gpsimd", lambda h: h.memset(
                    V(XQF, 2052)[:, :, 0:1], 0.0))
                ev_guard0 = em.op("gpsimd", lambda h: h.memset(
                    V(XQF, 2052)[:, :, 2049:2052], 0.0), inc=s_pl)

                ev_xqs = {}
                for tt in range(NT):
                    par = tt % 2
                    ev_ln = em.op("vector", lambda h, tt=tt, par=par: h.tensor_scalar(
                        LNS[:, par * E:(par + 1) * E], XR[:, E * tt:E * (tt + 1)],
                        STAT[:, 32 + tt:33 + tt], STAT[:, 48 + tt:49 + tt],
                        ALU.subtract, ALU.mult),
                        waits=[(s_dv, ev_stats)], inc=s_dv)
                    ev_mx = em.op("vector", lambda h, tt=tt, par=par: h.reduce_max(
                        QST[:, tt:tt + 1], LNS[:, par * E:(par + 1) * E], axis=AXX,
                        apply_absolute_value=True),
                        waits=[(s_dv, ev_ln)], inc=s_dv)
                    ev_mc = em.op("vector", lambda h, tt=tt: h.tensor_scalar(
                        QST[:, 16 + tt:17 + tt], QST[:, tt:tt + 1], 1e-5, None,
                        ALU.max), waits=[(s_dv, ev_mx)], inc=s_dv)
                    em.op("vector", lambda h, tt=tt: h.reciprocal(
                        QST[:, 32 + tt:33 + tt], QST[:, 16 + tt:17 + tt]),
                        waits=[(s_dv, ev_mc)])
                    ev_ri = em.op("vector", lambda h: h.drain(), inc=s_dv)
                    em.op("vector", lambda h, tt=tt, par=par: h.tensor_scalar(
                        LNS[:, par * E:(par + 1) * E], LNS[:, par * E:(par + 1) * E],
                        QST[:, 32 + tt:33 + tt], 127.0, ALU.mult, ALU.mult),
                        waits=[(s_dv, ev_ri)] + last.get(("XQS", par), []))
                    em.op("vector", lambda h, par=par: h.tensor_scalar(
                        LNS[:, par * E:(par + 1) * E], LNS[:, par * E:(par + 1) * E],
                        MAGIC, -MAGIC, ALU.add, ALU.add))
                    ev_xi = em.op("vector", lambda h, tt=tt, par=par: h.tensor_scalar(
                        XQS[:, par * E:(par + 1) * E], LNS[:, par * E:(par + 1) * E],
                        QST[:, 16 + tt:17 + tt], 1.0 / 127.0, ALU.mult, ALU.mult),
                        inc=s_dv)
                    ev_xqs[tt] = ev_xi
                    if tt == 0:
                        ev_h0 = dma("sync", ag_in[idx][0:1, :], XQS[0:1, 0:E],
                                    waits=[(s_dv, ev_xi)], sem=s_hl)
                    if tt == NT - 1:
                        ev_h1 = dma("sync", ag_in[idx][1:2, :], XQS[127:128, par * E:(par + 1) * E],
                                    waits=[(s_dv, ev_xi)], sem=s_hl)
                    for grp in range(2):
                        bk = grp
                        evt = None
                        for jj in range(4):
                            j = grp * 4 + jj
                            evt = em.op("tensor", lambda h, j=j, jj=jj, bk=bk, par=par: h.matmul(
                                P[bk][:, jj * 128:(jj + 1) * 128],
                                XQS[:, par * E + j * 128:par * E + (j + 1) * 128],
                                ID[:, :], start=True, stop=True),
                                waits=([(s_dv, ev_xi)] + last.get(("bank", bk), []))
                                      if jj == 0 else [],
                                inc=(s_pe if jj == 3 else None))
                        ev = em.op("vector", lambda h, tt=tt, grp=grp, bk=bk: h.tensor_copy(
                            V(XQF, 2052)[:, grp * 4:grp * 4 + 4, 1 + 128 * tt:1 + 128 * tt + 128],
                            P[bk][:, 0:512].rearrange("p (c w) -> p c w", w=128)),
                            waits=[(s_pe, evt)], inc=s_dv)
                        last[("bank", bk)] = [(s_dv, ev)]
                    last[("XQS", par)] = [(s_pe, s_pe.v)] + (
                        [(s_hl, ev_h0)] if tt == 0 else [])
                ev_ag = em.op("gpsimd", lambda h: h.collective_compute(
                    "AllGather", ALU.bypass, replica_groups=[list(range(NC))],
                    ins=[ag_in[idx].ap().opt()], outs=[ag_out[idx].ap().opt()]),
                    waits=[(s_hl, s_hl.v)], inc=s_cc)
                ev_ag16 = em.op("gpsimd", lambda h: h.dma_start(
                    out=AG16[:, :], in_=ag_out[idx][:, :]),
                    waits=[(s_cc, ev_ag)], inc=s_ld, inc_n=16)
                evg = None
                for side in range(2):
                    for j in range(8):
                        evg = em.op("tensor", lambda h, side=side, j=j: h.matmul(
                            P[2][:, 8 * side + j:8 * side + j + 1],
                            AG16[:, 128 * j:128 * j + 128], SELc[:, side:side + 1],
                            start=True, stop=True),
                            waits=[(s_ld, s_ld.v), (s_pl, ev_guard0)]
                                  if (side == 0 and j == 0) else [],
                            inc=(s_pe if (side == 1 and j == 7) else None))
                ev_gl = em.op("vector", lambda h: h.tensor_copy(
                    V(XQF, 2052)[:, :, 0:1],
                    P[2][:, 0:8].rearrange("p (c w) -> p c w", w=1)),
                    waits=[(s_pe, evg)], inc=s_dv)
                ev_gr = em.op("vector", lambda h: h.tensor_copy(
                    V(XQF, 2052)[:, :, 2049:2050],
                    P[2][:, 8:16].rearrange("p (c w) -> p c w", w=1)),
                    inc=s_dv)
                last[("bank", 2)] = [(s_dv, ev_gr)]
                ev_xqf = s_dv.v
                if FFN_PART <= 1:
                    em.flush(nc)
                    return

                spill1 = w1q_sp if idx == 0 or True else None
                def w1_dma(g):
                    slot = g % 2
                    wv = [(s_pe, last.get(("w1use", slot), 0)), (s_st, ev_spills)]
                    return em.op("sync", lambda h, g=g, slot=slot: h.dma_start(
                        out=V(W1S, 512)[:, slot * 8:slot * 8 + 8, :],
                        in_=w1q_sp.ap()[:, 512 * g:512 * (g + 1)].rearrange(
                            "(j p) f -> p j f", p=128)),
                        waits=[w for w in wv if w[1] > 0], inc=s_w1s[slot], inc_n=16)

                def w2_dma(g):
                    slot = g % 2
                    wv = [(s_pe, last.get(("w2use", slot), 0)), (s_st, ev_spills)]
                    return em.op("sync", lambda h, g=g, slot=slot: h.dma_start(
                        out=V(W2S, 1024)[:, slot * 4:slot * 4 + 4, :],
                        in_=w2q_sp.ap()[512 * g:512 * (g + 1), :].rearrange(
                            "(j p) f -> p j f", p=128)),
                        waits=[w for w in wv if w[1] > 0], inc=s_w2s[slot], inc_n=16)

                ev_gs = {}
                ev_mm2 = {}
                ev_gq = {}

                def pass1(k):
                    ev_w1 = {}
                    ev_w1[0] = w1_dma(0)
                    ev_w1[1] = w1_dma(1)
                    em.op("gpsimd", lambda h: h.memset(MB[:, 0:512], 0.0),
                          waits=last.get(("MBacc",), []) + last.get(("MBr", 0), []),
                          inc=s_pl)
                    for c in range(FC):
                        g = c // 4
                        slot = g % 2
                        par = c % 2
                        bk = par
                        evt = None
                        for d in range(EC):
                            evt = em.op("tensor", lambda h, c=c, d=d, k=k, slot=slot,
                                        bk=bk: h.matmul(
                                P[bk][:, 0:258],
                                V(W1S, 4096)[:, slot, 512 * d + 128 * (c % 4):
                                             512 * d + 128 * (c % 4) + 128],
                                XQF[:, 2052 * d + 256 * k:2052 * d + 256 * k + 258],
                                start=(d == 0), stop=(d == EC - 1)),
                                waits=([(s_w1s[slot], ev_w1[g]), (s_dv, ev_xqf)]
                                       + last.get(("bank", bk), [])) if d == 0 else [],
                                inc=(s_pe if d == EC - 1 else None))
                        if c % 4 == 3:
                            last[("w1use", slot)] = evt
                            if g + 2 <= 7:
                                ev_w1[g + 2] = w1_dma(g + 2)
                        ev_t0 = em.op("scalar", lambda h, c=c, par=par, bk=bk: h.activation(
                            CT[:, 512 * par:512 * par + 256], P[bk][:, 0:256],
                            AF.Copy, scale=CWM[:, 3 * c:3 * c + 1]),
                            waits=[(s_pe, evt)] + last.get(("CTp", par), []), inc=s_ac)
                        ev_t2 = em.op("scalar", lambda h, c=c, par=par, bk=bk: h.activation(
                            CT[:, 512 * par + 256:512 * par + 512], P[bk][:, 2:258],
                            AF.Copy, scale=CWM[:, 3 * c + 2:3 * c + 3]),
                            inc=s_ac)
                        ev_st = em.op("vector", lambda h, c=c, par=par, bk=bk: h.scalar_tensor_tensor(
                            CT[:, 512 * par:512 * par + 256], P[bk][:, 1:257],
                            CWM[:, 3 * c + 1:3 * c + 2],
                            CT[:, 512 * par:512 * par + 256], ALU.mult, ALU.add),
                            waits=[(s_ac, ev_t2)], inc=s_dv)
                        last[("bank", bk)] = [(s_dv, ev_st), (s_ac, ev_t2)]
                        ev_pl = em.op("gpsimd", lambda h, par=par: h.tensor_tensor(
                            CT[:, 512 * par:512 * par + 256],
                            CT[:, 512 * par:512 * par + 256],
                            CT[:, 512 * par + 256:512 * par + 512], ALU.add),
                            waits=[(s_dv, ev_st), (s_ac, ev_t2)], inc=s_pl)
                        ev_si = em.op("scalar", lambda h, c=c, par=par: h.activation(
                            G[:, 256 * c:256 * c + 256], CT[:, 512 * par:512 * par + 256],
                            AF.Silu, bias=CBc[:, c:c + 1], scale=1.0),
                            waits=[(s_pl, ev_pl)] + ([(s_dv, ev_gs[c])] if c in ev_gs else []),
                            inc=s_ac)
                        em.op("vector", lambda h, c=c: h.tensor_tensor(
                            MB[:, 0:256], MB[:, 0:256], G[:, 256 * c:256 * c + 256],
                            ALU.max),
                            waits=[(s_ac, ev_si)] + ([(s_pl, s_pl.v)] if c == 0 else []))
                        ev_mm = em.op("vector", lambda h, c=c: h.tensor_tensor(
                            MB[:, 256:512], MB[:, 256:512], G[:, 256 * c:256 * c + 256],
                            ALU.min), inc=(s_dv if c == FC - 1 else None))
                        last[("CTp", par)] = [(s_ac, ev_si), (s_pl, ev_pl)]
                    last[("Macc",)] = [(s_dv, s_dv.v)]

                def pass15(k):
                    ev_mb = em.op("vector", lambda h: h.scalar_tensor_tensor(
                        MB[:, 0:256], MB[:, 256:512], -1.0, MB[:, 0:256],
                        ALU.mult, ALU.max),
                        waits=last.get(("Macc",), []) + last.get(("MBr", 0), []),
                        inc=s_dv)
                    evt = None
                    for sM in range(2):
                        evt = em.op("tensor", lambda h, sM=sM: h.matmul(
                            P[2][:, 128 * sM:128 * sM + 128], MB[:, 128 * sM:128 * sM + 128],
                            IDF[:, :], start=True, stop=True),
                            waits=([(s_dv, ev_mb)] + last.get(("bank", 2), []))
                                  if sM == 0 else [],
                            inc=(s_pe if sM == 1 else None))
                    last[("MBr", 0)] = [(s_pe, evt)]
                    last[("MBacc",)] = [(s_dv, ev_mb)]
                    ev_mt = None
                    for sM in range(2):
                        ev_mt = em.op("vector", lambda h, sM=sM: h.reduce_max(
                            QST[:, 48 + sM:49 + sM], P[2][:, 128 * sM:128 * sM + 128],
                            axis=AXX), waits=[(s_pe, evt)], inc=s_dv)
                    ev_mc2 = em.op("vector", lambda h: h.tensor_scalar(
                        QST[:, 50:52], QST[:, 48:50], 1e-5, None, ALU.max),
                        waits=[(s_dv, ev_mt)], inc=s_dv)
                    ev_u2 = em.op("vector", lambda h: h.tensor_scalar(
                        QST[:, 52:54], QST[:, 50:52], M2B[:, 0:1], 1.0 / 127.0,
                        ALU.mult, ALU.mult), waits=[(s_dv, ev_mc2)], inc=s_dv)
                    em.op("vector", lambda h: h.reciprocal(
                        QST[:, 54:56], QST[:, 50:52]),
                        waits=[(s_dv, ev_mc2)])
                    ev_rc2 = em.op("vector", lambda h: h.drain(), inc=s_dv)
                    ev_s2c = em.op("vector", lambda h: h.tensor_scalar(
                        QST[:, 56:58], QST[:, 54:56], 127.0, None, ALU.mult),
                        waits=[(s_dv, ev_rc2)], inc=s_dv)
                    evt = None
                    for sM in range(2):
                        evt = em.op("tensor", lambda h, sM=sM: h.matmul(
                            P[3][0:1, 128 * sM:128 * sM + 128],
                            QST[:, 56 + sM:57 + sM], IDF[:, :],
                            start=True, stop=True),
                            waits=([(s_dv, ev_s2c)] + last.get(("bank", 3), []))
                                  if sM == 0 else [],
                            inc=(s_pe if sM == 1 else None))
                    ev_s2r = em.op("vector", lambda h: h.tensor_copy(
                        S2R[0:1, :], P[3][0:1, 0:256]),
                        waits=[(s_pe, evt)], inc=s_dv)
                    last[("bank", 3)] = [(s_dv, ev_s2r)]
                    ev_bc = em.op("tensor", lambda h: h.matmul(
                        P[2][:, 128:384], ONE1F[0:1, :], S2R[0:1, :],
                        start=True, stop=True),
                        waits=[(s_dv, ev_s2r), (s_dv, ev_mt)], inc=s_pe)
                    ev_s2b = em.op("vector", lambda h: h.tensor_copy(
                        S2B[:, :], P[2][:, 128:384]),
                        waits=[(s_pe, ev_bc)], inc=s_dv)
                    last[("bank", 2)] = [(s_dv, ev_s2b)]
                    return ev_s2b, ev_u2

                def quant(k, ev_s2b):
                    prev_mm2 = dict(ev_mm2)
                    ev_gs.clear()
                    ev_gq.clear()
                    for c in range(FC):
                        par = c % 2
                        ev_g = em.op("vector", lambda h, c=c, par=par: h.tensor_tensor(
                            GS[:, 256 * par:256 * par + 256], G[:, 256 * c:256 * c + 256],
                            S2B[:, :], ALU.mult),
                            waits=[(s_dv, ev_s2b)]
                                  + ([(s_pe, prev_mm2[c])] if c in prev_mm2 else []),
                            inc=s_dv)
                        ev_gs[c] = ev_g
                        ev_gq[c] = em.op("vector", lambda h, c=c, par=par: h.tensor_scalar(
                            GQ[:, 256 * c:256 * c + 256], GS[:, 256 * par:256 * par + 256],
                            MAGIC, -MAGIC, ALU.add, ALU.add), inc=s_dv)

                def pass2(k, ev_u2):
                    ev_w2 = {}
                    ev_w2[0] = w2_dma(0)
                    ev_w2[1] = w2_dma(1)
                    ev_mm2.clear()
                    for c in range(FC):
                        g = c // 4
                        slot = g % 2
                        evt = None
                        for sM in range(2):
                            for eh in range(2):
                                bi = 4 + 2 * sM + eh
                                evt = em.op("tensor", lambda h, c=c, sM=sM, eh=eh,
                                            slot=slot, bi=bi: h.matmul(
                                    P[bi][:, 0:512],
                                    GQ[:, 256 * c + 128 * sM:256 * c + 128 * sM + 128],
                                    V(W2S, 4096)[:, slot, 1024 * (c % 4) + 512 * eh:
                                                 1024 * (c % 4) + 512 * eh + 512],
                                    start=(c == 0), stop=(c == FC - 1)),
                                    waits=([(s_dv, ev_gq[c]), (s_w2s[slot], ev_w2[g])]
                                           + (last.get(("bank", bi), []) if c == 0 else [])),
                                    inc=(s_pe if (sM == 1 and eh == 1) else None))
                        ev_mm2[c] = evt
                        if c % 4 == 3:
                            last[("w2use", slot)] = evt
                            if g + 2 <= 7:
                                ev_w2[g + 2] = w2_dma(g + 2)
                    for sM in range(2):
                        for eh in range(2):
                            bi = 4 + 2 * sM + eh
                            ev = em.op("vector", lambda h, k=k, sM=sM, eh=eh, bi=bi: h.scalar_tensor_tensor(
                                XR[:, E * (2 * k + sM) + 512 * eh:E * (2 * k + sM) + 512 * eh + 512],
                                P[bi][:, 0:512], QST[:, 52 + sM:53 + sM],
                                XR[:, E * (2 * k + sM) + 512 * eh:E * (2 * k + sM) + 512 * eh + 512],
                                ALU.mult, ALU.add),
                                waits=[(s_pe, ev_mm2[FC - 1]), (s_dv, ev_u2)], inc=s_dv)
                            last[("bank", bi)] = [(s_dv, ev)]

                pass1(0)
                for k in range(8):
                    ev_s2b, ev_u2 = pass15(k)
                    quant(k, ev_s2b)
                    if k + 1 < 8:
                        pass1(k + 1)
                    pass2(k, ev_u2)

                em.flush(nc)

        if STAGES >= 2:
            emit_ffn(0)

        def emit_attnb():
            with nc.sbuf_tensor("LN1F", [128, EC * 2048], BF16) as LN1F:
                emit_stage_stats(1e-5, extra_rho=True)
                ev_rho = s_dv.v
                with nc.sbuf_tensor("WKV", [128, 2 * EC * E], BF16) as WKV, \
                     nc.sbuf_tensor("QP", [128, EC * 2048], BF16) as QP, \
                     nc.sbuf_tensor("LNT2", [128, E], BF16) as LNT2, \
                     nc.sbuf_tensor("KPT", [128, E], BF16) as KPT, \
                     nc.sbuf_tensor("VTT", [128, E], BF16) as VTT, \
                     nc.sbuf_tensor("AD1T", [128, E], F32) as AD1T, \
                     nc.sbuf_tensor("ATT", [128, E], BF16) as ATT, \
                     nc.sbuf_tensor("ATF", [128, E], BF16) as ATF, \
                     nc.sbuf_tensor("DENR", [128, 36], F32) as DENR, \
                     nc.sbuf_tensor("KVSB", [64, H * 65], F32) as KVSB, \
                     nc.sbuf_tensor("KVSS", [128, H * 65], BF16) as KVSS:
                    last = {}
                    ev_wk = em.op("gpsimd", lambda h: h.dma_start(
                        out=V(WKV, E)[:, 0:EC, :],
                        in_=wk_d.ap().rearrange("(c p) e -> p c e", p=128)),
                        inc=s_ld, inc_n=16)
                    ev_wv = em.op("gpsimd", lambda h: h.dma_start(
                        out=V(WKV, E)[:, EC:2 * EC, :],
                        in_=wv_d.ap().rearrange("(c p) e -> p c e", p=128)),
                        inc=s_ld, inc_n=16)
                    ev_wkv = s_ld.v
                    ev_l1 = {}
                    for tt in range(NT):
                        ev_ln = em.op("vector", lambda h, tt=tt: h.tensor_scalar(
                            LNT2[:, :], XR[:, E * tt:E * (tt + 1)],
                            STAT[:, 32 + tt:33 + tt], STAT[:, 64 + tt:65 + tt],
                            ALU.subtract, ALU.mult),
                            waits=[(s_dv, ev_rho)] + last.get(("LNT2",), []), inc=s_dv)
                        for grp in range(2):
                            bk = grp
                            evt = None
                            for jj in range(4):
                                j = grp * 4 + jj
                                evt = em.op("tensor", lambda h, j=j, jj=jj, bk=bk: h.matmul(
                                    P[bk][:, jj * 128:(jj + 1) * 128],
                                    LNT2[:, j * 128:(j + 1) * 128],
                                    ID[:, :], start=True, stop=True),
                                    waits=([(s_dv, ev_ln)] + last.get(("bank", bk), []))
                                          if jj == 0 else [],
                                    inc=(s_pe if jj == 3 else None))
                            ev = None
                            for jj in range(4):
                                j = grp * 4 + jj
                                ev = em.op("vector", lambda h, tt=tt, j=j, jj=jj, bk=bk: h.tensor_scalar(
                                    LN1F[:, 2048 * j + 128 * tt:2048 * j + 128 * tt + 128],
                                    P[bk][:, 128 * jj:128 * jj + 128],
                                    GAMc[:, j:j + 1], BETc[:, j:j + 1],
                                    ALU.mult, ALU.add),
                                    waits=[(s_pe, evt)] if jj == 0 else [],
                                    inc=(s_dv if jj == 3 else None))
                            last[("bank", bk)] = [(s_dv, ev)]
                        last[("LNT2",)] = [(s_pe, s_pe.v)]
                        ev_l1[tt] = s_dv.v
                    if C_PART <= 1:
                        em.flush(nc)
                        return
                    for tt in range(NT):
                        evk = []
                        for (bk, wof) in ((2, 0), (3, EC)):
                            for half in range(2):
                                evt = None
                                for c in range(EC):
                                    evt = em.op("tensor", lambda h, c=c, tt=tt, half=half,
                                                bk=bk, wof=wof: h.matmul(
                                        P[bk][:, 512 * half:512 * half + 512] if False else
                                        P[bk][:, 0:512],
                                        LN1F[:, 2048 * c + 128 * tt:2048 * c + 128 * tt + 128],
                                        WKV[:, E * (wof + c) + 512 * half:
                                            E * (wof + c) + 512 * half + 512],
                                        start=(c == 0), stop=(c == EC - 1)),
                                        waits=([(s_dv, ev_l1[tt]), (s_ld, ev_wkv)]
                                               + last.get(("bank", bk), [])) if c == 0 else [],
                                        inc=(s_pe if c == EC - 1 else None))
                                if bk == 2:
                                    ev = em.op("vector", lambda h, half=half: h.tensor_scalar(
                                        KPT[:, 512 * half:512 * half + 512], P[2][:, 0:512],
                                        0.0, 1e-3, ALU.max, ALU.add),
                                        waits=[(s_pe, evt)] + last.get(("KPT",), []),
                                        inc=s_dv)
                                else:
                                    ev = em.op("vector", lambda h, half=half: h.tensor_copy(
                                        VTT[:, 512 * half:512 * half + 512], P[3][:, 0:512]),
                                        waits=[(s_pe, evt)] + last.get(("VTT",), []),
                                        inc=s_dv)
                                last[("bank", bk)] = [(s_dv, ev)]
                                evk.append(ev)
                        ev_kv = s_dv.v
                        evt = None
                        for hh in range(H):
                            bk = 6 + hh // 8
                            em.op("tensor", lambda h, hh=hh, bk=bk, tt=tt: h.matmul(
                                P[bk][0:64, 64 * (hh % 8):64 * (hh % 8) + 64],
                                KPT[:, 64 * hh:64 * hh + 64],
                                VTT[:, 64 * hh:64 * hh + 64],
                                start=(tt == 0 and hh % 8 == 0),
                                stop=(tt == NT - 1 and hh % 8 == 7),
                                skip_group_check=True),
                                waits=[(s_dv, ev_kv)] if hh == 0 else [])
                            evt = em.op("tensor", lambda h, hh=hh, tt=tt: h.matmul(
                                P[4][0:64, hh:hh + 1],
                                KPT[:, 64 * hh:64 * hh + 64], ONECB[:, 0:1],
                                start=(tt == 0 and hh == 0),
                                stop=(tt == NT - 1 and hh == H - 1),
                                skip_group_check=True),
                                inc=(s_pe if hh == H - 1 else None))
                        last[("KPT",)] = [(s_pe, evt)]
                        last[("VTT",)] = [(s_pe, evt)]
                    ev_kvs_done = s_pe.v
                    if C_PART <= 2:
                        em.flush(nc)
                        return
                    ev_wq = em.op("gpsimd", lambda h: h.dma_start(
                        out=V(WKV, E)[:, 0:EC, :],
                        in_=wq_d.ap().rearrange("(c p) e -> p c e", p=128)),
                        waits=[(s_pe, ev_kvs_done)], inc=s_ld, inc_n=16)
                    ev_wo = em.op("gpsimd", lambda h: h.dma_start(
                        out=V(WKV, E)[:, EC:2 * EC, :],
                        in_=wo_d.ap().rearrange("(c p) e -> p c e", p=128)),
                        inc=s_ld, inc_n=16)
                    for tt in range(NT):
                        for grp in range(2):
                            bk = 2 + grp
                            evt = None
                            for jj in range(4):
                                j = grp * 4 + jj
                                ev0 = None
                                for c in range(EC):
                                    evt = em.op("tensor", lambda h, c=c, j=j, jj=jj, tt=tt,
                                                bk=bk: h.matmul(
                                        P[bk][:, 128 * jj:128 * jj + 128],
                                        WKV[:, E * c + 128 * j:E * c + 128 * j + 128],
                                        LN1F[:, 2048 * c + 128 * tt:2048 * c + 128 * tt + 128],
                                        start=(c == 0), stop=(c == EC - 1)),
                                        waits=([(s_ld, ev_wq)] + last.get(("bank", bk), []))
                                              if (jj == 0 and c == 0) else [],
                                        inc=(s_pe if (jj == 3 and c == EC - 1) else None))
                            ev = em.op("vector", lambda h, tt=tt, grp=grp, bk=bk: h.tensor_scalar(
                                V(QP, 2048)[:, grp * 4:grp * 4 + 4, 128 * tt:128 * tt + 128],
                                P[bk][:, 0:512].rearrange("p (c w) -> p c w", w=128),
                                0.0, 1e-3, ALU.max, ALU.add),
                                waits=[(s_pe, evt)], inc=s_dv)
                            last[("bank", bk)] = [(s_dv, ev)]
                    ev_qp = s_dv.v
                    evs = []
                    for hh in range(H):
                        bk = 6 + hh // 8
                        ev = em.op("vector", lambda h, hh=hh, bk=bk: h.tensor_copy(
                            KVSB[:, 65 * hh:65 * hh + 64],
                            P[bk][0:64, 64 * (hh % 8):64 * (hh % 8) + 64]),
                            waits=[(s_pe, ev_kvs_done)] if hh == 0 else [], inc=None)
                    ev_ks = em.op("vector", lambda h: h.tensor_copy(
                        KVSB[:, :].rearrange("p (x u) -> p x u", u=65)[:, :, 64:65],
                        P[4][0:64, 0:16].rearrange("p (x u) -> p x u", u=1)),
                        inc=s_dv)
                    ev_kvsb = em.op("sync", lambda h: h.dma_start(
                        out=ar_in.ap(), in_=KVSB[:, :]),
                        waits=[(s_dv, ev_ks)], inc=s_st, inc_n=16)
                    ev_ar = em.op("gpsimd", lambda h: h.collective_compute(
                        "AllReduce", ALU.add,
                        replica_groups=[[0, 1, 2, 3], [4, 5, 6, 7]],
                        ins=[ar_in.ap().opt()], outs=[ar_out.ap().opt()]),
                        waits=[(s_st, ev_kvsb)], inc=s_cc)
                    em.op("gpsimd", lambda h: h.dma_start(
                        out=KVSS[0:64, :], in_=ar_out.ap()),
                        waits=[(s_cc, ev_ar)], inc=s_ld, inc_n=16)
                    em.op("gpsimd", lambda h: h.dma_start(
                        out=KVSS[64:128, :], in_=ar_out.ap()),
                        inc=s_ld, inc_n=16)
                    ev_kvss_v = s_ld.v
                    last[("bank", 4)] = [(s_dv, ev_ks)]
                    last[("bank", 6)] = [(s_dv, ev_ks)]
                    last[("bank", 7)] = [(s_dv, ev_ks)]
                    if C_PART <= 3:
                        em.flush(nc)
                        return
                    if DEBUG_C:
                        dma("gpsimd", dbg["cln1f"][:, :], LN1F[:, 0:2048],
                            waits=[(s_dv, s_dv.v), (s_pe, s_pe.v)], sem=s_st)
                        dma("gpsimd", dbg["cqp"][:, :], QP[:, 0:2048],
                            waits=[(s_dv, s_dv.v)], sem=s_st)
                        dma("sync", dbg["ckvsb"][:, :], KVSB[:, :],
                            waits=[(s_dv, s_dv.v)], sem=s_st)
                        dma("gpsimd", dbg["ckvss"][:, :], KVSS[:, :],
                            waits=[(s_ld, s_ld.v)], sem=s_st)
                    for tt in range(NT):
                        ev_nd = [None] * 4
                        for hh in range(H):
                            bk = hh % 4
                            evt = em.op("tensor", lambda h, hh=hh, tt=tt, bk=bk: h.matmul(
                                P[bk][:, 68 * (hh // 4):68 * (hh // 4) + 65],
                                QP[64 * (hh % 2):64 * (hh % 2) + 64,
                                   2048 * (hh // 2) + 128 * tt:2048 * (hh // 2) + 128 * tt + 128],
                                KVSS[64 * (hh % 2):64 * (hh % 2) + 64, 65 * hh:65 * hh + 65],
                                start=True, stop=True),
                                waits=([(s_ld, ev_kvss_v), (s_dv, ev_qp)]
                                       + last.get(("bank", bk), [])) if hh < 4 else [],
                                inc=(s_pe if hh >= 12 else None))
                            if hh >= 12:
                                ev_nd[hh % 4] = evt
                        for b4 in range(4):
                            em.op("vector", lambda h, b4=b4: h.tensor_copy(
                                DENR[:, 4 * b4:4 * b4 + 4].rearrange("p (x u) -> p x u", u=1),
                                P[b4][:, 0:272].rearrange("p (x u) -> p x u", u=68)[:, :, 64:65]),
                                waits=[(s_pe, ev_nd[b4])] + (last.get(("DENR",), []) if b4 == 0 else []))
                        em.op("vector", lambda h: h.drain())
                        em.op("vector", lambda h: h.reciprocal(DENR[:, 16:32], DENR[:, 0:16]))
                        ev_den = em.op("vector", lambda h: h.drain(), inc=s_dv)
                        ev_att = None
                        for hh in range(H):
                            bk = hh % 4
                            di = 16 + 4 * (hh % 4) + hh // 4
                            ev_att = em.op("vector", lambda h, hh=hh, bk=bk, di=di: h.tensor_scalar(
                                ATT[:, 64 * hh:64 * hh + 64],
                                P[bk][:, 68 * (hh // 4):68 * (hh // 4) + 64],
                                DENR[:, di:di + 1], None, ALU.mult),
                                waits=([(s_dv, ev_den)] + last.get(("ATT",), []))
                                      if hh == 0 else [],
                                inc=(s_dv if hh == H - 1 else None))
                        for b4 in range(4):
                            last[("bank", b4)] = [(s_dv, ev_att)]
                        last[("DENR",)] = [(s_dv, ev_att)]
                        if C2_SUB <= 1:
                            continue
                        for grp in range(2):
                            evt = None
                            for jj in range(4):
                                j = grp * 4 + jj
                                evt = em.op("tensor", lambda h, j=j, jj=jj: h.matmul(
                                    P[5][:, 128 * jj:128 * jj + 128],
                                    ATT[:, 128 * j:128 * j + 128], ID[:, :],
                                    start=True, stop=True),
                                    waits=([(s_dv, ev_att)] + last.get(("bank", 5), []))
                                          if jj == 0 else [],
                                    inc=(s_pe if jj == 3 else None))
                            ev = em.op("scalar", lambda h, grp=grp: h.activation(
                                V(ATF, 128)[:, grp * 4:grp * 4 + 4, :],
                                P[5][:, 0:512].rearrange("p (c w) -> p c w", w=128),
                                AF.Copy),
                                waits=[(s_pe, evt)] + (last.get(("ATF",), []) if grp == 0 else []),
                                inc=s_ac)
                            last[("bank", 5)] = [(s_ac, ev)]
                        ev_atf = s_ac.v
                        last[("ATT",)] = [(s_pe, s_pe.v)]
                        ev_wo_mm = []
                        for half in range(2):
                            bk = 6 + half
                            evt = None
                            for j in range(EC):
                                evt = em.op("tensor", lambda h, j=j, half=half, bk=bk: h.matmul(
                                    P[bk][:, 0:512],
                                    ATF[:, 128 * j:128 * j + 128],
                                    WKV[:, E * (EC + j) + 512 * half:
                                        E * (EC + j) + 512 * half + 512],
                                    start=(j == 0), stop=(j == EC - 1)),
                                    waits=([(s_ac, ev_atf), (s_ld, ev_wo)]
                                           + last.get(("bank", bk), [])) if j == 0 else [],
                                    inc=(s_pe if j == EC - 1 else None))
                            ev_wo_mm.append(evt)
                        ev_lo = em.op("vector", lambda h, tt=tt: h.tensor_scalar(
                            AD1T[:, :], XR[:, E * tt:E * (tt + 1)],
                            STAT[:, 32 + tt:33 + tt], STAT[:, 48 + tt:49 + tt],
                            ALU.subtract, ALU.mult),
                            waits=[(s_dv, s_dv.v)] + last.get(("AD1T",), []), inc=s_dv)
                        for half in range(2):
                            bk = 6 + half
                            ev = em.op("vector", lambda h, half=half, bk=bk: h.tensor_tensor(
                                AD1T[:, 512 * half:512 * half + 512],
                                AD1T[:, 512 * half:512 * half + 512],
                                P[bk][:, 0:512], ALU.add),
                                waits=[(s_pe, ev_wo_mm[half]), (s_dv, ev_lo)], inc=s_dv)
                            last[("bank", bk)] = [(s_dv, ev)]
                        ev_add1 = s_dv.v
                        if C2_SUB <= 2:
                            last[("AD1T",)] = [(s_dv, ev_add1)]
                            continue
                        qb = 8 * (tt % 2)
                        ev_s = em.op("vector", lambda h, qb=qb: h.reduce_sum(
                            QST2[:, qb + 0:qb + 1], AD1T[:, :], axis=AXX),
                            waits=[(s_dv, ev_add1)] + last.get(("QST2", tt % 2), []),
                            inc=s_dv)
                        ev_q2 = em.op("scalar", lambda h, qb=qb: h.activation(
                            SQT[:, :], AD1T[:, :], AF.Square, accum_out=QST2[:, qb + 1:qb + 2]),
                            waits=[(s_dv, ev_add1)])
                        ev_q2 = em.op("scalar", lambda h: h.drain(), inc=s_ac)
                        em.op("vector", lambda h, qb=qb: h.tensor_scalar(
                            QST2[:, qb + 2:qb + 3], QST2[:, qb + 0:qb + 1], 1.0 / E, None, ALU.mult),
                            waits=[(s_dv, ev_s)], inc=s_dv)
                        ev_m2x = s_dv.v
                        ev_ms = em.op("scalar", lambda h, qb=qb: h.activation(
                            QST2[:, qb + 3:qb + 4], QST2[:, qb + 2:qb + 3], AF.Square),
                            waits=[(s_dv, ev_m2x)], inc=s_ac)
                        em.op("vector", lambda h, qb=qb: h.tensor_scalar(
                            QST2[:, qb + 4:qb + 5], QST2[:, qb + 1:qb + 2], 1.0 / E, None, ALU.mult),
                            waits=[(s_ac, ev_q2)])
                        em.op("vector", lambda h, qb=qb: h.tensor_tensor(
                            QST2[:, qb + 4:qb + 5], QST2[:, qb + 4:qb + 5], QST2[:, qb + 3:qb + 4],
                            ALU.subtract),
                            waits=[(s_ac, ev_ms)])
                        ev_v2 = em.op("vector", lambda h, qb=qb: h.tensor_scalar(
                            QST2[:, qb + 5:qb + 6], QST2[:, qb + 4:qb + 5], 1e-6, None, ALU.add),
                            inc=s_dv)
                        ev_sd2 = em.op("scalar", lambda h, qb=qb: h.activation(
                            QST2[:, qb + 6:qb + 7], QST2[:, qb + 5:qb + 6], AF.Sqrt),
                            waits=[(s_dv, ev_v2)], inc=s_ac)
                        em.op("vector", lambda h, qb=qb: h.reciprocal(
                            QST2[:, qb + 7:qb + 8], QST2[:, qb + 6:qb + 7]),
                            waits=[(s_ac, ev_sd2)])
                        ev_r2 = em.op("vector", lambda h: h.drain(), inc=s_dv)
                        last[("QST2", tt % 2)] = [(s_ac, ev_sd2)]
                        ev_ln2 = em.op("vector", lambda h, qb=qb: h.tensor_scalar(
                            LNT2[:, :], AD1T[:, :], QST2[:, qb + 2:qb + 3], QST2[:, qb + 7:qb + 8],
                            ALU.subtract, ALU.mult),
                            waits=[(s_dv, ev_r2)] + last.get(("LNT2",), []), inc=s_dv)
                        for grp in range(2):
                            bk = grp
                            evt = None
                            for jj in range(4):
                                j = grp * 4 + jj
                                evt = em.op("tensor", lambda h, j=j, jj=jj, bk=bk: h.matmul(
                                    P[bk][:, jj * 128:(jj + 1) * 128],
                                    LNT2[:, j * 128:(j + 1) * 128],
                                    ID[:, :], start=True, stop=True),
                                    waits=([(s_dv, ev_ln2)] + last.get(("bank", bk), []))
                                          if jj == 0 else [],
                                    inc=(s_pe if jj == 3 else None))
                            ev = None
                            for jj in range(4):
                                j = grp * 4 + jj
                                ev = em.op("vector", lambda h, tt=tt, j=j, jj=jj, bk=bk: h.tensor_scalar(
                                    LN1F[:, 2048 * j + 128 * tt:2048 * j + 128 * tt + 128],
                                    P[bk][:, 128 * jj:128 * jj + 128],
                                    GAMc[:, j:j + 1], BETc[:, j:j + 1],
                                    ALU.mult, ALU.add),
                                    waits=[(s_pe, evt)] if jj == 0 else [],
                                    inc=(s_dv if jj == 3 else None))
                            last[("bank", bk)] = [(s_dv, ev)]
                        last[("LNT2",)] = [(s_pe, s_pe.v)]
                        if DEBUG_C and tt == 12:
                            dma("sync", dbg["cad9"][:, :], AD1T[:, :],
                                waits=[(s_dv, s_dv.v)], sem=s_st)
                            em.op("vector", lambda h: h.engine_nop(),
                                  waits=[(s_st, s_st.v)])
                        if DEBUG_C and tt == 12:
                            dma("gpsimd", dbg["catt"][:, :], ATT[:, :],
                                waits=[(s_dv, s_dv.v), (s_pe, s_pe.v)], sem=s_st)
                            dma("sync", dbg["cad1"][:, :], AD1T[:, :],
                                waits=[(s_dv, s_dv.v)], sem=s_st)
                            dma("sync", dbg["cden"][:, :], DENR[:, :],
                                waits=[(s_dv, s_dv.v)], sem=s_st)
                            dma("gpsimd", dbg["cln2f"][:, :], LN1F[:, 0:2048],
                                waits=[(s_dv, s_dv.v)], sem=s_st)
                            em.op("vector", lambda h: h.engine_nop(),
                                  waits=[(s_st, s_st.v)])
                        if DEBUG_C and tt == 151:
                            dma("gpsimd", dbg["cln2t15"][:, :], LN1F[:, 0:2048],
                                waits=[(s_dv, s_dv.v), (s_pe, s_pe.v)], sem=s_st)
                            em.op("vector", lambda h: h.engine_nop(),
                                  waits=[(s_st, s_st.v)])
                            em.op("gpsimd", lambda h: h.engine_nop(),
                                  waits=[(s_st, s_st.v)])
                        ev_xu = em.op("vector", lambda h, tt=tt: h.tensor_tensor(
                            XR[:, E * tt:E * (tt + 1)], XR[:, E * tt:E * (tt + 1)],
                            AD1T[:, :], ALU.add), inc=s_dv)
                        last[("AD1T",)] = [(s_dv, ev_xu), (s_ac, ev_q2)]
                        if C2_SER:
                            em.op("vector", lambda h: h.drain(),
                                  waits=[(s_ac, s_ac.v), (s_pe, s_pe.v), (s_pl, s_pl.v)],
                                  inc=s_dv)
                            em.op("scalar", lambda h: h.drain(),
                                  waits=[(s_dv, s_dv.v)], inc=s_ac)
                            em.op("tensor", lambda h: h.nop(),
                                  waits=[(s_dv, s_dv.v), (s_ac, s_ac.v)])
                if C_PART <= 4:
                    em.flush(nc)
                    return
                em.flush(nc)
                with nc.sbuf_tensor("HG", [128, FC * 512], BF16) as HG, \
                     nc.sbuf_tensor("OF", [128, EC * 512], BF16) as OF, \
                     nc.sbuf_tensor("BW1S", [128, 2 * 4096], BF16) as BW1S, \
                     nc.sbuf_tensor("BW2S", [128, 2 * 4096], BF16) as BW2S:
                    last = {}
                    s_b1 = [Sem(es, nc, f"s_b1{i}") for i in range(2)]
                    s_b2 = [Sem(es, nc, f"s_b2{i}") for i in range(2)]

                    def b1_dma(g):
                        slot = g % 2
                        wv = [(s_pe, last.get(("b1use", slot), 0)), (s_st, ev_spills)]
                        return em.op("sync", lambda h, g=g, slot=slot: h.dma_start(
                            out=V(BW1S, 512)[:, slot * 8:slot * 8 + 8, :],
                            in_=bw1_sp.ap()[:, 512 * g:512 * (g + 1)].rearrange(
                                "(j p) f -> p j f", p=128)),
                            waits=[w for w in wv if w[1] > 0], inc=s_b1[slot], inc_n=16)

                    def b2_dma(g):
                        slot = g % 2
                        wv = [(s_pe, last.get(("b2use", slot), 0)), (s_st, ev_spills)]
                        return em.op("sync", lambda h, g=g, slot=slot: h.dma_start(
                            out=V(BW2S, 1024)[:, slot * 4:slot * 4 + 4, :],
                            in_=bw2_sp.ap()[512 * g:512 * (g + 1), :].rearrange(
                                "(j p) f -> p j f", p=128)),
                            waits=[w for w in wv if w[1] > 0], inc=s_b2[slot], inc_n=16)

                    for tq in range(4):
                        ev_b1 = {}
                        ev_b1[0] = b1_dma(0)
                        ev_b1[1] = b1_dma(1)
                        for c in range(FC):
                            g = c // 4
                            slot = g % 2
                            bk = c % 2
                            evt = None
                            for j in range(EC):
                                evt = em.op("tensor", lambda h, c=c, j=j, tq=tq, slot=slot,
                                            bk=bk: h.matmul(
                                    P[bk][:, 0:512],
                                    V(BW1S, 512)[:, slot * 8 + j, 128 * (c % 4):
                                                 128 * (c % 4) + 128],
                                    LN1F[:, 2048 * j + 512 * tq:2048 * j + 512 * tq + 512],
                                    start=(j == 0), stop=(j == EC - 1)),
                                    waits=([(s_b1[slot], ev_b1[g]), (s_dv, s_dv.v)]
                                           + last.get(("bank", bk), [])) if j == 0 else [],
                                    inc=(s_pe if j == EC - 1 else None))
                            if c % 4 == 3:
                                last[("b1use", slot)] = evt
                                if g + 2 <= 7:
                                    ev_b1[g + 2] = b1_dma(g + 2)
                            ev = em.op("scalar", lambda h, c=c, bk=bk: h.activation(
                                HG[:, 512 * c:512 * c + 512], P[bk][:, 0:512],
                                AF.Gelu, bias=B1c[:, c:c + 1], scale=1.0),
                                waits=[(s_pe, evt)] + last.get(("HGw", c % 2), []),
                                inc=s_ac)
                            last[("bank", bk)] = [(s_ac, ev)]
                        ev_hg = s_ac.v
                        ev_b2 = {}
                        ev_b2[0] = b2_dma(0)
                        ev_b2[1] = b2_dma(1)
                        ev_ph2 = None
                        for c in range(FC):
                            g = c // 4
                            slot = g % 2
                            for j in range(EC):
                                ev_ph2 = em.op("tensor", lambda h, c=c, j=j, slot=slot: h.matmul(
                                    P[j][:, 0:512],
                                    V(BW2S, 1024)[:, slot * 4 + (c % 4), 128 * j:128 * j + 128],
                                    HG[:, 512 * c:512 * c + 512],
                                    start=(c == 0), stop=(c == FC - 1)),
                                    waits=([(s_b2[slot], ev_b2[g]), (s_ac, ev_hg)]
                                           + sum((last.get(("bank", bb), []) for bb in range(8)), []))
                                          if (c == 0 and j == 0) else [],
                                    inc=(s_pe if (c % 4 == 3 and j == EC - 1) else None))
                            if c % 4 == 3:
                                last[("b2use", slot)] = ev_ph2
                                if g + 2 <= 7:
                                    ev_b2[g + 2] = b2_dma(g + 2)
                        ev_of = []
                        for j in range(EC):
                            ev = em.op("scalar", lambda h, j=j: h.activation(
                                OF[:, 512 * j:512 * j + 512], P[j][:, 0:512],
                                AF.Gelu, bias=B2c[:, j:j + 1], scale=1.0),
                                waits=[(s_pe, ev_ph2)] + (last.get(("OFw",), []) if j == 0 else []),
                                inc=s_ac)
                            ev_of.append(ev)
                        for j in range(EC):
                            last[("bank", j)] = [(s_ac, ev_of[-1])]
                        for sq in range(4):
                            for grp in range(2):
                                bk = grp
                                evt = None
                                for jj in range(4):
                                    j = grp * 4 + jj
                                    evt = em.op("tensor", lambda h, j=j, jj=jj, sq=sq,
                                                bk=bk: h.matmul(
                                        P[bk][:, 128 * jj:128 * jj + 128],
                                        OF[:, 512 * j + 128 * sq:512 * j + 128 * sq + 128],
                                        ID[:, :], start=True, stop=True),
                                        waits=([(s_ac, ev_of[-1])] + last.get(("bank", bk), []))
                                              if jj == 0 else [],
                                        inc=(s_pe if jj == 3 else None))
                                tile = 4 * tq + sq
                                ev = em.op("vector", lambda h, tile=tile, grp=grp, bk=bk: h.tensor_tensor(
                                    XR[:, E * tile + 512 * grp:E * tile + 512 * grp + 512],
                                    XR[:, E * tile + 512 * grp:E * tile + 512 * grp + 512],
                                    P[bk][:, 0:512], ALU.add),
                                    waits=[(s_pe, evt)], inc=s_dv)
                                last[("bank", bk)] = [(s_dv, ev)]
                        last[("OFw",)] = [(s_pe, s_pe.v)]
                        last[("HGw", 0)] = [(s_pe, s_pe.v)]
                        last[("HGw", 1)] = [(s_pe, s_pe.v)]
                        if C3_SER:
                            em.op("vector", lambda h: h.drain(),
                                  waits=[(s_ac, s_ac.v), (s_pe, s_pe.v), (s_pl, s_pl.v)],
                                  inc=s_dv)
                            em.op("scalar", lambda h: h.drain(),
                                  waits=[(s_dv, s_dv.v)], inc=s_ac)
                            em.op("tensor", lambda h: h.nop(),
                                  waits=[(s_dv, s_dv.v), (s_ac, s_ac.v)])
                em.flush(nc)

        if STAGES >= 3:
            with nc.sbuf_tensor("QST2g", [128, 16], F32) as QST2:
                emit_attnb()
        if STAGES >= 4:
            emit_ffn(1)

        ev_out = dma("sync", out_d.ap().rearrange("(i p) f -> p i f", p=128),
                     V(XR, E), waits=[(s_dv, s_dv.v), (s_ac, s_ac.v), (s_pe, s_pe.v)],
                     sem=s_st)
        em.op("sync", lambda h: h.nop(), waits=[(s_st, ev_out)])
        em.flush(nc)

    return nc


def host_layout(inputs):
    xfull = np.ascontiguousarray(inputs["L0_attn_a_x"], dtype=np.float32)
    lay = dict(
        wattnT=np.ascontiguousarray(inputs["L0_attn_a_w_attn"].T),
        wprojT=np.ascontiguousarray(inputs["L0_attn_a_w_proj"].T),
        w1T=np.ascontiguousarray(inputs["L0_ffn_w1"].T),
        w2T=np.ascontiguousarray(inputs["L0_ffn_w2"].T),
        cw=np.ascontiguousarray(
            inputs["L0_ffn_conv_w"].reshape(F, 3).reshape(FC, 128, 3)
            .transpose(1, 0, 2).reshape(128, 96)),
        cb=np.ascontiguousarray(inputs["L0_ffn_conv_b"].reshape(FC, 128).T),
        gam=np.ascontiguousarray(inputs["L0_attn_b_gamma1"].reshape(EC, 128).T),
        bet=np.ascontiguousarray(inputs["L0_attn_b_beta1"].reshape(EC, 128).T),
        b1c=np.ascontiguousarray(inputs["L0_attn_b_b1"].reshape(FC, 128).T),
        b2c=np.ascontiguousarray(inputs["L0_attn_b_b2"].reshape(EC, 128).T),
        wqm=np.ascontiguousarray(inputs["L0_attn_b_wq"].reshape(E, E)),
        wkm=np.ascontiguousarray(inputs["L0_attn_b_wk"].reshape(E, E)),
        wvm=np.ascontiguousarray(inputs["L0_attn_b_wv"].reshape(E, E)),
        wom=np.ascontiguousarray(inputs["L0_attn_b_wo"].reshape(E, E)),
        bw1=np.ascontiguousarray(inputs["L0_attn_b_w1"]),
        bw2=np.ascontiguousarray(inputs["L0_attn_b_w2"]),
    )
    import ml_dtypes
    in_maps = []
    for c in range(NC):
        b, q = c // 4, c % 4
        sel = np.zeros((16, 2), ml_dtypes.bfloat16)
        if q > 0:
            sel[2 * (c - 1) + 1, 0] = 1
        if q < 3:
            sel[2 * (c + 1), 1] = 1
        m = dict(lay)
        m["x"] = np.ascontiguousarray(xfull[b, q * T:(q + 1) * T, :])
        m["sel"] = sel
        in_maps.append(m)
    return in_maps


def kernel(**inputs):
    in_maps = host_layout(inputs)
    nc = build_nc()
    res = run_bass_kernel_spmd(nc, in_maps, core_ids=list(range(NC)))
    outf = np.empty((2, 8192, E), np.float32)
    for c in range(NC):
        b, q = c // 4, c % 4
        outf[b, q * T:(q + 1) * T, :] = res.results[c]["out"]
    return outf
